# version 04
# baseline (speedup 1.0000x reference)
import os
import contextlib

import numpy as np
import ml_dtypes
import concourse.bass as bass
import concourse.mybir as mybir
from concourse.bass_utils import run_bass_kernel_spmd

F32 = mybir.dt.float32
BF16 = mybir.dt.bfloat16
AF = mybir.ActivationFunctionType
ALU = mybir.AluOpType
AX = mybir.AxisListType

MAGIC = float(np.float32(1.5 * 2.0 ** 52)) if os.environ.get("BASS_SIM_FP64") == "1" \
    else float(np.float32(1.5 * 2.0 ** 23))
D = 1024
T = 2048
NT = 16
NB = 4
EPS = 1e-5
ISCALE = 1.0 / 32.0

A_XQT = 0
A_WG = 4096
A_WU = 6144
A_GUQ = 8192
A_VST = 16384
A_STG = 22528
A_KT = 0
A_V = 16384
A_V = 8192
A_QT = 16640
A_PT = 20736
A_HST = 22784

SC_RSTD1, SC_NMR1 = 0, 16
SC_RSTD2, SC_NMR2 = 32, 48
SC_SX, SC_INVSX = 64, 80
SC_S2 = 96
SC_INVS2 = 100
SC_WSC = 104
SC_TMP = 112
SC_QMX = 128
SC_KMX = 144
SC_NEGC = 148


class Prog:
    def __init__(self, nc, stack):
        self.nc = nc
        self.streams = {k: [] for k in ("sync", "tensor", "scalar", "vector", "gpsimd")}
        self.sems = {}
        self.cnt = {}
        self._stack = stack

    def sem(self, name):
        self.sems[name] = self._stack.enter_context(self.nc.semaphore(name=name))
        self.cnt[name] = 0

    def wait(self, eng, sname, val):
        if val is None or val <= 0:
            return
        h = self.sems[sname]
        self.streams[eng].append(lambda e, h=h, v=val: e.wait_ge(h, v))

    def op(self, eng, build, sname=None, n=1):
        if sname is None:
            self.streams[eng].append(build)
            return None
        h = self.sems[sname]
        self.cnt[sname] += n
        v = self.cnt[sname]

        def f(e, build=build, h=h, n=n):
            build(e).then_inc(h, n)

        self.streams[eng].append(f)
        return v

    def emit_block(self):
        with self.nc.Block() as block:
            @block.sync
            def _(e):
                for f in self.streams["sync"]:
                    f(e)

            @block.tensor
            def _(e):
                for f in self.streams["tensor"]:
                    f(e)

            @block.scalar
            def _(e):
                for f in self.streams["scalar"]:
                    f(e)

            @block.vector
            def _(e):
                for f in self.streams["vector"]:
                    f(e)

            @block.gpsimd
            def _(e):
                for f in self.streams["gpsimd"]:
                    f(e)


def build(nc, groups=None, stop_after=None):
    if groups is None:
        groups = [[0, 1], [2, 3], [4, 5], [6, 7]]
    dt = nc.dram_tensor
    xd = dt("x", [T, D], F32, kind="ExternalInput").ap()
    wgpm = dt("wgpm", [128, 32 * 1024], F32, kind="ExternalInput").ap()
    wupm = dt("wupm", [128, 32 * 1024], F32, kind="ExternalInput").ap()
    wdpm = dt("wdpm", [128, 32 * 1024], F32, kind="ExternalInput").ap()
    wqkpm = dt("wqkpm", [128, 16 * 1024], F32, kind="ExternalInput").ap()
    wvpm = dt("wvpm", [128, 8 * 1024], F32, kind="ExternalInput").ap()
    wopm = dt("wopm", [128, 8 * 1024], F32, kind="ExternalInput").ap()
    bqkc = dt("bqkc", [128, 16], F32, kind="ExternalInput").ap()
    bvr = dt("bvr", [1, D], F32, kind="ExternalInput").ap()
    bord = dt("bor", [1, D], F32, kind="ExternalInput").ap()
    identd = dt("ident", [128, 128], BF16, kind="ExternalInput").ap()
    identf = dt("identf", [128, 128], F32, kind="ExternalInput").ap()
    outd = dt("out", [T, D], F32, kind="ExternalOutput").ap()

    tgd = dt("tgs", [128, 32 * 1024], BF16, kind="Internal").ap()
    tud = dt("tus", [128, 32 * 1024], BF16, kind="Internal").ap()
    tdd = dt("tds", [128, 32 * 1024], BF16, kind="Internal").ap()
    wqkb = dt("wqkbs", [128, 16 * 1024], BF16, kind="Internal").ap()
    wvbd = dt("wvbs", [128, 8 * 1024], BF16, kind="Internal").ap()
    wobd = dt("wobs", [128, 8 * 1024], BF16, kind="Internal").ap()
    qtd = dt("qts", [128, 8 * 2048], BF16, kind="Internal").ap()
    htd = dt("hts", [128, 8 * 2048], BF16, kind="Internal").ap()
    kvin = dt("kvin", [128, 8 * 2048 + 8 * 2080], BF16, kind="Internal").ap()
    kvout = dt("kvout", [256, 8 * 2048 + 8 * 2080], BF16, kind="Internal").ap()
    kkin = kvin[:, 0:8 * 2048]
    vvin = kvin[:, 8 * 2048:]
    kkout = kvout[:, 0:8 * 2048]
    vvout = kvout[:, 8 * 2048:]

    stack = contextlib.ExitStack()
    sbuf = lambda name, shape, dty: stack.enter_context(nc.sbuf_tensor(name, shape, dty))
    pst = lambda name, shape, dty=F32: stack.enter_context(nc.psum_tensor(name, shape, dty))

    X = sbuf("X", [128, NT * 1024], F32)
    GU = sbuf("GU", [128, 32 * 512], F32)
    XLN = sbuf("XLN", [128, 2 * 1024], F32)
    TGR = sbuf("TGR", [128, 2048], F32)
    CTX = sbuf("CTX", [128, 512], F32)
    MNA = sbuf("MNA", [128, 512], F32)
    STAT = sbuf("STAT", [128, 160], F32)
    IBC = sbuf("IBC", [128, 1024], F32)
    ABF = sbuf("ABF", [128, 24576], BF16)
    BB = sbuf("BB", [128, 2048], BF16)
    IDT = sbuf("IDT", [128, 128], BF16)
    IDF = sbuf("IDF", [128, 128], F32)
    BQK = sbuf("BQK", [128, 16], F32)
    ONES = sbuf("ONES", [128, 128], F32)
    ONES2 = sbuf("ONES2", [128, 2], F32)

    P = [pst(f"P{i}", [128, 512]) for i in range(4)]
    P45 = pst("P45", [128, 1024])
    P6 = pst("P6", [128, 512])
    PT8 = pst("PT8", [128, 1024], BF16)

    prog = Prog(nc, stack)
    for s in ("LD", "LW0", "LW1", "LK0", "LK1", "GD", "S20", "S21",
              "SV0", "SV1", "SH0", "SH1", "TE", "SC", "VE", "CC"):
        prog.sem(s)
    W, O, C = prog.wait, prog.op, prog.cnt

    def dma(eng, out, in_, sname="LD", **kw):
        return O(eng, lambda e, o=out, i=in_, kw=kw: e.dma_start(out=o, in_=i, **kw),
                 sname, 16)

    def vdrain():
        O("vector", lambda e: e.drain())

    def stc(c):
        return STAT[:, c:c + 1]

    def abf(lo, n, p0=0, p1=128):
        return ABF[p0:p1, lo:lo + n]

    dma("sync", IDT[:], identd[:])
    dma("sync", IDF[:], identf[:])
    for t in range(NT):
        dma("sync", X[:, t * 1024:(t + 1) * 1024], xd[128 * t:128 * t + 128, :])
    dma("sync", BQK[:], bqkc[:])
    dma("sync", TGR[0:1, 0:1024], bvr[:])
    ld_b = dma("sync", TGR[0:1, 1024:2048], bord[:])
    O("vector", lambda e: e.memset(ONES2[:, :], 0.0))
    vdrain()
    O("vector", lambda e: e.memset(ONES2[0:64, 0:1], 1.0))
    vdrain()
    O("vector", lambda e: e.memset(ONES2[64:128, 1:2], 1.0))
    vdrain()
    v_ones = O("vector", lambda e: e.memset(ONES[:], 1.0), "VE")
    W("tensor", "VE", v_ones)
    W("tensor", "LD", ld_b)
    v_bb = 0
    for h in range(2):
        W("tensor", "VE", v_bb)
        O("tensor", lambda e, h=h: e.matmul(
            P[0][:, :], ONES[0:1, 0:128], TGR[0:1, 512 * h:512 * (h + 1)],
            start=True, stop=True))
        te_b = O("tensor", lambda e, h=h: e.matmul(
            P[1][:, :], ONES[0:1, 0:128], TGR[0:1, 1024 + 512 * h:1536 + 512 * h],
            start=True, stop=True), "TE")
        W("vector", "TE", te_b)
        O("vector", lambda e, h=h: e.tensor_copy(BB[:, 512 * h:512 * (h + 1)], P[0][:, :]))
        v_bb = O("vector", lambda e, h=h: e.tensor_copy(
            BB[:, 1024 + 512 * h:1536 + 512 * h], P[1][:, :]), "VE")
    ve_bias = C["VE"]

    NCOL = 128 * 32 * 1024

    def wprep(src, dst, wsc_col, first_wait):
        v_red = [0] * 8
        for i in range(8):
            p = i % 2
            W("sync", "VE", v_red[i - 2] if i >= 2 else first_wait)
            lv = dma("sync", GU[:, p * 4096:(p + 1) * 4096], src[:, i * 4096:(i + 1) * 4096],
                     f"LW{p}")
            W("vector", f"LW{p}", lv)
            v_red[i] = O("vector", lambda e, i=i, p=p: e.reduce_sum(
                stc(SC_TMP + i), GU[:, p * 4096:(p + 1) * 4096], axis=AX.X,
                apply_absolute_value=True), "VE")
        vdrain()
        O("vector", lambda e: e.reduce_sum(
            stc(SC_TMP + 8), STAT[:, SC_TMP:SC_TMP + 8], axis=AX.X))
        v_s = O("vector", lambda e: e.drain(), "VE")
        W("tensor", "VE", v_s)
        te_s = O("tensor", lambda e: e.matmul(
            P6[0:1, 0:1], stc(SC_TMP + 8), ONES[:, 0:1], start=True, stop=True), "TE")
        W("vector", "TE", te_s)
        O("vector", lambda e: e.tensor_scalar(
            stc(SC_TMP + 9)[0:1, :], P6[0:1, 0:1], 1.0 / NCOL, 1e-5, ALU.mult, ALU.max))
        vdrain()
        v_r = O("vector", lambda e: e.reciprocal(
            stc(SC_TMP + 10)[0:1, :], stc(SC_TMP + 9)[0:1, :]), "VE")
        W("tensor", "VE", v_r)
        O("tensor", lambda e: e.matmul(
            P6[0:128, 0:1], ONES[0:1, 0:128], stc(SC_TMP + 10)[0:1, :],
            start=True, stop=True))
        te_b2 = O("tensor", lambda e: e.matmul(
            P6[0:128, 1:2], ONES[0:1, 0:128], stc(SC_TMP + 9)[0:1, :],
            start=True, stop=True), "TE")
        W("vector", "TE", te_b2)
        O("vector", lambda e: e.tensor_copy(stc(SC_TMP + 11), P6[0:128, 0:1]))
        v_bc = O("vector", lambda e, c=wsc_col: e.tensor_copy(stc(c), P6[0:128, 1:2]), "VE")
        st_prev = [C["S20"], C["S21"]]
        s_done = [0] * 8
        v_done = [0] * 8
        for i in range(8):
            p = i % 2
            W("sync", "VE", v_done[i - 2] if i >= 2 else v_bc)
            lv = dma("sync", GU[:, p * 4096:(p + 1) * 4096], src[:, i * 4096:(i + 1) * 4096],
                     f"LW{p}")
            W("scalar", f"LW{p}", lv)
            W("scalar", "VE", v_done[i - 1] if i >= 1 else v_bc)
            s_done[i] = O("scalar", lambda e, p=p: e.activation(
                GU[:, 8192 + p * 4096:8192 + (p + 1) * 4096],
                GU[:, p * 4096:(p + 1) * 4096],
                AF.Copy, bias=MAGIC, scale=stc(SC_TMP + 11)[:, 0:1]), "SC")
            W("vector", "SC", s_done[i])
            W("vector", f"S2{p}", st_prev[p])
            O("vector", lambda e, p=p: e.tensor_scalar(
                abf(p * 4096, 4096), GU[:, 8192 + p * 4096:8192 + (p + 1) * 4096],
                -MAGIC, 1.0, ALU.add, ALU.min))
            vdrain()
            v_done[i] = O("vector", lambda e, p=p: e.tensor_scalar(
                abf(p * 4096, 4096), abf(p * 4096, 4096), -1.0, None, ALU.max), "VE")
            W("scalar", "VE", v_done[i])
            st_prev[p] = dma("scalar", dst[:, i * 4096:(i + 1) * 4096],
                             abf(p * 4096, 4096), f"S2{p}")
        return v_done[7]

    v_p = wprep(wgpm, tgd, SC_WSC + 0, ve_bias)
    v_p = wprep(wupm, tud, SC_WSC + 1, v_p)
    v_p = wprep(wdpm, tdd, SC_WSC + 2, v_p)
    vdrain()
    O("vector", lambda e: e.tensor_scalar(
        stc(SC_WSC + 2), stc(SC_WSC + 2), 0.5, None, ALU.mult), "VE")

    def wcast(src, dst, ncols, v_first):
        v_prev = [0, 0]
        st_prev = [C["S20"], C["S21"]]
        for i in range(ncols // 4096):
            p = i % 2
            W("sync", "VE", v_prev[p] if v_prev[p] else v_first)
            lv = dma("sync", GU[:, p * 4096:(p + 1) * 4096], src[:, i * 4096:(i + 1) * 4096],
                     f"LW{p}")
            W("vector", f"LW{p}", lv)
            W("vector", f"S2{p}", st_prev[p])
            v_prev[p] = O("vector", lambda e, p=p: e.tensor_copy(
                abf(p * 4096, 4096), GU[:, p * 4096:(p + 1) * 4096]), "VE")
            W("scalar", "VE", v_prev[p])
            st_prev[p] = dma("scalar", dst[:, i * 4096:(i + 1) * 4096],
                             abf(p * 4096, 4096), f"S2{p}")
        return v_prev[(ncols // 4096 - 1) % 2]

    v_c = wcast(wqkpm, wqkb, 16 * 1024, C["VE"])
    v_c = wcast(wvpm, wvbd, 8 * 1024, v_c)
    v_c = wcast(wopm, wobd, 8 * 1024, v_c)

    def ln_prepass(rstd_base, nmr_base):
        s_prev = 0
        W("vector", "LD", C["LD"])
        for t in range(NT):
            xt = X[:, t * 1024:(t + 1) * 1024]
            O("vector", lambda e, xt=xt: e.reduce_sum(stc(SC_TMP + 0), xt, axis=AX.X))
            v0 = O("vector", lambda e: e.drain(), "VE")
            W("scalar", "VE", v0)
            W("scalar", "SC", s_prev)
            s0 = O("scalar", lambda e, xt=xt: e.activation(
                XLN[:, 0:1024], xt, AF.Square, accum_out=stc(SC_TMP + 1)), "SC")
            W("vector", "SC", s0)
            O("vector", lambda e: e.tensor_scalar(
                stc(SC_TMP + 2), stc(SC_TMP + 0), 1.0 / D, None, ALU.mult))
            O("vector", lambda e: e.tensor_scalar(
                stc(SC_TMP + 3), stc(SC_TMP + 1), 1.0 / D, None, ALU.mult))
            vdrain()
            O("vector", lambda e: e.tensor_tensor(
                stc(SC_TMP + 4), stc(SC_TMP + 2), stc(SC_TMP + 2), ALU.mult))
            vdrain()
            O("vector", lambda e: e.tensor_tensor(
                stc(SC_TMP + 5), stc(SC_TMP + 3), stc(SC_TMP + 4), ALU.subtract))
            vdrain()
            O("vector", lambda e: e.tensor_scalar(
                stc(SC_TMP + 6), stc(SC_TMP + 5), EPS, None, ALU.add))
            vdrain()
            v1 = O("vector", lambda e: e.reciprocal(stc(SC_TMP + 7), stc(SC_TMP + 6)), "VE")
            W("scalar", "VE", v1)
            s1 = O("scalar", lambda e, t=t: e.activation(
                stc(rstd_base + t), stc(SC_TMP + 7), AF.Sqrt), "SC")
            s_prev = s1
            W("vector", "SC", s1)
            O("vector", lambda e, t=t: e.tensor_tensor(
                stc(SC_TMP + 8), stc(SC_TMP + 2), stc(rstd_base + t), ALU.mult))
            vdrain()
            O("vector", lambda e, t=t: e.tensor_scalar(
                stc(nmr_base + t), stc(SC_TMP + 8), -1.0, None, ALU.mult), "VE")

    st8 = {"seq": 0, "v_xl": {}, "te_tp": {}, "v_cp": {}, "v_tg": {}}

    def xln_tile(t, ti, rstd_base, nmr_base, quant, entry_te=None):
        k = st8["seq"]
        st8["seq"] += 1
        sl = k % 2
        xt = X[:, t * 1024:(t + 1) * 1024]
        xl = XLN[:, sl * 1024:(sl + 1) * 1024]
        W("scalar", "VE", st8["v_xl"].get(k - 2))
        s0 = O("scalar", lambda e, xt=xt, xl=xl, t=t: e.activation(
            xl, xt, AF.Identity, bias=stc(nmr_base + t)[:, 0:1],
            scale=stc(rstd_base + t)[:, 0:1]), "SC")
        W("vector", "SC", s0)
        if entry_te:
            W("vector", "TE", entry_te)
        if quant:
            O("vector", lambda e, xl=xl: e.reduce_max(
                stc(SC_TMP + 0), xl, axis=AX.X, apply_absolute_value=True))
            vdrain()
            O("vector", lambda e: e.tensor_scalar(
                stc(SC_TMP + 1), stc(SC_TMP + 0), 1e-5, None, ALU.max))
            vdrain()
            O("vector", lambda e, t=t: e.tensor_scalar(
                stc(SC_INVSX + t % 16), stc(SC_TMP + 1), 1.0 / 127.0, None, ALU.mult))
            O("vector", lambda e: e.reciprocal(stc(SC_TMP + 2), stc(SC_TMP + 1)))
            vdrain()
            v0 = O("vector", lambda e, t=t: e.tensor_scalar(
                stc(SC_SX + t % 16), stc(SC_TMP + 2), 127.0, None, ALU.mult), "VE")
            st8["v_xl"][k] = v0
            W("scalar", "VE", v0)
            W("scalar", "VE", st8["v_tg"].get(k - 1))
            s1 = O("scalar", lambda e, xl=xl, t=t: e.activation(
                TGR[:, 0:1024], xl, AF.Copy, bias=MAGIC,
                scale=stc(SC_SX + t % 16)[:, 0:1]), "SC")
            W("vector", "SC", s1)
            W("vector", "TE", st8["te_tp"].get(k - 2))
            v1 = O("vector", lambda e, sl=sl: e.tensor_scalar(
                abf(A_STG + sl * 1024, 1024), TGR[:, 0:1024], -MAGIC, None, ALU.add), "VE")
            st8["v_tg"][k] = v1
        else:
            W("vector", "TE", st8["te_tp"].get(k - 2))
            v1 = O("vector", lambda e, xl=xl, sl=sl: e.tensor_copy(
                abf(A_STG + sl * 1024, 1024), xl), "VE")
            st8["v_xl"][k] = v1
        W("tensor", "VE", v1)
        W("tensor", "VE", st8["v_cp"].get(k - 1))
        for fc in range(8):
            te_x = O("tensor", lambda e, fc=fc, sl=sl: e.transpose(
                PT8[:, fc * 128:(fc + 1) * 128],
                abf(A_STG + sl * 1024 + fc * 128, 128), IDT[:]),
                "TE" if fc == 7 else None)
        st8["te_tp"][k] = te_x
        W("vector", "TE", st8["te_tp"][k])
        v_c = 0
        for fc in range(8):
            v_c = O("vector", lambda e, fc=fc, ti=ti: e.tensor_copy(
                abf(A_XQT + fc * 512 + ti * 128, 128), PT8[:, fc * 128:(fc + 1) * 128]),
                "VE" if fc == 7 else None)
        st8["v_cp"][k] = v_c
        return v_c

    def bcast4(col0, ibc_lo, psum_guard=0):
        vdrain()
        v_a = O("vector", lambda e, c=col0: e.tensor_copy(
            TGR[:, 1024:1028], STAT[:, c:c + 4]), "VE")
        W("tensor", "VE", v_a)
        W("tensor", "SC", psum_guard)
        for q in range(4):
            te_a = O("tensor", lambda e, q=q: e.transpose(
                P6[:, q * 128:(q + 1) * 128], TGR[:, 1024 + q:1024 + q + 128], IDF[:]),
                "TE" if q == 3 else None)
        W("vector", "TE", te_a)
        for q in range(4):
            O("vector", lambda e, q=q: e.tensor_copy(
                TGR[0:1, 1152 + q * 128:1152 + (q + 1) * 128],
                P6[0:1, q * 128:(q + 1) * 128]))
        v_b = O("vector", lambda e: e.drain(), "VE")
        W("tensor", "VE", v_b)
        te_c = O("tensor", lambda e: e.matmul(
            P6[:, 0:512], ONES[0:1, 0:128], TGR[0:1, 1152:1664],
            start=True, stop=True), "TE")
        W("vector", "TE", te_c)
        return O("vector", lambda e, lo=ibc_lo: e.tensor_copy(
            IBC[:, lo:lo + 512], P6[:, 0:512]), "VE")

    fst = {"call": 0, "te_gu": None, "s_down": 0, "te_dlast": None, "te_blk": None}

    def ffn(rstd_base, nmr_base):
        W("sync", "TE", C["TE"])
        for sn in ("S20", "S21", "SV0", "SV1", "SH0", "SH1"):
            W("sync", sn, C[sn])
            W("vector", sn, C[sn])
        for blk in range(NB):
            v_cp_last = 0
            for ti in range(4):
                t = blk * 4 + ti
                v_cp_last = xln_tile(t, ti, rstd_base, nmr_base, True,
                                     entry_te=(fst["te_blk"] if ti == 0 else None))
            v_ibc = bcast4(SC_INVSX + (blk * 4) % 16, 0, psum_guard=fst["s_down"])
            te_j = {}
            v_pu = {}
            s_g = {}
            v_mm = {}
            for j in range(32):
                p = j % 2
                W("sync", "TE", te_j.get(j - 2, fst["te_dlast"] if j < 2 else None))
                dma("sync", abf(A_WG + p * 1024, 1024), tgd[:, j * 1024:(j + 1) * 1024],
                    f"LW{p}")
                ld_w = dma("sync", abf(A_WU + p * 1024, 1024), tud[:, j * 1024:(j + 1) * 1024],
                           f"LW{p}")
                gb = P[0] if p == 0 else P[2]
                ub = P[1] if p == 0 else P[3]
                W("tensor", f"LW{p}", ld_w)
                if j == 0:
                    W("tensor", "VE", v_ibc)
                    W("tensor", "SC", fst["s_down"])
                W("tensor", "VE", v_pu.get(j - 2))
                for ic in range(8):
                    O("tensor", lambda e, ic=ic, p=p, gb=gb: e.matmul(
                        gb[:, :], abf(A_WG + p * 1024 + ic * 128, 128),
                        abf(A_XQT + ic * 512, 512), start=(ic == 0), stop=(ic == 7)))
                for ic in range(8):
                    te_jj = O("tensor", lambda e, ic=ic, p=p, ub=ub: e.matmul(
                        ub[:, :], abf(A_WU + p * 1024 + ic * 128, 128),
                        abf(A_XQT + ic * 512, 512), start=(ic == 0), stop=(ic == 7)),
                        "TE" if ic == 7 else None)
                te_j[j] = te_jj
                W("vector", "TE", te_j[j])
                O("vector", lambda e, gb=gb: e.tensor_tensor(
                    TGR[:, 1024:1536], gb[:, :], IBC[:, 0:512], ALU.mult))
                v_pg = O("vector", lambda e: e.drain(), "VE")
                W("scalar", "VE", v_pg)
                s_g[j] = O("scalar", lambda e: e.activation(
                    TGR[:, 1536:2048], TGR[:, 1024:1536], AF.Silu,
                    scale=stc(SC_WSC + 0)[:, 0:1]), "SC")
                W("vector", "SC", s_g[j])
                O("vector", lambda e, ub=ub: e.tensor_tensor(
                    TGR[:, 1024:1536], ub[:, :], IBC[:, 0:512], ALU.mult))
                v_pu[j] = O("vector", lambda e: e.drain(), "VE")
                O("vector", lambda e, j=j: e.scalar_tensor_tensor(
                    GU[:, j * 512:(j + 1) * 512], TGR[:, 1024:1536],
                    stc(SC_WSC + 1)[:, 0:1], TGR[:, 1536:2048], ALU.mult, ALU.mult))
                vdrain()
                if j == 0:
                    O("vector", lambda e: e.tensor_copy(CTX[:, :], GU[:, 0:512]))
                    v_mm[j] = O("vector", lambda e: e.tensor_copy(MNA[:, :], GU[:, 0:512]), "VE")
                else:
                    O("vector", lambda e, j=j: e.tensor_tensor(
                        CTX[:, :], CTX[:, :], GU[:, j * 512:(j + 1) * 512], ALU.max))
                    v_mm[j] = O("vector", lambda e, j=j: e.tensor_tensor(
                        MNA[:, :], MNA[:, :], GU[:, j * 512:(j + 1) * 512], ALU.min), "VE")
                vdrain()
            fst["te_gu"] = te_j[31]

            O("vector", lambda e: e.tensor_scalar(MNA[:, :], MNA[:, :], -1.0, None, ALU.mult))
            vdrain()
            O("vector", lambda e: e.tensor_tensor(CTX[:, :], CTX[:, :], MNA[:, :], ALU.max))
            vdrain()
            v_mx = O("vector", lambda e: e.drain(), "VE")
            W("tensor", "VE", v_mx)
            W("tensor", "SC", fst["s_down"])
            for q in range(4):
                te_mx = O("tensor", lambda e, q=q: e.transpose(
                    P6[:, q * 128:(q + 1) * 128], CTX[:, q * 128:(q + 1) * 128], IDF[:]),
                    "TE" if q == 3 else None)
            W("vector", "TE", te_mx)
            for q in range(4):
                O("vector", lambda e, q=q: e.reduce_max(
                    stc(SC_TMP + q), P6[:, q * 128:(q + 1) * 128], axis=AX.X))
            vdrain()
            for q in range(4):
                O("vector", lambda e, q=q: e.tensor_scalar(
                    stc(SC_TMP + 4 + q), stc(SC_TMP + q), 1e-5, None, ALU.max))
            vdrain()
            for q in range(4):
                O("vector", lambda e, q=q: e.tensor_scalar(
                    stc(SC_INVS2 + q), stc(SC_TMP + 4 + q), 1.0 / 127.0, None, ALU.mult))
                O("vector", lambda e, q=q: e.reciprocal(
                    stc(SC_TMP + 8 + q), stc(SC_TMP + 4 + q)))
            vdrain()
            for q in range(4):
                O("vector", lambda e, q=q: e.tensor_scalar(
                    stc(SC_TMP + q), stc(SC_TMP + 8 + q), 127.0, None, ALU.mult))
                O("vector", lambda e, q=q: e.tensor_tensor(
                    stc(SC_S2 + q), stc(SC_INVS2 + q), stc(SC_WSC + 2), ALU.mult))
            v_s2bc = bcast4(SC_TMP, 512, psum_guard=fst["s_down"])

            for hf in range(2):
                v_rnd = {}
                s_t2 = {}
                for j in range(32):
                    if j == 0:
                        W("vector", "VE", v_s2bc)
                        if fst["te_dlast"]:
                            W("vector", "TE", fst["te_dlast"])
                    W("vector", "SC", s_t2.get(j - 1))
                    O("vector", lambda e, j=j, hf=hf: e.tensor_tensor(
                        TGR[:, 1024:1280],
                        GU[:, j * 512 + hf * 256:j * 512 + hf * 256 + 256],
                        IBC[:, 512 + hf * 256:512 + hf * 256 + 256], ALU.mult))
                    v_t1 = O("vector", lambda e: e.drain(), "VE")
                    W("scalar", "VE", v_t1)
                    s_t2[j] = O("scalar", lambda e: e.activation(
                        TGR[:, 1280:1536], TGR[:, 1024:1280], AF.Copy, bias=MAGIC), "SC")
                    W("vector", "SC", s_t2[j])
                    v_rnd[j] = O("vector", lambda e, j=j: e.tensor_scalar(
                        abf(A_GUQ + j * 256, 256), TGR[:, 1280:1536],
                        -MAGIC, None, ALU.add), "VE")
                te_dj = {}
                wdsl = A_WG if hf == 0 else A_WU
                for j in range(32):
                    p = j % 2
                    W("sync", "TE", te_dj.get(
                        j - 2, (fst["te_gu"] if hf == 0 else fst["te_dlast"])
                        if j < 2 else None))
                    lw = dma("sync", abf(wdsl + p * 1024, 1024),
                             tdd[:, j * 1024:(j + 1) * 1024], f"LW{p}")
                    W("tensor", f"LW{p}", lw)
                    W("tensor", "VE", v_rnd[j])
                    if j == 0:
                        W("tensor", "SC", fst["s_down"])
                    O("tensor", lambda e, j=j, p=p, w=wdsl: e.matmul(
                        P45[:, 0:512], abf(A_GUQ + j * 256, 128),
                        abf(w + p * 1024, 512), start=(j == 0), stop=(j == 31)))
                    O("tensor", lambda e, j=j, p=p, w=wdsl: e.matmul(
                        P45[:, 512:1024], abf(A_GUQ + j * 256, 128),
                        abf(w + p * 1024 + 512, 512), start=(j == 0), stop=(j == 31)))
                    O("tensor", lambda e, j=j, p=p, w=wdsl: e.matmul(
                        P[0][:, :], abf(A_GUQ + j * 256 + 128, 128),
                        abf(w + p * 1024, 512), start=(j == 0), stop=(j == 31)))
                    te_dj[j] = O("tensor", lambda e, j=j, p=p, w=wdsl: e.matmul(
                        P[1][:, :], abf(A_GUQ + j * 256 + 128, 128),
                        abf(w + p * 1024 + 512, 512), start=(j == 0), stop=(j == 31)), "TE")
                fst["te_dlast"] = te_dj[31]
                W("scalar", "TE", te_dj[31])
                W("scalar", "VE", v_s2bc)
                v_x = 0
                for tt2 in range(2):
                    tt = hf * 2 + tt2
                    t = blk * 4 + tt
                    if tt2 == 1:
                        W("scalar", "VE", v_x)
                    if tt2 == 0:
                        O("scalar", lambda e, tt=tt: e.activation(
                            TGR[:, 1024:1536], P45[:, 0:512], AF.Identity,
                            scale=stc(SC_S2 + tt)[:, 0:1]))
                        s_e = O("scalar", lambda e, tt=tt: e.activation(
                            TGR[:, 1536:2048], P45[:, 512:1024], AF.Identity,
                            scale=stc(SC_S2 + tt)[:, 0:1]), "SC")
                    else:
                        O("scalar", lambda e, tt=tt: e.activation(
                            TGR[:, 1024:1536], P[0][:, :], AF.Identity,
                            scale=stc(SC_S2 + tt)[:, 0:1]))
                        s_e = O("scalar", lambda e, tt=tt: e.activation(
                            TGR[:, 1536:2048], P[1][:, :], AF.Identity,
                            scale=stc(SC_S2 + tt)[:, 0:1]), "SC")
                    fst["s_down"] = s_e
                    W("vector", "SC", s_e)
                    v_x = O("vector", lambda e, t=t: e.tensor_tensor(
                        X[:, t * 1024:(t + 1) * 1024], X[:, t * 1024:(t + 1) * 1024],
                        TGR[:, 1024:2048], ALU.add), "VE")
                    vdrain()
            fst["te_blk"] = fst["te_dlast"]
        return v_x

    W("vector", "S20", C["S20"])
    W("vector", "S21", C["S21"])
    ln_prepass(SC_RSTD1, SC_NMR1)
    v_last = ffn(SC_RSTD1, SC_NMR1)

    if stop_after == "ffn1":
        W("scalar", "VE", C["VE"])
        for t in range(NT):
            dma("scalar", outd[128 * t:128 * t + 128, :], X[:, t * 1024:(t + 1) * 1024], "S20")
        W("sync", "S20", C["S20"])
        W("gpsimd", "S20", C["S20"])
        prog.emit_block()
        stack.close()
        return nc

    ln_prepass(SC_RSTD2, SC_NMR2)

    W("sync", "TE", C["TE"])
    W("sync", "S20", C["S20"])
    W("sync", "S21", C["S21"])
    ld_wv = 0
    for ic in range(8):
        ld_wv = dma("sync", abf(A_GUQ + ic * 1024, 1024), wvbd[:, ic * 1024:(ic + 1) * 1024])

    st_kq = [0, 0]
    st_v = [0, 0]
    v_qrd = [0, 0]
    te_o = {}
    s_ev = {}
    v_vs = {}
    te_vlast = None
    for blk in range(NB):
        v_cp_last = 0
        W("vector", "S20", st_kq[0])
        W("vector", "S21", st_kq[1])
        for ti in range(4):
            t = blk * 4 + ti
            v_cp_last = xln_tile(t, ti, SC_RSTD2, SC_NMR2, False,
                                 entry_te=(te_vlast if ti == 0 else None))
        for i, oc in enumerate(list(range(8, 16)) + list(range(0, 8))):
            g = blk * 16 + i
            p = g % 2
            W("sync", "TE", te_o.get(g - 2))
            ld_w = dma("sync", abf(A_WG + p * 1024, 1024), wqkb[:, oc * 1024:(oc + 1) * 1024],
                       f"LW{p}")
            W("tensor", f"LW{p}", ld_w)
            if i == 0:
                W("tensor", "VE", v_cp_last)
                if blk == 0:
                    W("tensor", "LD", ld_wv)
            W("tensor", "SC", s_ev.get(g - 2))
            for ic in range(8):
                te_og = O("tensor", lambda e, ic=ic, p=p: e.matmul(
                    P[p][:, :], abf(A_WG + p * 1024 + ic * 128, 128),
                    abf(A_XQT + ic * 512, 512), start=(ic == 0), stop=(ic == 7)),
                    "TE" if ic == 7 else None)
            te_o[g] = te_og
            W("scalar", "TE", te_o[g])
            W("scalar", f"S2{g % 2}", st_kq[g % 2])
            W("scalar", "VE", v_qrd[g % 2])
            sl = g % 2
            s_ev[g] = O("scalar", lambda e, p=p, oc=oc, sl=sl: e.activation(
                abf(A_STG + sl * 1024, 512), P[p][:, :], AF.Identity,
                bias=BQK[:, oc:oc + 1]), "SC")
            if oc >= 8:
                W("scalar", "SC", s_ev[g])
                st_kq[g % 2] = dma(
                    "scalar",
                    kkin[:, (oc - 8) * 2048 + blk * 512:(oc - 8) * 2048 + blk * 512 + 512],
                    abf(A_STG + sl * 1024, 512), f"S2{g % 2}")
            else:
                W("vector", "SC", s_ev[g])
                v_sq = O("vector", lambda e, sl=sl: e.tensor_tensor(
                    TGR[:, 0:512], abf(A_STG + sl * 1024, 512),
                    abf(A_STG + sl * 1024, 512), ALU.mult), "VE")
                W("tensor", "VE", v_sq)
                te_q2 = O("tensor", lambda e: e.matmul(
                    P6[0:2, 0:512], ONES2[:, :], TGR[:, 0:512],
                    start=True, stop=True), "TE")
                W("vector", "TE", te_q2)
                O("vector", lambda e: e.reduce_max(
                    STAT[0:2, SC_TMP:SC_TMP + 1], P6[0:2, 0:512], axis=AX.X))
                vdrain()
                if blk == 0:
                    O("vector", lambda e, oc=oc: e.tensor_copy(
                        STAT[0:2, SC_QMX + 2 * oc:SC_QMX + 2 * oc + 1],
                        STAT[0:2, SC_TMP:SC_TMP + 1]), "VE")
                else:
                    O("vector", lambda e, oc=oc: e.tensor_tensor(
                        STAT[0:2, SC_QMX + 2 * oc:SC_QMX + 2 * oc + 1],
                        STAT[0:2, SC_QMX + 2 * oc:SC_QMX + 2 * oc + 1],
                        STAT[0:2, SC_TMP:SC_TMP + 1], ALU.max), "VE")
                v_qrd[g % 2] = O("vector", lambda e: e.drain(), "VE")
                W("scalar", "SC", s_ev[g])
                st_kq[g % 2] = dma("scalar",
                            qtd[:, oc * 2048 + blk * 512:oc * 2048 + blk * 512 + 512],
                            abf(A_STG + sl * 1024, 512), f"S2{g % 2}")
        for tt in range(4):
            t = blk * 4 + tt
            g2 = blk * 4 + tt
            sl = g2 % 2
            W("vector", f"SV{sl}", st_v[sl])
            v_ms = O("vector", lambda e, sl=sl: e.memset(abf(A_VST + sl * 1040, 1040), 1.0), "VE")
            for oh in range(2):
                pb = P[2 + oh]
                W("tensor", "VE", v_vs.get((g2 - 1, oh)))
                if g2 == 0 and oh == 0:
                    W("tensor", "VE", v_ms)
                for ic in range(8):
                    te_v = O("tensor", lambda e, ic=ic, tt=tt, oh=oh, pb=pb: e.matmul(
                        pb[:, :], abf(A_XQT + ic * 512 + tt * 128, 128),
                        abf(A_GUQ + ic * 1024 + oh * 512, 512),
                        start=(ic == 0), stop=(ic == 7)), "TE" if ic == 7 else None)
                te_vlast = te_v
                W("vector", "TE", te_v)
                if oh == 0 and g2 > 0:
                    W("vector", "VE", v_ms)
                v_vs[(g2, oh)] = O("vector", lambda e, sl=sl, oh=oh, pb=pb: e.tensor_tensor(
                    abf(A_VST + sl * 1040, 1040).rearrange(
                        "p (h c) -> p h c", c=65)[:, 8 * oh:8 * (oh + 1), 0:64],
                    pb[:, :].rearrange("p (h c) -> p h c", c=64),
                    BB[:, oh * 512:(oh + 1) * 512].rearrange("p (h c) -> p h c", c=64),
                    ALU.add), "VE")
            W("scalar", "VE", v_vs[(g2, 1)])
            for pr in range(8):
                st_v[sl] = dma("scalar",
                            vvin[:, pr * 2080 + t * 130:pr * 2080 + t * 130 + 130],
                            abf(A_VST + sl * 1040 + pr * 130, 130), f"SV{sl}")

    W("gpsimd", "S20", C["S20"])
    W("gpsimd", "S21", C["S21"])
    W("gpsimd", "SV0", C["SV0"])
    W("gpsimd", "SV1", C["SV1"])
    O("gpsimd", lambda e: e.collective_compute(
        "AllGather", ALU.bypass, ins=[kvin[:, :]], outs=[kvout[:, :]],
        replica_groups=groups), "CC")

    te_pr = {}
    st_h = [0, 0]
    te_nc_prev = None
    s_exp_prev = None
    vev = {}
    ld_kt = {}
    ld_v = {}
    ld_qt = {}
    for pr in range(8):
        sl = pr % 2
        if pr == 0:
            W("sync", "CC", 1)
            W("sync", "TE", C["TE"])
        W("sync", "TE", te_pr.get(pr - 2))
        for h in range(2):
            dma("sync", abf(A_KT + sl * 4096 + h * 2048, 2048),
                kkout[h * 128:(h + 1) * 128, pr * 2048:(pr + 1) * 2048], f"LK{sl}")
            dma("sync", abf(A_V + sl * 4160 + h * 2080, 2080),
                vvout[h * 128:(h + 1) * 128, pr * 2080:(pr + 1) * 2080], f"LK{sl}")
        ld_v[pr] = C[f"LK{sl}"]
        ld_qt[pr] = dma("sync", abf(A_QT + sl * 2048, 2048), qtd[:, pr * 2048:(pr + 1) * 2048],
                        f"LK{sl}")
        W("vector", f"LK{sl}", ld_qt[pr])
        if te_nc_prev:
            W("vector", "TE", te_nc_prev)
        for h2 in range(2):
            v_ksq = O("vector", lambda e, sl=sl, h2=h2: e.tensor_tensor(
                GU[:, h2 * 2048:(h2 + 1) * 2048], abf(A_KT + sl * 4096 + h2 * 2048, 2048),
                abf(A_KT + sl * 4096 + h2 * 2048, 2048), ALU.mult), "VE")
        W("tensor", "VE", v_ksq)
        v_kacc = 0
        for i8 in range(8):
            te_c2 = O("tensor", lambda e, i8=i8: e.matmul(
                P6[0:2, 0:512], ONES2[:, :], GU[:, i8 * 512:(i8 + 1) * 512],
                start=True, stop=True), "TE")
            W("vector", "TE", te_c2)
            O("vector", lambda e: e.reduce_max(
                STAT[0:2, SC_TMP:SC_TMP + 1], P6[0:2, 0:512], axis=AX.X))
            vdrain()
            if i8 == 0:
                O("vector", lambda e: e.tensor_copy(
                    STAT[0:2, SC_KMX:SC_KMX + 1], STAT[0:2, SC_TMP:SC_TMP + 1]))
            else:
                O("vector", lambda e: e.tensor_tensor(
                    STAT[0:2, SC_KMX:SC_KMX + 1], STAT[0:2, SC_KMX:SC_KMX + 1],
                    STAT[0:2, SC_TMP:SC_TMP + 1], ALU.max))
            v_kacc = O("vector", lambda e: e.drain(), "VE")
            if i8 < 7:
                W("tensor", "VE", v_kacc)
        if s_exp_prev:
            W("vector", "SC", s_exp_prev)
        O("vector", lambda e, pr=pr: e.tensor_tensor(
            STAT[0:2, SC_TMP:SC_TMP + 1], STAT[0:2, SC_KMX:SC_KMX + 1],
            STAT[0:2, SC_QMX + 2 * pr:SC_QMX + 2 * pr + 1], ALU.add))
        vdrain()
        v_nc = O("vector", lambda e: e.tensor_scalar(
            STAT[0:2, SC_TMP + 1:SC_TMP + 2], STAT[0:2, SC_TMP:SC_TMP + 1],
            -1.0 / 64.0, 40.0, ALU.mult, ALU.add), "VE")
        W("gpsimd", "VE", v_nc)
        g_nc = dma("gpsimd", TGR[0:1, 1664:1666], STAT[0:2, SC_TMP + 1:SC_TMP + 2], "GD")
        W("tensor", "GD", g_nc)
        te_nc = None
        for h2 in range(2):
            te_nc = O("tensor", lambda e, h2=h2: e.matmul(
                P6[0:128, h2:h2 + 1], ONES[0:1, 0:128], TGR[0:1, 1664 + h2:1665 + h2],
                start=True, stop=True), "TE")
        te_nc_prev = te_nc
        W("vector", "TE", te_nc)
        v_negc = O("vector", lambda e: e.tensor_copy(
            STAT[:, SC_NEGC:SC_NEGC + 2], P6[0:128, 0:2]), "VE")

        for qb in range(4):
            s_exp = {}
            te_av = {}
            for kc in range(32):
                h, c16 = kc // 16, kc % 16
                pp = (kc % 2) * 2
                if kc == 0:
                    W("tensor", "VE", v_negc)
                    W("tensor", f"LK{sl}", ld_qt[pr])
                    prv = vev.get((pr, qb - 1)) if qb > 0 else vev.get((pr - 1, 3))
                    W("tensor", "VE", prv)
                W("tensor", "SC", s_exp.get(kc - 2))
                O("tensor", lambda e, sl=sl, h=h, c16=c16, pp=pp, qb=qb: e.matmul(
                    P[pp][:, :], abf(A_KT + sl * 4096 + h * 2048 + c16 * 128, 128, 0, 64),
                    abf(A_QT + sl * 2048 + qb * 512, 512, 0, 64),
                    start=True, stop=True, tile_position=(0, 0)))
                te_s = O("tensor", lambda e, sl=sl, h=h, c16=c16, pp=pp, qb=qb: e.matmul(
                    P[pp + 1][:, :], abf(A_KT + sl * 4096 + h * 2048 + c16 * 128, 128, 64, 128),
                    abf(A_QT + sl * 2048 + qb * 512, 512, 64, 128),
                    start=True, stop=True, tile_position=(64, 0)), "TE")
                W("scalar", "TE", te_s)
                if kc >= 2:
                    W("scalar", "TE", te_av[kc - 2])
                sp = (kc % 2) * 1024
                O("scalar", lambda e, pp=pp, sp=sp: e.activation(
                    abf(A_PT + sp, 512), P[pp][:, :], AF.Exp,
                    scale=ISCALE, bias=stc(SC_NEGC)[:, 0:1]))
                s_exp[kc] = O("scalar", lambda e, pp=pp, sp=sp: e.activation(
                    abf(A_PT + sp + 512, 512), P[pp + 1][:, :], AF.Exp,
                    scale=ISCALE, bias=stc(SC_NEGC + 1)[:, 0:1]), "SC")
                W("tensor", "SC", s_exp[kc])
                O("tensor", lambda e, sl=sl, h=h, c16=c16, sp=sp, kc=kc: e.matmul(
                    P45[0:65, 0:512], abf(A_V + sl * 4160 + h * 2080 + c16 * 130, 65),
                    abf(A_PT + sp, 512), start=(kc == 0), stop=(kc == 31)))
                te_av[kc] = O("tensor", lambda e, sl=sl, h=h, c16=c16, sp=sp, kc=kc: e.matmul(
                    P45[0:65, 512:1024], abf(A_V + sl * 4160 + h * 2080 + c16 * 130 + 65, 65),
                    abf(A_PT + sp + 512, 512), start=(kc == 0), stop=(kc == 31)), "TE")
            s_exp_prev = s_exp[31]
            v_last2 = 0
            for hh in range(2):
                if hh == 0:
                    W("vector", "TE", te_av[31])
                v_cx = O("vector", lambda e, hh=hh: e.tensor_copy(
                    CTX[0:65, :], P45[0:65, hh * 512:(hh + 1) * 512]), "VE")
                W("gpsimd", "VE", v_cx)
                g_rs = dma("gpsimd", TGR[0:1, 1536:2048], CTX[64:65, 0:512], "GD")
                W("vector", "GD", g_rs)
                O("vector", lambda e: e.reciprocal(
                    TGR[0:1, 1536:2048], TGR[0:1, 1536:2048]))
                v_rc = O("vector", lambda e: e.drain(), "VE")
                W("tensor", "VE", v_rc)
                te_rp = O("tensor", lambda e: e.matmul(
                    P6[0:64, 0:512], ONES[0:1, 0:64], TGR[0:1, 1536:2048],
                    start=True, stop=True), "TE")
                W("vector", "TE", te_rp)
                W("vector", f"SH{hh}", st_h[hh])
                v_h = O("vector", lambda e, hh=hh: e.tensor_tensor(
                    abf(A_HST + hh * 512, 512, 0, 64), CTX[0:64, :], P6[0:64, 0:512],
                    ALU.mult), "VE")
                W("scalar", "VE", v_h)
                st_h[hh] = dma("scalar",
                    htd[64 * hh:64 * (hh + 1),
                        pr * 2048 + qb * 512:pr * 2048 + qb * 512 + 512],
                    abf(A_HST + hh * 512, 512, 0, 64), f"SH{hh}")
                vdrain()
                v_last2 = v_h
            vev[(pr, qb)] = v_last2
        te_pr[pr] = C["TE"]

    W("sync", "TE", C["TE"])
    ld_wo = 0
    for ic in range(8):
        ld_wo = dma("sync", abf(A_GUQ + ic * 1024, 1024), wobd[:, ic * 1024:(ic + 1) * 1024])
    te_op = {}
    s_op = {}
    v_xo = {}
    for t in range(NT):
        p = t % 2
        W("sync", "TE", te_op.get(t - 2))
        if t == 0:
            W("sync", "SH0", C["SH0"])
            W("sync", "SH1", C["SH1"])
        ld_h = 0
        for fc in range(8):
            ld_h = dma("sync", abf(A_QT + p * 1024 + fc * 128, 128),
                       htd[:, fc * 2048 + t * 128:fc * 2048 + t * 128 + 128], f"LK{p}")
        W("tensor", f"LK{p}", ld_h)
        if t == 0:
            W("tensor", "LD", ld_wo)
        if t == 0:
            W("tensor", "SC", C["SC"])
            W("tensor", "VE", C["VE"])
        W("tensor", "VE", v_xo.get(t - 2))
        ob0 = P45[:, 0:512] if p == 0 else P[0][:, :]
        ob1 = P45[:, 512:1024] if p == 0 else P[1][:, :]
        for fc in range(8):
            O("tensor", lambda e, fc=fc, p=p, ob0=ob0: e.matmul(
                ob0, abf(A_QT + p * 1024 + fc * 128, 128),
                abf(A_GUQ + fc * 1024, 512), start=(fc == 0), stop=(fc == 7)))
            te_ot = O("tensor", lambda e, fc=fc, p=p, ob1=ob1: e.matmul(
                ob1, abf(A_QT + p * 1024 + fc * 128, 128),
                abf(A_GUQ + fc * 1024 + 512, 512), start=(fc == 0), stop=(fc == 7)),
                "TE" if fc == 7 else None)
        te_op[t] = te_ot
        W("scalar", "TE", te_op[t])
        W("scalar", "SC", s_op.get(t - 1, 0))
        W("scalar", "VE", v_xo.get(t - 1))
        s_a = O("scalar", lambda e, t=t: e.activation(
            XLN[:, 0:1024], X[:, t * 1024:(t + 1) * 1024], AF.Identity,
            bias=stc(SC_NMR2 + t)[:, 0:1], scale=stc(SC_RSTD2 + t)[:, 0:1]), "SC")
        s_op[t] = s_a
        W("vector", "SC", s_a)
        for oh in range(2):
            ob = (P45[:, oh * 512:(oh + 1) * 512]) if p == 0 else (P[oh][:, :])
            O("vector", lambda e, oh=oh, ob=ob: e.tensor_tensor(
                TGR[:, oh * 512:(oh + 1) * 512], ob,
                BB[:, 1024 + oh * 512:1536 + oh * 512], ALU.add))
        vdrain()
        O("vector", lambda e: e.tensor_tensor(
            TGR[:, 1024:2048], TGR[:, 0:1024], XLN[:, 0:1024], ALU.add))
        vdrain()
        v_xo[t] = O("vector", lambda e, t=t: e.tensor_tensor(
            X[:, t * 1024:(t + 1) * 1024], X[:, t * 1024:(t + 1) * 1024],
            TGR[:, 1024:2048], ALU.add), "VE")
        vdrain()
    fst["te_blk"] = te_op[NT - 1]
    fst["s_down"] = s_op[NT - 1]

    if stop_after != "attn":
        ln_prepass(SC_RSTD1, SC_NMR1)
        ffn(SC_RSTD1, SC_NMR1)

    W("scalar", "VE", C["VE"])
    for t in range(NT):
        dma("scalar", outd[128 * t:128 * t + 128, :], X[:, t * 1024:(t + 1) * 1024], "S20")
    for sn in ("S20", "S21", "SV0", "SV1", "SH0", "SH1"):
        W("sync", sn, C[sn])
        W("gpsimd", sn, C[sn])
    prog.emit_block()
    stack.close()
    return nc


def _pm(w, nj, nic):
    return np.ascontiguousarray(
        w.reshape(nj, 128, nic, 128).transpose(3, 0, 2, 1).reshape(128, nj * nic * 128))


def make_inputs(L0_attn_x, L0_attn_w_qkv, L0_attn_b_qkv, L0_attn_w_o, L0_attn_b_o,
                L0_ffn_w_gate, L0_ffn_w_up, L0_ffn_w_down, n_cores=8):
    x = np.ascontiguousarray(np.asarray(L0_attn_x, dtype=np.float32))
    w_qkv = np.asarray(L0_attn_w_qkv, dtype=np.float32)
    b_qkv = np.asarray(L0_attn_b_qkv, dtype=np.float32)
    w_o = np.asarray(L0_attn_w_o, dtype=np.float32)
    b_o = np.asarray(L0_attn_b_o, dtype=np.float32)
    w_gate = np.asarray(L0_ffn_w_gate, dtype=np.float32)
    w_up = np.asarray(L0_ffn_w_up, dtype=np.float32)
    w_down = np.asarray(L0_ffn_w_down, dtype=np.float32)

    shared = {
        "wgpm": _pm(w_gate, 32, 8),
        "wupm": _pm(w_up, 32, 8),
        "wdpm": np.ascontiguousarray(
            w_down.T.reshape(32, 128, 1024).transpose(1, 0, 2).reshape(128, 32 * 1024)),
        "wqkpm": _pm(w_qkv[0:2048], 16, 8),
        "wvpm": np.ascontiguousarray(
            w_qkv[2048:3072].T.reshape(8, 128, 1024).transpose(1, 0, 2).reshape(128, 8 * 1024)),
        "wopm": np.ascontiguousarray(
            w_o.T.reshape(8, 128, 1024).transpose(1, 0, 2).reshape(128, 8 * 1024)),
        "bqkc": np.ascontiguousarray(b_qkv[0:2048].reshape(16, 128).T),
        "bvr": np.ascontiguousarray(b_qkv[2048:3072].reshape(1, 1024)),
        "bor": np.ascontiguousarray(b_o.reshape(1, 1024)),
        "ident": np.eye(128, dtype=ml_dtypes.bfloat16),
        "identf": np.eye(128, dtype=np.float32),
    }
    in_maps = []
    for c in range(n_cores):
        b, h = c // 2, c % 2
        m = dict(shared)
        m["x"] = np.ascontiguousarray(x[b, 2048 * h:2048 * h + 2048])
        in_maps.append(m)
    return in_maps


def kernel(L0_attn_x, L0_attn_w_qkv, L0_attn_b_qkv, L0_attn_w_o, L0_attn_b_o,
           L0_ffn_w_gate, L0_ffn_w_up, L0_ffn_w_down):
    in_maps = make_inputs(L0_attn_x, L0_attn_w_qkv, L0_attn_b_qkv, L0_attn_w_o,
                          L0_attn_b_o, L0_ffn_w_gate, L0_ffn_w_up, L0_ffn_w_down)
    nc = bass.Bass("TRN2", target_bir_lowering=False)
    build(nc)
    res = run_bass_kernel_spmd(nc, in_maps, core_ids=list(range(8)))
    out = np.empty((4, 4096, 1024), dtype=np.float32)
    for c in range(8):
        b, h = c // 2, c % 2
        out[b, 2048 * h:2048 * h + 2048] = res.results[c]["out"]
    return out


# version 05
# speedup vs baseline: 1.0510x; 1.0510x over previous
import os
import contextlib

import numpy as np
import ml_dtypes
import concourse.bass as bass
import concourse.mybir as mybir
from concourse.bass_utils import run_bass_kernel_spmd

F32 = mybir.dt.float32
BF16 = mybir.dt.bfloat16
AF = mybir.ActivationFunctionType
ALU = mybir.AluOpType
AX = mybir.AxisListType

MAGIC = float(np.float32(1.5 * 2.0 ** 52)) if os.environ.get("BASS_SIM_FP64") == "1" \
    else float(np.float32(1.5 * 2.0 ** 23))
D = 1024
T = 2048
NT = 16
NB = 4
EPS = 1e-5
ISCALE = 1.0 / 32.0

A_XQT = 0
A_WG = 4096
A_WU = 6144
A_GUQ = 8192
A_VST = 16384
A_STG = 22528
A_KT = 0
A_V = 16384
A_V = 8192
A_QT = 16640
A_PT = 20736
A_HST = 22784

SC_RSTD1, SC_NMR1 = 0, 16
SC_RSTD2, SC_NMR2 = 32, 48
SC_SX, SC_INVSX = 64, 80
SC_S2 = 96
SC_INVS2 = 100
SC_WSC = 104
SC_TMP = 112
SC_QMX = 128
SC_KMX = 144
SC_NEGC = 148


class Prog:
    def __init__(self, nc, stack):
        self.nc = nc
        self.streams = {k: [] for k in ("sync", "tensor", "scalar", "vector", "gpsimd")}
        self.sems = {}
        self.cnt = {}
        self._stack = stack

    def sem(self, name):
        self.sems[name] = self._stack.enter_context(self.nc.semaphore(name=name))
        self.cnt[name] = 0

    def wait(self, eng, sname, val):
        if val is None or val <= 0:
            return
        h = self.sems[sname]
        self.streams[eng].append(lambda e, h=h, v=val: e.wait_ge(h, v))

    def op(self, eng, build, sname=None, n=1):
        if sname is None:
            self.streams[eng].append(build)
            return None
        h = self.sems[sname]
        self.cnt[sname] += n
        v = self.cnt[sname]

        def f(e, build=build, h=h, n=n):
            build(e).then_inc(h, n)

        self.streams[eng].append(f)
        return v

    def emit_block(self):
        with self.nc.Block() as block:
            @block.sync
            def _(e):
                for f in self.streams["sync"]:
                    f(e)

            @block.tensor
            def _(e):
                for f in self.streams["tensor"]:
                    f(e)

            @block.scalar
            def _(e):
                for f in self.streams["scalar"]:
                    f(e)

            @block.vector
            def _(e):
                for f in self.streams["vector"]:
                    f(e)

            @block.gpsimd
            def _(e):
                for f in self.streams["gpsimd"]:
                    f(e)


def build(nc, groups=None, stop_after=None):
    if groups is None:
        groups = [[0, 1], [2, 3], [4, 5], [6, 7]]
    dt = nc.dram_tensor
    xd = dt("x", [T, D], F32, kind="ExternalInput").ap()
    wgpm = dt("wgpm", [128, 32 * 1024], F32, kind="ExternalInput").ap()
    wupm = dt("wupm", [128, 32 * 1024], F32, kind="ExternalInput").ap()
    wdpm = dt("wdpm", [128, 32 * 1024], F32, kind="ExternalInput").ap()
    wqkpm = dt("wqkpm", [128, 16 * 1024], F32, kind="ExternalInput").ap()
    wvpm = dt("wvpm", [128, 8 * 1024], F32, kind="ExternalInput").ap()
    wopm = dt("wopm", [128, 8 * 1024], F32, kind="ExternalInput").ap()
    bqkc = dt("bqkc", [128, 16], F32, kind="ExternalInput").ap()
    bvr = dt("bvr", [1, D], F32, kind="ExternalInput").ap()
    bord = dt("bor", [1, D], F32, kind="ExternalInput").ap()
    identd = dt("ident", [128, 128], BF16, kind="ExternalInput").ap()
    identf = dt("identf", [128, 128], F32, kind="ExternalInput").ap()
    outd = dt("out", [T, D], F32, kind="ExternalOutput").ap()

    tgd = dt("tgs", [128, 32 * 1024], BF16, kind="Internal").ap()
    tud = dt("tus", [128, 32 * 1024], BF16, kind="Internal").ap()
    tdd = dt("tds", [128, 32 * 1024], BF16, kind="Internal").ap()
    wqkb = dt("wqkbs", [128, 16 * 1024], BF16, kind="Internal").ap()
    wvbd = dt("wvbs", [128, 8 * 1024], BF16, kind="Internal").ap()
    wobd = dt("wobs", [128, 8 * 1024], BF16, kind="Internal").ap()
    qtd = dt("qts", [128, 8 * 2048], BF16, kind="Internal").ap()
    htd = dt("hts", [128, 8 * 2048], BF16, kind="Internal").ap()
    kvin = dt("kvin", [128, 8 * 2048 + 8 * 2080], BF16, kind="Internal").ap()
    kvout = dt("kvout", [256, 8 * 2048 + 8 * 2080], BF16, kind="Internal").ap()
    kkin = kvin[:, 0:8 * 2048]
    vvin = kvin[:, 8 * 2048:]
    kkout = kvout[:, 0:8 * 2048]
    vvout = kvout[:, 8 * 2048:]

    stack = contextlib.ExitStack()
    sbuf = lambda name, shape, dty: stack.enter_context(nc.sbuf_tensor(name, shape, dty))
    pst = lambda name, shape, dty=F32: stack.enter_context(nc.psum_tensor(name, shape, dty))

    X = sbuf("X", [128, NT * 1024], F32)
    GU = sbuf("GU", [128, 32 * 512], F32)
    XLN = sbuf("XLN", [128, 2 * 1024], F32)
    TGR = sbuf("TGR", [128, 2048], F32)
    CTX = sbuf("CTX", [128, 512], F32)
    MNA = sbuf("MNA", [128, 512], F32)
    STAT = sbuf("STAT", [128, 160], F32)
    IBC = sbuf("IBC", [128, 1024], F32)
    ABF = sbuf("ABF", [128, 24576], BF16)
    BB = sbuf("BB", [128, 2048], BF16)
    IDT = sbuf("IDT", [128, 128], BF16)
    IDF = sbuf("IDF", [128, 128], F32)
    BQK = sbuf("BQK", [128, 16], F32)
    ONES = sbuf("ONES", [128, 128], F32)
    ONES2 = sbuf("ONES2", [128, 2], F32)

    P = [pst(f"P{i}", [128, 512]) for i in range(2)]
    PA = pst("PA", [128, 1024])
    PB = pst("PB", [128, 1024])
    P6 = pst("P6", [128, 512])
    PT8 = pst("PT8", [128, 1024], BF16)

    prog = Prog(nc, stack)
    for s in ("LD", "LW0", "LW1", "LK0", "LK1", "GD", "S20", "S21",
              "SV0", "SV1", "SH0", "SH1", "TE", "SC", "VE", "CC"):
        prog.sem(s)
    W, O, C = prog.wait, prog.op, prog.cnt

    def dma(eng, out, in_, sname="LD", **kw):
        return O(eng, lambda e, o=out, i=in_, kw=kw: e.dma_start(out=o, in_=i, **kw),
                 sname, 16)

    def vdrain():
        O("vector", lambda e: e.drain())

    def stc(c):
        return STAT[:, c:c + 1]

    def abf(lo, n, p0=0, p1=128):
        return ABF[p0:p1, lo:lo + n]

    dma("sync", IDT[:], identd[:])
    dma("sync", IDF[:], identf[:])
    for t in range(NT):
        dma("sync", X[:, t * 1024:(t + 1) * 1024], xd[128 * t:128 * t + 128, :])
    dma("sync", BQK[:], bqkc[:])
    dma("sync", TGR[0:1, 0:1024], bvr[:])
    ld_b = dma("sync", TGR[0:1, 1024:2048], bord[:])
    O("vector", lambda e: e.memset(ONES2[:, :], 0.0))
    vdrain()
    O("vector", lambda e: e.memset(ONES2[0:64, 0:1], 1.0))
    vdrain()
    O("vector", lambda e: e.memset(ONES2[64:128, 1:2], 1.0))
    vdrain()
    v_ones = O("vector", lambda e: e.memset(ONES[:], 1.0), "VE")
    W("tensor", "VE", v_ones)
    W("tensor", "LD", ld_b)
    v_bb = 0
    for h in range(2):
        W("tensor", "VE", v_bb)
        O("tensor", lambda e, h=h: e.matmul(
            P[0][:, :], ONES[0:1, 0:128], TGR[0:1, 512 * h:512 * (h + 1)],
            start=True, stop=True))
        te_b = O("tensor", lambda e, h=h: e.matmul(
            P[1][:, :], ONES[0:1, 0:128], TGR[0:1, 1024 + 512 * h:1536 + 512 * h],
            start=True, stop=True), "TE")
        W("vector", "TE", te_b)
        O("vector", lambda e, h=h: e.tensor_copy(BB[:, 512 * h:512 * (h + 1)], P[0][:, :]))
        v_bb = O("vector", lambda e, h=h: e.tensor_copy(
            BB[:, 1024 + 512 * h:1536 + 512 * h], P[1][:, :]), "VE")
    ve_bias = C["VE"]

    NCOL = 128 * 32 * 1024

    def wprep(src, dst, wsc_col, first_wait):
        v_red = [0] * 8
        for i in range(8):
            p = i % 2
            W("sync", "VE", v_red[i - 2] if i >= 2 else first_wait)
            lv = dma("sync", GU[:, p * 4096:(p + 1) * 4096], src[:, i * 4096:(i + 1) * 4096],
                     f"LW{p}")
            W("vector", f"LW{p}", lv)
            v_red[i] = O("vector", lambda e, i=i, p=p: e.reduce_sum(
                stc(SC_TMP + i), GU[:, p * 4096:(p + 1) * 4096], axis=AX.X,
                apply_absolute_value=True), "VE")
        vdrain()
        O("vector", lambda e: e.reduce_sum(
            stc(SC_TMP + 8), STAT[:, SC_TMP:SC_TMP + 8], axis=AX.X))
        v_s = O("vector", lambda e: e.drain(), "VE")
        W("tensor", "VE", v_s)
        te_s = O("tensor", lambda e: e.matmul(
            P6[0:1, 0:1], stc(SC_TMP + 8), ONES[:, 0:1], start=True, stop=True), "TE")
        W("vector", "TE", te_s)
        O("vector", lambda e: e.tensor_scalar(
            stc(SC_TMP + 9)[0:1, :], P6[0:1, 0:1], 1.0 / NCOL, 1e-5, ALU.mult, ALU.max))
        vdrain()
        v_r = O("vector", lambda e: e.reciprocal(
            stc(SC_TMP + 10)[0:1, :], stc(SC_TMP + 9)[0:1, :]), "VE")
        W("tensor", "VE", v_r)
        O("tensor", lambda e: e.matmul(
            P6[0:128, 0:1], ONES[0:1, 0:128], stc(SC_TMP + 10)[0:1, :],
            start=True, stop=True))
        te_b2 = O("tensor", lambda e: e.matmul(
            P6[0:128, 1:2], ONES[0:1, 0:128], stc(SC_TMP + 9)[0:1, :],
            start=True, stop=True), "TE")
        W("vector", "TE", te_b2)
        O("vector", lambda e: e.tensor_copy(stc(SC_TMP + 11), P6[0:128, 0:1]))
        v_bc = O("vector", lambda e, c=wsc_col: e.tensor_copy(stc(c), P6[0:128, 1:2]), "VE")
        st_prev = [C["S20"], C["S21"]]
        s_done = [0] * 8
        v_done = [0] * 8
        for i in range(8):
            p = i % 2
            W("sync", "VE", v_done[i - 2] if i >= 2 else v_bc)
            lv = dma("sync", GU[:, p * 4096:(p + 1) * 4096], src[:, i * 4096:(i + 1) * 4096],
                     f"LW{p}")
            W("scalar", f"LW{p}", lv)
            W("scalar", "VE", v_done[i - 1] if i >= 1 else v_bc)
            s_done[i] = O("scalar", lambda e, p=p: e.activation(
                GU[:, 8192 + p * 4096:8192 + (p + 1) * 4096],
                GU[:, p * 4096:(p + 1) * 4096],
                AF.Copy, bias=MAGIC, scale=stc(SC_TMP + 11)[:, 0:1]), "SC")
            W("vector", "SC", s_done[i])
            W("vector", f"S2{p}", st_prev[p])
            O("vector", lambda e, p=p: e.tensor_scalar(
                abf(p * 4096, 4096), GU[:, 8192 + p * 4096:8192 + (p + 1) * 4096],
                -MAGIC, 1.0, ALU.add, ALU.min))
            vdrain()
            v_done[i] = O("vector", lambda e, p=p: e.tensor_scalar(
                abf(p * 4096, 4096), abf(p * 4096, 4096), -1.0, None, ALU.max), "VE")
            W("scalar", "VE", v_done[i])
            st_prev[p] = dma("scalar", dst[:, i * 4096:(i + 1) * 4096],
                             abf(p * 4096, 4096), f"S2{p}")
        return v_done[7]

    v_p = wprep(wgpm, tgd, SC_WSC + 0, ve_bias)
    v_p = wprep(wupm, tud, SC_WSC + 1, v_p)
    v_p = wprep(wdpm, tdd, SC_WSC + 2, v_p)
    vdrain()
    O("vector", lambda e: e.tensor_scalar(
        stc(SC_WSC + 2), stc(SC_WSC + 2), 0.5, None, ALU.mult), "VE")

    def wcast(src, dst, ncols, v_first):
        v_prev = [0, 0]
        st_prev = [C["S20"], C["S21"]]
        for i in range(ncols // 4096):
            p = i % 2
            W("sync", "VE", v_prev[p] if v_prev[p] else v_first)
            lv = dma("sync", GU[:, p * 4096:(p + 1) * 4096], src[:, i * 4096:(i + 1) * 4096],
                     f"LW{p}")
            W("vector", f"LW{p}", lv)
            W("vector", f"S2{p}", st_prev[p])
            v_prev[p] = O("vector", lambda e, p=p: e.tensor_copy(
                abf(p * 4096, 4096), GU[:, p * 4096:(p + 1) * 4096]), "VE")
            W("scalar", "VE", v_prev[p])
            st_prev[p] = dma("scalar", dst[:, i * 4096:(i + 1) * 4096],
                             abf(p * 4096, 4096), f"S2{p}")
        return v_prev[(ncols // 4096 - 1) % 2]

    v_c = wcast(wqkpm, wqkb, 16 * 1024, C["VE"])
    v_c = wcast(wvpm, wvbd, 8 * 1024, v_c)
    v_c = wcast(wopm, wobd, 8 * 1024, v_c)

    def ln_prepass(rstd_base, nmr_base):
        W("vector", "LD", C["LD"])
        W("scalar", "VE", C["VE"])
        W("scalar", "LW0", C["LW0"])
        W("scalar", "LW1", C["LW1"])
        for t in range(NT):
            O("vector", lambda e, t=t: e.reduce_sum(
                stc(SC_TMP + t), X[:, t * 1024:(t + 1) * 1024], axis=AX.X))
        v0 = O("vector", lambda e: e.drain(), "VE")
        s0 = 0
        s_hist = {}
        for t in range(NT):
            W("scalar", "SC", s_hist.get(t - 4, 0))
            s_hist[t] = O("scalar", lambda e, t=t: e.activation(
                GU[:, (t % 4) * 1024:(t % 4) * 1024 + 1024],
                X[:, t * 1024:(t + 1) * 1024], AF.Square,
                accum_out=stc(nmr_base + t)), "SC")
        s0 = s_hist[NT - 1]
        W("vector", "SC", s0)
        TM = STAT[:, SC_TMP:SC_TMP + 16]
        SQ = STAT[:, nmr_base:nmr_base + 16]
        MM = STAT[:, SC_SX:SC_SX + 16]
        O("vector", lambda e, TM=TM: e.tensor_scalar(TM, TM, 1.0 / D, None, ALU.mult))
        O("vector", lambda e, SQ=SQ: e.tensor_scalar(SQ, SQ, 1.0 / D, None, ALU.mult))
        vdrain()
        O("vector", lambda e, TM=TM, MM=MM: e.tensor_tensor(MM, TM, TM, ALU.mult))
        vdrain()
        O("vector", lambda e, SQ=SQ, MM=MM: e.tensor_tensor(SQ, SQ, MM, ALU.subtract))
        vdrain()
        O("vector", lambda e, SQ=SQ: e.tensor_scalar(SQ, SQ, EPS, None, ALU.add))
        vdrain()
        v1 = O("vector", lambda e, SQ=SQ, MM=MM: e.reciprocal(MM, SQ), "VE")
        W("scalar", "VE", v1)
        s1 = O("scalar", lambda e, MM=MM, rb=rstd_base: e.activation(
            STAT[:, rb:rb + 16], MM, AF.Sqrt), "SC")
        W("vector", "SC", s1)
        O("vector", lambda e, TM=TM, rb=rstd_base, nb=nmr_base: e.scalar_tensor_tensor(
            STAT[:, nb:nb + 16], TM, -1.0, STAT[:, rb:rb + 16], ALU.mult, ALU.mult), "VE")
        vdrain()

    st8 = {"seq": 0, "v_xl": {}, "te_tp": {}, "v_cp": {}, "v_tg": {}}

    def xln_tile(t, ti, rstd_base, nmr_base, quant, entry_te=None):
        k = st8["seq"]
        st8["seq"] += 1
        sl = k % 2
        xt = X[:, t * 1024:(t + 1) * 1024]
        xl = XLN[:, sl * 1024:(sl + 1) * 1024]
        W("scalar", "VE", st8["v_xl"].get(k - 2))
        s0 = O("scalar", lambda e, xt=xt, xl=xl, t=t: e.activation(
            xl, xt, AF.Identity, bias=stc(nmr_base + t)[:, 0:1],
            scale=stc(rstd_base + t)[:, 0:1]), "SC")
        W("vector", "SC", s0)
        if entry_te:
            W("vector", "TE", entry_te)
        if quant:
            O("vector", lambda e, xl=xl: e.reduce_max(
                stc(SC_TMP + 0), xl, axis=AX.X, apply_absolute_value=True))
            vdrain()
            O("vector", lambda e: e.tensor_scalar(
                stc(SC_TMP + 1), stc(SC_TMP + 0), 1e-5, None, ALU.max))
            vdrain()
            O("vector", lambda e, t=t: e.tensor_scalar(
                stc(SC_INVSX + t % 16), stc(SC_TMP + 1), 1.0 / 127.0, None, ALU.mult))
            O("vector", lambda e: e.reciprocal(stc(SC_TMP + 2), stc(SC_TMP + 1)))
            vdrain()
            v0 = O("vector", lambda e, t=t: e.tensor_scalar(
                stc(SC_SX + t % 16), stc(SC_TMP + 2), 127.0, None, ALU.mult), "VE")
            st8["v_xl"][k] = v0
            W("scalar", "VE", v0)
            W("scalar", "VE", st8["v_tg"].get(k - 1))
            s1 = O("scalar", lambda e, xl=xl, t=t: e.activation(
                TGR[:, 0:1024], xl, AF.Copy, bias=MAGIC,
                scale=stc(SC_SX + t % 16)[:, 0:1]), "SC")
            W("vector", "SC", s1)
            W("vector", "TE", st8["te_tp"].get(k - 2))
            v1 = O("vector", lambda e, sl=sl: e.tensor_scalar(
                abf(A_STG + sl * 1024, 1024), TGR[:, 0:1024], -MAGIC, None, ALU.add), "VE")
            st8["v_tg"][k] = v1
        else:
            W("vector", "TE", st8["te_tp"].get(k - 2))
            v1 = O("vector", lambda e, xl=xl, sl=sl: e.tensor_copy(
                abf(A_STG + sl * 1024, 1024), xl), "VE")
            st8["v_xl"][k] = v1
        W("tensor", "VE", v1)
        W("tensor", "VE", st8["v_cp"].get(k - 1))
        for fc in range(8):
            te_x = O("tensor", lambda e, fc=fc, sl=sl: e.transpose(
                PT8[:, fc * 128:(fc + 1) * 128],
                abf(A_STG + sl * 1024 + fc * 128, 128), IDT[:]),
                "TE" if fc == 7 else None)
        st8["te_tp"][k] = te_x
        W("vector", "TE", st8["te_tp"][k])
        v_c = 0
        for fc in range(8):
            v_c = O("vector", lambda e, fc=fc, ti=ti: e.tensor_copy(
                abf(A_XQT + fc * 512 + ti * 128, 128), PT8[:, fc * 128:(fc + 1) * 128]),
                "VE" if fc == 7 else None)
        st8["v_cp"][k] = v_c
        return v_c

    def bcast4(col0, ibc_lo, psum_guard=0):
        vdrain()
        v_a = O("vector", lambda e, c=col0: e.tensor_copy(
            TGR[:, 1024:1028], STAT[:, c:c + 4]), "VE")
        W("tensor", "VE", v_a)
        W("tensor", "SC", psum_guard)
        for q in range(4):
            te_a = O("tensor", lambda e, q=q: e.transpose(
                P6[:, q * 128:(q + 1) * 128], TGR[:, 1024 + q:1024 + q + 128], IDF[:]),
                "TE" if q == 3 else None)
        W("vector", "TE", te_a)
        for q in range(4):
            O("vector", lambda e, q=q: e.tensor_copy(
                TGR[0:1, 1152 + q * 128:1152 + (q + 1) * 128],
                P6[0:1, q * 128:(q + 1) * 128]))
        v_b = O("vector", lambda e: e.drain(), "VE")
        W("tensor", "VE", v_b)
        te_c = O("tensor", lambda e: e.matmul(
            P6[:, 0:512], ONES[0:1, 0:128], TGR[0:1, 1152:1664],
            start=True, stop=True), "TE")
        W("vector", "TE", te_c)
        return O("vector", lambda e, lo=ibc_lo: e.tensor_copy(
            IBC[:, lo:lo + 512], P6[:, 0:512]), "VE")

    fst = {"call": 0, "te_gu": None, "s_down": 0, "te_dlast": None, "te_blk": None}

    def ffn(rstd_base, nmr_base):
        W("scalar", "VE", C["VE"])
        W("sync", "TE", C["TE"])
        for sn in ("S20", "S21", "SV0", "SV1", "SH0", "SH1"):
            W("sync", sn, C[sn])
            W("vector", sn, C[sn])
        for blk in range(NB):
            v_cp_last = 0
            for ti in range(4):
                t = blk * 4 + ti
                v_cp_last = xln_tile(t, ti, rstd_base, nmr_base, True,
                                     entry_te=(fst["te_blk"] if ti == 0 else None))
            v_ibc = bcast4(SC_INVSX + (blk * 4) % 16, 0, psum_guard=fst["s_down"])
            te_j = {}
            v_pu = {}
            s_g = {}
            v_mm = {}
            for j in range(32):
                p = j % 2
                W("sync", "TE", te_j.get(j - 2, fst["te_dlast"] if j < 2 else None))
                dma("sync", abf(A_WG + p * 1024, 1024), tgd[:, j * 1024:(j + 1) * 1024],
                    f"LW{p}")
                ld_w = dma("sync", abf(A_WU + p * 1024, 1024), tud[:, j * 1024:(j + 1) * 1024],
                           f"LW{p}")
                gb = PA[:, 0:512] if p == 0 else PB[:, 0:512]
                ub = PA[:, 512:1024] if p == 0 else PB[:, 512:1024]
                W("tensor", f"LW{p}", ld_w)
                if j == 0:
                    W("tensor", "VE", v_ibc)
                    W("tensor", "SC", fst["s_down"])
                W("tensor", "VE", v_pu.get(j - 2))
                for ic in range(8):
                    O("tensor", lambda e, ic=ic, p=p, gb=gb: e.matmul(
                        gb, abf(A_WG + p * 1024 + ic * 128, 128),
                        abf(A_XQT + ic * 512, 512), start=(ic == 0), stop=(ic == 7)))
                for ic in range(8):
                    te_jj = O("tensor", lambda e, ic=ic, p=p, ub=ub: e.matmul(
                        ub, abf(A_WU + p * 1024 + ic * 128, 128),
                        abf(A_XQT + ic * 512, 512), start=(ic == 0), stop=(ic == 7)),
                        "TE" if ic == 7 else None)
                te_j[j] = te_jj
                W("vector", "TE", te_j[j])
                W("vector", "SC", s_g.get(j - 1))
                O("vector", lambda e, gb=gb: e.tensor_tensor(
                    TGR[:, 1024:1536], gb, IBC[:, 0:512], ALU.mult))
                O("vector", lambda e, ub=ub: e.tensor_tensor(
                    TGR[:, 0:512], ub, IBC[:, 0:512], ALU.mult))
                v_pu[j] = O("vector", lambda e: e.drain(), "VE")
                W("scalar", "VE", v_pu[j])
                s_g[j] = O("scalar", lambda e: e.activation(
                    TGR[:, 1536:2048], TGR[:, 1024:1536], AF.Silu,
                    scale=stc(SC_WSC + 0)[:, 0:1]), "SC")
                W("vector", "SC", s_g[j])
                O("vector", lambda e, j=j: e.scalar_tensor_tensor(
                    GU[:, j * 512:(j + 1) * 512], TGR[:, 0:512],
                    stc(SC_WSC + 1)[:, 0:1], TGR[:, 1536:2048], ALU.mult, ALU.mult))
                vdrain()
                if j == 0:
                    O("vector", lambda e: e.tensor_copy(CTX[:, :], GU[:, 0:512]))
                    v_mm[j] = O("vector", lambda e: e.tensor_copy(MNA[:, :], GU[:, 0:512]), "VE")
                else:
                    O("vector", lambda e, j=j: e.tensor_tensor(
                        CTX[:, :], CTX[:, :], GU[:, j * 512:(j + 1) * 512], ALU.max))
                    v_mm[j] = O("vector", lambda e, j=j: e.tensor_tensor(
                        MNA[:, :], MNA[:, :], GU[:, j * 512:(j + 1) * 512], ALU.min), "VE")
                vdrain()
            fst["te_gu"] = te_j[31]

            O("vector", lambda e: e.tensor_scalar(MNA[:, :], MNA[:, :], -1.0, None, ALU.mult))
            vdrain()
            O("vector", lambda e: e.tensor_tensor(CTX[:, :], CTX[:, :], MNA[:, :], ALU.max))
            vdrain()
            v_mx = O("vector", lambda e: e.drain(), "VE")
            W("tensor", "VE", v_mx)
            W("tensor", "SC", fst["s_down"])
            for q in range(4):
                te_mx = O("tensor", lambda e, q=q: e.transpose(
                    P6[:, q * 128:(q + 1) * 128], CTX[:, q * 128:(q + 1) * 128], IDF[:]),
                    "TE" if q == 3 else None)
            W("vector", "TE", te_mx)
            for q in range(4):
                O("vector", lambda e, q=q: e.reduce_max(
                    stc(SC_TMP + q), P6[:, q * 128:(q + 1) * 128], axis=AX.X))
            vdrain()
            for q in range(4):
                O("vector", lambda e, q=q: e.tensor_scalar(
                    stc(SC_TMP + 4 + q), stc(SC_TMP + q), 1e-5, None, ALU.max))
            vdrain()
            for q in range(4):
                O("vector", lambda e, q=q: e.tensor_scalar(
                    stc(SC_INVS2 + q), stc(SC_TMP + 4 + q), 1.0 / 127.0, None, ALU.mult))
                O("vector", lambda e, q=q: e.reciprocal(
                    stc(SC_TMP + 8 + q), stc(SC_TMP + 4 + q)))
            vdrain()
            for q in range(4):
                O("vector", lambda e, q=q: e.tensor_scalar(
                    stc(SC_TMP + q), stc(SC_TMP + 8 + q), 127.0, None, ALU.mult))
                O("vector", lambda e, q=q: e.tensor_tensor(
                    stc(SC_S2 + q), stc(SC_INVS2 + q), stc(SC_WSC + 2), ALU.mult))
            v_s2bc = bcast4(SC_TMP, 512, psum_guard=fst["s_down"])

            for hf in range(2):
                v_rnd = {}
                s_t2 = {}
                for j in range(32):
                    if j == 0:
                        W("vector", "VE", v_s2bc)
                        if fst["te_dlast"]:
                            W("vector", "TE", fst["te_dlast"])
                    W("vector", "SC", s_t2.get(j - 1))
                    O("vector", lambda e, j=j, hf=hf: e.tensor_tensor(
                        TGR[:, 1024:1280],
                        GU[:, j * 512 + hf * 256:j * 512 + hf * 256 + 256],
                        IBC[:, 512 + hf * 256:512 + hf * 256 + 256], ALU.mult))
                    v_t1 = O("vector", lambda e: e.drain(), "VE")
                    W("scalar", "VE", v_t1)
                    s_t2[j] = O("scalar", lambda e: e.activation(
                        TGR[:, 1280:1536], TGR[:, 1024:1280], AF.Copy, bias=MAGIC), "SC")
                    W("vector", "SC", s_t2[j])
                    v_rnd[j] = O("vector", lambda e, j=j: e.tensor_scalar(
                        abf(A_GUQ + j * 256, 256), TGR[:, 1280:1536],
                        -MAGIC, None, ALU.add), "VE")
                te_dj = {}
                wdsl = A_WG if hf == 0 else A_WU
                for j in range(32):
                    p = j % 2
                    W("sync", "TE", te_dj.get(
                        j - 2, (fst["te_gu"] if hf == 0 else fst["te_dlast"])
                        if j < 2 else None))
                    lw = dma("sync", abf(wdsl + p * 1024, 1024),
                             tdd[:, j * 1024:(j + 1) * 1024], f"LW{p}")
                    W("tensor", f"LW{p}", lw)
                    W("tensor", "VE", v_rnd[j])
                    if j == 0:
                        W("tensor", "SC", fst["s_down"])
                    O("tensor", lambda e, j=j, p=p, w=wdsl: e.matmul(
                        PA[:, 0:512], abf(A_GUQ + j * 256, 128),
                        abf(w + p * 1024, 512), start=(j == 0), stop=(j == 31)))
                    O("tensor", lambda e, j=j, p=p, w=wdsl: e.matmul(
                        PA[:, 512:1024], abf(A_GUQ + j * 256, 128),
                        abf(w + p * 1024 + 512, 512), start=(j == 0), stop=(j == 31)))
                    O("tensor", lambda e, j=j, p=p, w=wdsl: e.matmul(
                        P[0][:, :], abf(A_GUQ + j * 256 + 128, 128),
                        abf(w + p * 1024, 512), start=(j == 0), stop=(j == 31)))
                    te_dj[j] = O("tensor", lambda e, j=j, p=p, w=wdsl: e.matmul(
                        P[1][:, :], abf(A_GUQ + j * 256 + 128, 128),
                        abf(w + p * 1024 + 512, 512), start=(j == 0), stop=(j == 31)), "TE")
                fst["te_dlast"] = te_dj[31]
                W("scalar", "TE", te_dj[31])
                W("scalar", "VE", v_s2bc)
                v_x = 0
                for tt2 in range(2):
                    tt = hf * 2 + tt2
                    t = blk * 4 + tt
                    if tt2 == 1:
                        W("scalar", "VE", v_x)
                    if tt2 == 0:
                        O("scalar", lambda e, tt=tt: e.activation(
                            TGR[:, 1024:1536], PA[:, 0:512], AF.Identity,
                            scale=stc(SC_S2 + tt)[:, 0:1]))
                        s_e = O("scalar", lambda e, tt=tt: e.activation(
                            TGR[:, 1536:2048], PA[:, 512:1024], AF.Identity,
                            scale=stc(SC_S2 + tt)[:, 0:1]), "SC")
                    else:
                        O("scalar", lambda e, tt=tt: e.activation(
                            TGR[:, 1024:1536], P[0][:, :], AF.Identity,
                            scale=stc(SC_S2 + tt)[:, 0:1]))
                        s_e = O("scalar", lambda e, tt=tt: e.activation(
                            TGR[:, 1536:2048], P[1][:, :], AF.Identity,
                            scale=stc(SC_S2 + tt)[:, 0:1]), "SC")
                    fst["s_down"] = s_e
                    W("vector", "SC", s_e)
                    v_x = O("vector", lambda e, t=t: e.tensor_tensor(
                        X[:, t * 1024:(t + 1) * 1024], X[:, t * 1024:(t + 1) * 1024],
                        TGR[:, 1024:2048], ALU.add), "VE")
                    vdrain()
            fst["te_blk"] = fst["te_dlast"]
        return v_x

    W("vector", "S20", C["S20"])
    W("vector", "S21", C["S21"])
    ln_prepass(SC_RSTD1, SC_NMR1)
    v_last = ffn(SC_RSTD1, SC_NMR1)

    if stop_after == "ffn1":
        W("scalar", "VE", C["VE"])
        for t in range(NT):
            dma("scalar", outd[128 * t:128 * t + 128, :], X[:, t * 1024:(t + 1) * 1024], "S20")
        W("sync", "S20", C["S20"])
        W("gpsimd", "S20", C["S20"])
        prog.emit_block()
        stack.close()
        return nc

    ln_prepass(SC_RSTD2, SC_NMR2)

    W("sync", "TE", C["TE"])
    W("sync", "S20", C["S20"])
    W("sync", "S21", C["S21"])
    ld_wv = 0
    for ic in range(8):
        ld_wv = dma("sync", abf(A_GUQ + ic * 1024, 1024), wvbd[:, ic * 1024:(ic + 1) * 1024])

    st_kq = [0, 0]
    st_v = [0, 0]
    v_qrd = [0, 0]
    W("scalar", "VE", C["VE"])
    te_o = {}
    s_ev = {}
    v_vs = {}
    te_vlast = None
    for blk in range(NB):
        v_cp_last = 0
        W("vector", "S20", st_kq[0])
        W("vector", "S21", st_kq[1])
        for ti in range(4):
            t = blk * 4 + ti
            v_cp_last = xln_tile(t, ti, SC_RSTD2, SC_NMR2, False,
                                 entry_te=(te_vlast if ti == 0 else None))
        for i, oc in enumerate(list(range(8, 16)) + list(range(0, 8))):
            g = blk * 16 + i
            p = g % 2
            W("sync", "TE", te_o.get(g - 2))
            ld_w = dma("sync", abf(A_WG + p * 1024, 1024), wqkb[:, oc * 1024:(oc + 1) * 1024],
                       f"LW{p}")
            W("tensor", f"LW{p}", ld_w)
            if i == 0:
                W("tensor", "VE", v_cp_last)
                if blk == 0:
                    W("tensor", "LD", ld_wv)
                    W("tensor", "SC", fst["s_down"])
            W("tensor", "SC", s_ev.get(g - 2))
            for ic in range(8):
                te_og = O("tensor", lambda e, ic=ic, p=p: e.matmul(
                    P[p][:, :], abf(A_WG + p * 1024 + ic * 128, 128),
                    abf(A_XQT + ic * 512, 512), start=(ic == 0), stop=(ic == 7)),
                    "TE" if ic == 7 else None)
            te_o[g] = te_og
            W("scalar", "TE", te_o[g])
            W("scalar", f"S2{g % 2}", st_kq[g % 2])
            W("scalar", "VE", v_qrd[g % 2])
            sl = g % 2
            s_ev[g] = O("scalar", lambda e, p=p, oc=oc, sl=sl: e.activation(
                abf(A_STG + sl * 1024, 512), P[p][:, :], AF.Identity,
                bias=BQK[:, oc:oc + 1]), "SC")
            if oc >= 8:
                W("scalar", "SC", s_ev[g])
                st_kq[g % 2] = dma(
                    "scalar",
                    kkin[:, (oc - 8) * 2048 + blk * 512:(oc - 8) * 2048 + blk * 512 + 512],
                    abf(A_STG + sl * 1024, 512), f"S2{g % 2}")
            else:
                W("vector", "SC", s_ev[g])
                v_sq = O("vector", lambda e, sl=sl: e.tensor_tensor(
                    TGR[:, 0:512], abf(A_STG + sl * 1024, 512),
                    abf(A_STG + sl * 1024, 512), ALU.mult), "VE")
                W("tensor", "VE", v_sq)
                te_q2 = O("tensor", lambda e: e.matmul(
                    P6[0:2, 0:512], ONES2[:, :], TGR[:, 0:512],
                    start=True, stop=True), "TE")
                W("vector", "TE", te_q2)
                O("vector", lambda e: e.reduce_max(
                    STAT[0:2, SC_TMP:SC_TMP + 1], P6[0:2, 0:512], axis=AX.X))
                vdrain()
                if blk == 0:
                    O("vector", lambda e, oc=oc: e.tensor_copy(
                        STAT[0:2, SC_QMX + 2 * oc:SC_QMX + 2 * oc + 1],
                        STAT[0:2, SC_TMP:SC_TMP + 1]), "VE")
                else:
                    O("vector", lambda e, oc=oc: e.tensor_tensor(
                        STAT[0:2, SC_QMX + 2 * oc:SC_QMX + 2 * oc + 1],
                        STAT[0:2, SC_QMX + 2 * oc:SC_QMX + 2 * oc + 1],
                        STAT[0:2, SC_TMP:SC_TMP + 1], ALU.max), "VE")
                v_qrd[g % 2] = O("vector", lambda e: e.drain(), "VE")
                W("scalar", "SC", s_ev[g])
                st_kq[g % 2] = dma("scalar",
                            qtd[:, oc * 2048 + blk * 512:oc * 2048 + blk * 512 + 512],
                            abf(A_STG + sl * 1024, 512), f"S2{g % 2}")
        for tt in range(4):
            t = blk * 4 + tt
            g2 = blk * 4 + tt
            sl = g2 % 2
            W("vector", f"SV{sl}", st_v[sl])
            v_ms = O("vector", lambda e, sl=sl: e.memset(abf(A_VST + sl * 1040, 1040), 1.0), "VE")
            for oh in range(2):
                pbank = PA if g2 % 2 == 0 else PB
                pb = pbank[:, oh * 512:(oh + 1) * 512]
                W("tensor", "VE", v_vs.get((g2 - 2, oh)))
                if g2 == 0 and oh == 0:
                    W("tensor", "VE", v_ms)
                for ic in range(8):
                    te_v = O("tensor", lambda e, ic=ic, tt=tt, oh=oh, pb=pb: e.matmul(
                        pb, abf(A_XQT + ic * 512 + tt * 128, 128),
                        abf(A_GUQ + ic * 1024 + oh * 512, 512),
                        start=(ic == 0), stop=(ic == 7)), "TE" if ic == 7 else None)
                te_vlast = te_v
                W("vector", "TE", te_v)
                if oh == 0 and g2 > 0:
                    W("vector", "VE", v_ms)
                v_vs[(g2, oh)] = O("vector", lambda e, sl=sl, oh=oh, pb=pb: e.tensor_tensor(
                    abf(A_VST + sl * 1040, 1040).rearrange(
                        "p (h c) -> p h c", c=65)[:, 8 * oh:8 * (oh + 1), 0:64],
                    pb.rearrange("p (h c) -> p h c", c=64),
                    BB[:, oh * 512:(oh + 1) * 512].rearrange("p (h c) -> p h c", c=64),
                    ALU.add), "VE")
            W("scalar", "VE", v_vs[(g2, 1)])
            for pr in range(8):
                st_v[sl] = dma("scalar",
                            vvin[:, pr * 2080 + t * 130:pr * 2080 + t * 130 + 130],
                            abf(A_VST + sl * 1040 + pr * 130, 130), f"SV{sl}")

    W("gpsimd", "S20", C["S20"])
    W("gpsimd", "S21", C["S21"])
    W("gpsimd", "SV0", C["SV0"])
    W("gpsimd", "SV1", C["SV1"])
    O("gpsimd", lambda e: e.collective_compute(
        "AllGather", ALU.bypass, ins=[kvin[:, :]], outs=[kvout[:, :]],
        replica_groups=groups), "CC")

    te_pr = {}
    st_h = [0, 0]
    te_nc_prev = None
    s_exp_prev = None
    vev = {}
    ld_kt = {}
    ld_v = {}
    ld_qt = {}
    for pr in range(8):
        sl = pr % 2
        if pr == 0:
            W("sync", "CC", 1)
            W("sync", "TE", C["TE"])
        W("sync", "TE", te_pr.get(pr - 2))
        for h in range(2):
            dma("sync", abf(A_KT + sl * 4096 + h * 2048, 2048),
                kkout[h * 128:(h + 1) * 128, pr * 2048:(pr + 1) * 2048], f"LK{sl}")
            dma("sync", abf(A_V + sl * 4160 + h * 2080, 2080),
                vvout[h * 128:(h + 1) * 128, pr * 2080:(pr + 1) * 2080], f"LK{sl}")
        ld_v[pr] = C[f"LK{sl}"]
        ld_qt[pr] = dma("sync", abf(A_QT + sl * 2048, 2048), qtd[:, pr * 2048:(pr + 1) * 2048],
                        f"LK{sl}")
        W("vector", f"LK{sl}", ld_qt[pr])
        if te_nc_prev:
            W("vector", "TE", te_nc_prev)
        for h2 in range(2):
            v_ksq = O("vector", lambda e, sl=sl, h2=h2: e.tensor_tensor(
                GU[:, h2 * 2048:(h2 + 1) * 2048], abf(A_KT + sl * 4096 + h2 * 2048, 2048),
                abf(A_KT + sl * 4096 + h2 * 2048, 2048), ALU.mult), "VE")
        W("tensor", "VE", v_ksq)
        v_kacc = 0
        for i8 in range(8):
            te_c2 = O("tensor", lambda e, i8=i8: e.matmul(
                P6[0:2, 0:512], ONES2[:, :], GU[:, i8 * 512:(i8 + 1) * 512],
                start=True, stop=True), "TE")
            W("vector", "TE", te_c2)
            O("vector", lambda e: e.reduce_max(
                STAT[0:2, SC_TMP:SC_TMP + 1], P6[0:2, 0:512], axis=AX.X))
            vdrain()
            if i8 == 0:
                O("vector", lambda e: e.tensor_copy(
                    STAT[0:2, SC_KMX:SC_KMX + 1], STAT[0:2, SC_TMP:SC_TMP + 1]))
            else:
                O("vector", lambda e: e.tensor_tensor(
                    STAT[0:2, SC_KMX:SC_KMX + 1], STAT[0:2, SC_KMX:SC_KMX + 1],
                    STAT[0:2, SC_TMP:SC_TMP + 1], ALU.max))
            v_kacc = O("vector", lambda e: e.drain(), "VE")
            if i8 < 7:
                W("tensor", "VE", v_kacc)
        if s_exp_prev:
            W("vector", "SC", s_exp_prev)
        O("vector", lambda e, pr=pr: e.tensor_tensor(
            STAT[0:2, SC_TMP:SC_TMP + 1], STAT[0:2, SC_KMX:SC_KMX + 1],
            STAT[0:2, SC_QMX + 2 * pr:SC_QMX + 2 * pr + 1], ALU.add))
        vdrain()
        v_nc = O("vector", lambda e: e.tensor_scalar(
            STAT[0:2, SC_TMP + 1:SC_TMP + 2], STAT[0:2, SC_TMP:SC_TMP + 1],
            -1.0 / 64.0, 40.0, ALU.mult, ALU.add), "VE")
        W("gpsimd", "VE", v_nc)
        g_nc = dma("gpsimd", TGR[0:1, 1664:1666], STAT[0:2, SC_TMP + 1:SC_TMP + 2], "GD")
        W("vector", "GD", g_nc)
        O("vector", lambda e: e.tensor_reduce(
            TGR[0:1, 1666:1667], TGR[0:1, 1664:1666], AX.X, ALU.min))
        v_nc2 = O("vector", lambda e: e.drain(), "VE")
        W("tensor", "VE", v_nc2)
        te_nc = O("tensor", lambda e: e.matmul(
            P6[0:128, 0:1], ONES[0:1, 0:128], TGR[0:1, 1666:1667],
            start=True, stop=True), "TE")
        te_nc_prev = te_nc
        W("vector", "TE", te_nc)
        v_negc = O("vector", lambda e: e.tensor_copy(
            STAT[:, SC_NEGC:SC_NEGC + 1], P6[0:128, 0:1]), "VE")

        for qb in range(4):
            s_exp = {}
            te_av = {}
            for kc in range(32):
                h, c16 = kc // 16, kc % 16
                sb = PA if kc % 2 == 0 else PB
                if kc == 0:
                    W("tensor", "VE", v_negc)
                    W("tensor", f"LK{sl}", ld_qt[pr])
                    prv = vev.get((pr, qb - 1)) if qb > 0 else vev.get((pr - 1, 3))
                    W("tensor", "VE", prv)
                W("tensor", "SC", s_exp.get(kc - 2))
                O("tensor", lambda e, sl=sl, h=h, c16=c16, sb=sb, qb=qb: e.matmul(
                    sb[:, 0:512], abf(A_KT + sl * 4096 + h * 2048 + c16 * 128, 128, 0, 64),
                    abf(A_QT + sl * 2048 + qb * 512, 512, 0, 64),
                    start=True, stop=True, tile_position=(0, 0)))
                te_s = O("tensor", lambda e, sl=sl, h=h, c16=c16, sb=sb, qb=qb: e.matmul(
                    sb[:, 512:1024], abf(A_KT + sl * 4096 + h * 2048 + c16 * 128, 128, 64, 128),
                    abf(A_QT + sl * 2048 + qb * 512, 512, 64, 128),
                    start=True, stop=True, tile_position=(64, 0)), "TE")
                W("scalar", "TE", te_s)
                if kc >= 2:
                    W("scalar", "TE", te_av[kc - 2])
                sp = (kc % 2) * 1024
                s_exp[kc] = O("scalar", lambda e, sb=sb, sp=sp: e.activation(
                    abf(A_PT + sp, 1024), sb[:, 0:1024], AF.Exp,
                    scale=ISCALE, bias=stc(SC_NEGC)[:, 0:1]), "SC")
                W("tensor", "SC", s_exp[kc])
                O("tensor", lambda e, sl=sl, h=h, c16=c16, sp=sp, kc=kc: e.matmul(
                    P[0][0:65, :], abf(A_V + sl * 4160 + h * 2080 + c16 * 130, 65),
                    abf(A_PT + sp, 512), start=(kc == 0), stop=(kc == 31)))
                te_av[kc] = O("tensor", lambda e, sl=sl, h=h, c16=c16, sp=sp, kc=kc: e.matmul(
                    P[1][0:65, :], abf(A_V + sl * 4160 + h * 2080 + c16 * 130 + 65, 65),
                    abf(A_PT + sp + 512, 512), start=(kc == 0), stop=(kc == 31)), "TE")
            s_exp_prev = s_exp[31]
            v_last2 = 0
            for hh in range(2):
                if hh == 0:
                    W("vector", "TE", te_av[31])
                v_cx = O("vector", lambda e, hh=hh: e.tensor_copy(
                    CTX[0:65, :], P[hh][0:65, :]), "VE")
                W("gpsimd", "VE", v_cx)
                g_rs = dma("gpsimd", TGR[0:1, 1536:2048], CTX[64:65, 0:512], "GD")
                W("vector", "GD", g_rs)
                O("vector", lambda e: e.reciprocal(
                    TGR[0:1, 1536:2048], TGR[0:1, 1536:2048]))
                v_rc = O("vector", lambda e: e.drain(), "VE")
                W("tensor", "VE", v_rc)
                te_rp = O("tensor", lambda e: e.matmul(
                    P6[0:64, 0:512], ONES[0:1, 0:64], TGR[0:1, 1536:2048],
                    start=True, stop=True), "TE")
                W("vector", "TE", te_rp)
                W("vector", f"SH{hh}", st_h[hh])
                v_h = O("vector", lambda e, hh=hh: e.tensor_tensor(
                    abf(A_HST + hh * 512, 512, 0, 64), CTX[0:64, :], P6[0:64, 0:512],
                    ALU.mult), "VE")
                W("scalar", "VE", v_h)
                st_h[hh] = dma("scalar",
                    htd[64 * hh:64 * (hh + 1),
                        pr * 2048 + qb * 512:pr * 2048 + qb * 512 + 512],
                    abf(A_HST + hh * 512, 512, 0, 64), f"SH{hh}")
                vdrain()
                v_last2 = v_h
            vev[(pr, qb)] = v_last2
        te_pr[pr] = C["TE"]

    W("sync", "TE", C["TE"])
    ld_wo = 0
    for ic in range(8):
        ld_wo = dma("sync", abf(A_GUQ + ic * 1024, 1024), wobd[:, ic * 1024:(ic + 1) * 1024])
    te_op = {}
    s_op = {}
    v_xo = {}
    for t in range(NT):
        p = t % 2
        W("sync", "TE", te_op.get(t - 2))
        if t == 0:
            W("sync", "SH0", C["SH0"])
            W("sync", "SH1", C["SH1"])
        ld_h = 0
        for fc in range(8):
            ld_h = dma("sync", abf(A_QT + p * 1024 + fc * 128, 128),
                       htd[:, fc * 2048 + t * 128:fc * 2048 + t * 128 + 128], f"LK{p}")
        W("tensor", f"LK{p}", ld_h)
        if t == 0:
            W("tensor", "LD", ld_wo)
        if t == 0:
            W("tensor", "SC", C["SC"])
            W("tensor", "VE", C["VE"])
        W("tensor", "VE", v_xo.get(t - 2))
        pbank = PA if p == 0 else PB
        ob0 = pbank[:, 0:512]
        ob1 = pbank[:, 512:1024]
        for fc in range(8):
            O("tensor", lambda e, fc=fc, p=p, ob0=ob0: e.matmul(
                ob0, abf(A_QT + p * 1024 + fc * 128, 128),
                abf(A_GUQ + fc * 1024, 512), start=(fc == 0), stop=(fc == 7)))
            te_ot = O("tensor", lambda e, fc=fc, p=p, ob1=ob1: e.matmul(
                ob1, abf(A_QT + p * 1024 + fc * 128, 128),
                abf(A_GUQ + fc * 1024 + 512, 512), start=(fc == 0), stop=(fc == 7)),
                "TE" if fc == 7 else None)
        te_op[t] = te_ot
        W("scalar", "TE", te_op[t])
        W("scalar", "SC", s_op.get(t - 1, 0))
        W("scalar", "VE", v_xo.get(t - 1))
        s_a = O("scalar", lambda e, t=t: e.activation(
            XLN[:, 0:1024], X[:, t * 1024:(t + 1) * 1024], AF.Identity,
            bias=stc(SC_NMR2 + t)[:, 0:1], scale=stc(SC_RSTD2 + t)[:, 0:1]), "SC")
        s_op[t] = s_a
        W("vector", "SC", s_a)
        for oh in range(2):
            ob = (PA if p == 0 else PB)[:, oh * 512:(oh + 1) * 512]
            O("vector", lambda e, oh=oh, ob=ob: e.tensor_tensor(
                TGR[:, oh * 512:(oh + 1) * 512], ob,
                BB[:, 1024 + oh * 512:1536 + oh * 512], ALU.add))
        vdrain()
        O("vector", lambda e: e.tensor_tensor(
            TGR[:, 1024:2048], TGR[:, 0:1024], XLN[:, 0:1024], ALU.add))
        vdrain()
        v_xo[t] = O("vector", lambda e, t=t: e.tensor_tensor(
            X[:, t * 1024:(t + 1) * 1024], X[:, t * 1024:(t + 1) * 1024],
            TGR[:, 1024:2048], ALU.add), "VE")
        vdrain()
    fst["te_blk"] = te_op[NT - 1]
    fst["s_down"] = s_op[NT - 1]

    if stop_after != "attn":
        ln_prepass(SC_RSTD1, SC_NMR1)
        ffn(SC_RSTD1, SC_NMR1)

    W("scalar", "VE", C["VE"])
    for t in range(NT):
        dma("scalar", outd[128 * t:128 * t + 128, :], X[:, t * 1024:(t + 1) * 1024], "S20")
    for sn in ("S20", "S21", "SV0", "SV1", "SH0", "SH1"):
        W("sync", sn, C[sn])
        W("gpsimd", sn, C[sn])
    prog.emit_block()
    stack.close()
    return nc


def _pm(w, nj, nic):
    return np.ascontiguousarray(
        w.reshape(nj, 128, nic, 128).transpose(3, 0, 2, 1).reshape(128, nj * nic * 128))


def make_inputs(L0_attn_x, L0_attn_w_qkv, L0_attn_b_qkv, L0_attn_w_o, L0_attn_b_o,
                L0_ffn_w_gate, L0_ffn_w_up, L0_ffn_w_down, n_cores=8):
    x = np.ascontiguousarray(np.asarray(L0_attn_x, dtype=np.float32))
    w_qkv = np.asarray(L0_attn_w_qkv, dtype=np.float32)
    b_qkv = np.asarray(L0_attn_b_qkv, dtype=np.float32)
    w_o = np.asarray(L0_attn_w_o, dtype=np.float32)
    b_o = np.asarray(L0_attn_b_o, dtype=np.float32)
    w_gate = np.asarray(L0_ffn_w_gate, dtype=np.float32)
    w_up = np.asarray(L0_ffn_w_up, dtype=np.float32)
    w_down = np.asarray(L0_ffn_w_down, dtype=np.float32)

    shared = {
        "wgpm": _pm(w_gate, 32, 8),
        "wupm": _pm(w_up, 32, 8),
        "wdpm": np.ascontiguousarray(
            w_down.T.reshape(32, 128, 1024).transpose(1, 0, 2).reshape(128, 32 * 1024)),
        "wqkpm": _pm(w_qkv[0:2048], 16, 8),
        "wvpm": np.ascontiguousarray(
            w_qkv[2048:3072].T.reshape(8, 128, 1024).transpose(1, 0, 2).reshape(128, 8 * 1024)),
        "wopm": np.ascontiguousarray(
            w_o.T.reshape(8, 128, 1024).transpose(1, 0, 2).reshape(128, 8 * 1024)),
        "bqkc": np.ascontiguousarray(b_qkv[0:2048].reshape(16, 128).T),
        "bvr": np.ascontiguousarray(b_qkv[2048:3072].reshape(1, 1024)),
        "bor": np.ascontiguousarray(b_o.reshape(1, 1024)),
        "ident": np.eye(128, dtype=ml_dtypes.bfloat16),
        "identf": np.eye(128, dtype=np.float32),
    }
    in_maps = []
    for c in range(n_cores):
        b, h = c // 2, c % 2
        m = dict(shared)
        m["x"] = np.ascontiguousarray(x[b, 2048 * h:2048 * h + 2048])
        in_maps.append(m)
    return in_maps


def kernel(L0_attn_x, L0_attn_w_qkv, L0_attn_b_qkv, L0_attn_w_o, L0_attn_b_o,
           L0_ffn_w_gate, L0_ffn_w_up, L0_ffn_w_down):
    in_maps = make_inputs(L0_attn_x, L0_attn_w_qkv, L0_attn_b_qkv, L0_attn_w_o,
                          L0_attn_b_o, L0_ffn_w_gate, L0_ffn_w_up, L0_ffn_w_down)
    nc = bass.Bass("TRN2", target_bir_lowering=False)
    build(nc)
    res = run_bass_kernel_spmd(nc, in_maps, core_ids=list(range(8)))
    out = np.empty((4, 4096, 1024), dtype=np.float32)
    for c in range(8):
        b, h = c // 2, c % 2
        out[b, 2048 * h:2048 * h + 2048] = res.results[c]["out"]
    return out


# version 06
# speedup vs baseline: 1.3226x; 1.2584x over previous
import os
import contextlib

import numpy as np
import ml_dtypes
import concourse.bass as bass
import concourse.mybir as mybir
from concourse.bass_utils import run_bass_kernel_spmd

F32 = mybir.dt.float32
BF16 = mybir.dt.bfloat16
AF = mybir.ActivationFunctionType
ALU = mybir.AluOpType
AX = mybir.AxisListType

MAGIC = float(np.float32(1.5 * 2.0 ** 52)) if os.environ.get("BASS_SIM_FP64") == "1" \
    else float(np.float32(1.5 * 2.0 ** 23))
D = 1024
T = 2048
NT = 16
NB = 4
EPS = 1e-5
ISCALE = 1.0 / 32.0

A_XQT = 0
A_WG = 4096
A_WU = 6144
A_GUQ = 8192
A_VST = 16384
A_STG = 22528
A_KT = 0
A_V = 16384
A_V = 8192
A_QT = 16640
A_PT = 20736
A_HST = 22784

SC_RSTD1, SC_NMR1 = 0, 16
SC_RSTD2, SC_NMR2 = 32, 48
SC_SX, SC_INVSX = 64, 80
SC_S2 = 96
SC_INVS2 = 100
SC_WSC = 104
SC_TMP = 112
SC_QMX = 128
SC_KMX = 144
SC_NEGC = 148


class Prog:
    def __init__(self, nc, stack):
        self.nc = nc
        self.streams = {k: [] for k in ("sync", "tensor", "scalar", "vector", "gpsimd")}
        self.sems = {}
        self.cnt = {}
        self._stack = stack

    def sem(self, name):
        self.sems[name] = self._stack.enter_context(self.nc.semaphore(name=name))
        self.cnt[name] = 0

    def wait(self, eng, sname, val):
        if val is None or val <= 0:
            return
        h = self.sems[sname]
        self.streams[eng].append(lambda e, h=h, v=val: e.wait_ge(h, v))

    def op(self, eng, build, sname=None, n=1):
        if sname is None:
            self.streams[eng].append(build)
            return None
        h = self.sems[sname]
        self.cnt[sname] += n
        v = self.cnt[sname]

        def f(e, build=build, h=h, n=n):
            build(e).then_inc(h, n)

        self.streams[eng].append(f)
        return v

    def emit_block(self):
        with self.nc.Block() as block:
            @block.sync
            def _(e):
                for f in self.streams["sync"]:
                    f(e)

            @block.tensor
            def _(e):
                for f in self.streams["tensor"]:
                    f(e)

            @block.scalar
            def _(e):
                for f in self.streams["scalar"]:
                    f(e)

            @block.vector
            def _(e):
                for f in self.streams["vector"]:
                    f(e)

            @block.gpsimd
            def _(e):
                for f in self.streams["gpsimd"]:
                    f(e)


def build(nc, groups=None, stop_after=None):
    if groups is None:
        groups = [[0, 1], [2, 3], [4, 5], [6, 7]]
    dt = nc.dram_tensor
    xd = dt("x", [T, D], F32, kind="ExternalInput").ap()
    wgpm = dt("wgpm", [128, 32 * 1024], F32, kind="ExternalInput").ap()
    wupm = dt("wupm", [128, 32 * 1024], F32, kind="ExternalInput").ap()
    wdpm = dt("wdpm", [128, 32 * 1024], F32, kind="ExternalInput").ap()
    wqkpm = dt("wqkpm", [128, 16 * 1024], F32, kind="ExternalInput").ap()
    wvpm = dt("wvpm", [128, 8 * 1024], F32, kind="ExternalInput").ap()
    wopm = dt("wopm", [128, 8 * 1024], F32, kind="ExternalInput").ap()
    bqkc = dt("bqkc", [128, 16], F32, kind="ExternalInput").ap()
    bvr = dt("bvr", [1, D], F32, kind="ExternalInput").ap()
    bord = dt("bor", [1, D], F32, kind="ExternalInput").ap()
    identd = dt("ident", [128, 128], BF16, kind="ExternalInput").ap()
    identf = dt("identf", [128, 128], F32, kind="ExternalInput").ap()
    outd = dt("out", [T, D], F32, kind="ExternalOutput").ap()

    tgd = dt("tgs", [128, 32 * 1024], BF16, kind="Internal").ap()
    tud = dt("tus", [128, 32 * 1024], BF16, kind="Internal").ap()
    tdd = dt("tds", [128, 32 * 1024], BF16, kind="Internal").ap()
    wqkb = dt("wqkbs", [128, 16 * 1024], BF16, kind="Internal").ap()
    wvbd = dt("wvbs", [128, 8 * 1024], BF16, kind="Internal").ap()
    wobd = dt("wobs", [128, 8 * 1024], BF16, kind="Internal").ap()
    qtd = dt("qts", [128, 8 * 2048], BF16, kind="Internal").ap()
    htd = dt("hts", [128, 8 * 2048], BF16, kind="Internal").ap()
    kvin = dt("kvin", [128, 8 * 2048 + 8 * 2080], BF16, kind="Internal").ap()
    kvout = dt("kvout", [256, 8 * 2048 + 8 * 2080], BF16, kind="Internal").ap()
    kkin = kvin[:, 0:8 * 2048]
    vvin = kvin[:, 8 * 2048:]
    kkout = kvout[:, 0:8 * 2048]
    vvout = kvout[:, 8 * 2048:]

    stack = contextlib.ExitStack()
    sbuf = lambda name, shape, dty: stack.enter_context(nc.sbuf_tensor(name, shape, dty))
    pst = lambda name, shape, dty=F32: stack.enter_context(nc.psum_tensor(name, shape, dty))

    X = sbuf("X", [128, NT * 1024], F32)
    GU = sbuf("GU", [128, 32 * 512], F32)
    XLN = sbuf("XLN", [128, 2 * 1024], F32)
    TGR = sbuf("TGR", [128, 2048], F32)
    CTX = sbuf("CTX", [128, 512], F32)
    MNA = sbuf("MNA", [128, 512], F32)
    STAT = sbuf("STAT", [128, 160], F32)
    IBC = sbuf("IBC", [128, 1024], F32)
    ABF = sbuf("ABF", [128, 24576], BF16)
    BB = sbuf("BB", [128, 2048], BF16)
    IDT = sbuf("IDT", [128, 128], BF16)
    IDF = sbuf("IDF", [128, 128], F32)
    BQK = sbuf("BQK", [128, 16], F32)
    ONES = sbuf("ONES", [128, 128], F32)
    ONES2 = sbuf("ONES2", [128, 2], F32)

    P = [pst(f"P{i}", [128, 512]) for i in range(2)]
    PA = pst("PA", [128, 1024])
    PB = pst("PB", [128, 1024])
    P6 = pst("P6", [128, 512])
    PT8 = pst("PT8", [128, 1024], BF16)

    prog = Prog(nc, stack)
    for s in ("LD", "LW0", "LW1", "LK0", "LK1", "GD", "S20", "S21",
              "SV0", "SV1", "SH0", "SH1", "TE", "SC", "VE", "CC"):
        prog.sem(s)
    W, O, C = prog.wait, prog.op, prog.cnt

    def dma(eng, out, in_, sname="LD", **kw):
        return O(eng, lambda e, o=out, i=in_, kw=kw: e.dma_start(out=o, in_=i, **kw),
                 sname, 16)

    def vdrain():
        O("vector", lambda e: e.drain())

    def stc(c):
        return STAT[:, c:c + 1]

    def abf(lo, n, p0=0, p1=128):
        return ABF[p0:p1, lo:lo + n]

    dma("sync", IDT[:], identd[:])
    dma("sync", IDF[:], identf[:])
    for t in range(NT):
        dma("sync", X[:, t * 1024:(t + 1) * 1024], xd[128 * t:128 * t + 128, :])
    dma("sync", BQK[:], bqkc[:])
    dma("sync", TGR[0:1, 0:1024], bvr[:])
    ld_b = dma("sync", TGR[0:1, 1024:2048], bord[:])
    O("vector", lambda e: e.memset(ONES2[:, :], 0.0))
    vdrain()
    O("vector", lambda e: e.memset(ONES2[0:64, 0:1], 1.0))
    vdrain()
    O("vector", lambda e: e.memset(ONES2[64:128, 1:2], 1.0))
    vdrain()
    v_ones = O("vector", lambda e: e.memset(ONES[:], 1.0), "VE")
    W("tensor", "VE", v_ones)
    W("tensor", "LD", ld_b)
    v_bb = 0
    for h in range(2):
        W("tensor", "VE", v_bb)
        O("tensor", lambda e, h=h: e.matmul(
            P[0][:, :], ONES[0:1, 0:128], TGR[0:1, 512 * h:512 * (h + 1)],
            start=True, stop=True))
        te_b = O("tensor", lambda e, h=h: e.matmul(
            P[1][:, :], ONES[0:1, 0:128], TGR[0:1, 1024 + 512 * h:1536 + 512 * h],
            start=True, stop=True), "TE")
        W("vector", "TE", te_b)
        O("vector", lambda e, h=h: e.tensor_copy(BB[:, 512 * h:512 * (h + 1)], P[0][:, :]))
        v_bb = O("vector", lambda e, h=h: e.tensor_copy(
            BB[:, 1024 + 512 * h:1536 + 512 * h], P[1][:, :]), "VE")
    ve_bias = C["VE"]

    NCOL = 128 * 32 * 1024

    def wprep(src, dst, wsc_col, first_wait):
        v_red = [0] * 8
        for i in range(8):
            p = i % 2
            W("sync", "VE", v_red[i - 2] if i >= 2 else first_wait)
            lv = dma("sync", GU[:, p * 4096:(p + 1) * 4096], src[:, i * 4096:(i + 1) * 4096],
                     f"LW{p}")
            W("vector", f"LW{p}", lv)
            v_red[i] = O("vector", lambda e, i=i, p=p: e.reduce_sum(
                stc(SC_TMP + i), GU[:, p * 4096:(p + 1) * 4096], axis=AX.X,
                apply_absolute_value=True), "VE")
        vdrain()
        O("vector", lambda e: e.reduce_sum(
            stc(SC_TMP + 8), STAT[:, SC_TMP:SC_TMP + 8], axis=AX.X))
        v_s = O("vector", lambda e: e.drain(), "VE")
        W("tensor", "VE", v_s)
        te_s = O("tensor", lambda e: e.matmul(
            P6[0:1, 0:1], stc(SC_TMP + 8), ONES[:, 0:1], start=True, stop=True), "TE")
        W("vector", "TE", te_s)
        O("vector", lambda e: e.tensor_scalar(
            stc(SC_TMP + 9)[0:1, :], P6[0:1, 0:1], 1.0 / NCOL, 1e-5, ALU.mult, ALU.max))
        vdrain()
        v_r = O("vector", lambda e: e.reciprocal(
            stc(SC_TMP + 10)[0:1, :], stc(SC_TMP + 9)[0:1, :]), "VE")
        W("tensor", "VE", v_r)
        O("tensor", lambda e: e.matmul(
            P6[0:128, 0:1], ONES[0:1, 0:128], stc(SC_TMP + 10)[0:1, :],
            start=True, stop=True))
        te_b2 = O("tensor", lambda e: e.matmul(
            P6[0:128, 1:2], ONES[0:1, 0:128], stc(SC_TMP + 9)[0:1, :],
            start=True, stop=True), "TE")
        W("vector", "TE", te_b2)
        O("vector", lambda e: e.tensor_copy(stc(SC_TMP + 11), P6[0:128, 0:1]))
        v_bc = O("vector", lambda e, c=wsc_col: e.tensor_copy(stc(c), P6[0:128, 1:2]), "VE")
        st_prev = [C["S20"], C["S21"]]
        s_done = [0] * 8
        v_done = [0] * 8
        for i in range(8):
            p = i % 2
            W("sync", "VE", v_done[i - 2] if i >= 2 else v_bc)
            lv = dma("sync", GU[:, p * 4096:(p + 1) * 4096], src[:, i * 4096:(i + 1) * 4096],
                     f"LW{p}")
            W("scalar", f"LW{p}", lv)
            W("scalar", "VE", v_done[i - 1] if i >= 1 else v_bc)
            s_done[i] = O("scalar", lambda e, p=p: e.activation(
                GU[:, 8192 + p * 4096:8192 + (p + 1) * 4096],
                GU[:, p * 4096:(p + 1) * 4096],
                AF.Copy, bias=MAGIC, scale=stc(SC_TMP + 11)[:, 0:1]), "SC")
            W("vector", "SC", s_done[i])
            W("vector", f"S2{p}", st_prev[p])
            O("vector", lambda e, p=p: e.tensor_scalar(
                abf(p * 4096, 4096), GU[:, 8192 + p * 4096:8192 + (p + 1) * 4096],
                -MAGIC, 1.0, ALU.add, ALU.min))
            vdrain()
            v_done[i] = O("vector", lambda e, p=p: e.tensor_scalar(
                abf(p * 4096, 4096), abf(p * 4096, 4096), -1.0, None, ALU.max), "VE")
            W("scalar", "VE", v_done[i])
            st_prev[p] = dma("scalar", dst[:, i * 4096:(i + 1) * 4096],
                             abf(p * 4096, 4096), f"S2{p}")
        return v_done[7]

    v_p = wprep(wgpm, tgd, SC_WSC + 0, ve_bias)
    v_p = wprep(wupm, tud, SC_WSC + 1, v_p)
    v_p = wprep(wdpm, tdd, SC_WSC + 2, v_p)
    vdrain()
    O("vector", lambda e: e.tensor_scalar(
        stc(SC_WSC + 2), stc(SC_WSC + 2), 0.5, None, ALU.mult), "VE")

    def wcast(src, dst, ncols, v_first):
        v_prev = [0, 0]
        st_prev = [C["S20"], C["S21"]]
        for i in range(ncols // 4096):
            p = i % 2
            W("sync", "VE", v_prev[p] if v_prev[p] else v_first)
            lv = dma("sync", GU[:, p * 4096:(p + 1) * 4096], src[:, i * 4096:(i + 1) * 4096],
                     f"LW{p}")
            W("vector", f"LW{p}", lv)
            W("vector", f"S2{p}", st_prev[p])
            v_prev[p] = O("vector", lambda e, p=p: e.tensor_copy(
                abf(p * 4096, 4096), GU[:, p * 4096:(p + 1) * 4096]), "VE")
            W("scalar", "VE", v_prev[p])
            st_prev[p] = dma("scalar", dst[:, i * 4096:(i + 1) * 4096],
                             abf(p * 4096, 4096), f"S2{p}")
        return v_prev[(ncols // 4096 - 1) % 2]

    v_c = wcast(wqkpm, wqkb, 16 * 1024, C["VE"])
    v_c = wcast(wvpm, wvbd, 8 * 1024, v_c)
    v_c = wcast(wopm, wobd, 8 * 1024, v_c)

    def ln_prepass(rstd_base, nmr_base):
        W("vector", "LD", C["LD"])
        W("scalar", "VE", C["VE"])
        W("scalar", "LW0", C["LW0"])
        W("scalar", "LW1", C["LW1"])
        for t in range(NT):
            O("vector", lambda e, t=t: e.reduce_sum(
                stc(SC_TMP + t), X[:, t * 1024:(t + 1) * 1024], axis=AX.X))
        v0 = O("vector", lambda e: e.drain(), "VE")
        s0 = 0
        s_hist = {}
        for t in range(NT):
            W("scalar", "SC", s_hist.get(t - 4, 0))
            s_hist[t] = O("scalar", lambda e, t=t: e.activation(
                GU[:, (t % 4) * 1024:(t % 4) * 1024 + 1024],
                X[:, t * 1024:(t + 1) * 1024], AF.Square,
                accum_out=stc(nmr_base + t)), "SC")
        s0 = s_hist[NT - 1]
        W("vector", "SC", s0)
        TM = STAT[:, SC_TMP:SC_TMP + 16]
        SQ = STAT[:, nmr_base:nmr_base + 16]
        MM = STAT[:, SC_SX:SC_SX + 16]
        O("vector", lambda e, TM=TM: e.tensor_scalar(TM, TM, 1.0 / D, None, ALU.mult))
        O("vector", lambda e, SQ=SQ: e.tensor_scalar(SQ, SQ, 1.0 / D, None, ALU.mult))
        vdrain()
        O("vector", lambda e, TM=TM, MM=MM: e.tensor_tensor(MM, TM, TM, ALU.mult))
        vdrain()
        O("vector", lambda e, SQ=SQ, MM=MM: e.tensor_tensor(SQ, SQ, MM, ALU.subtract))
        vdrain()
        O("vector", lambda e, SQ=SQ: e.tensor_scalar(SQ, SQ, EPS, None, ALU.add))
        vdrain()
        v1 = O("vector", lambda e, SQ=SQ, MM=MM: e.reciprocal(MM, SQ), "VE")
        W("scalar", "VE", v1)
        s1 = O("scalar", lambda e, MM=MM, rb=rstd_base: e.activation(
            STAT[:, rb:rb + 16], MM, AF.Sqrt), "SC")
        W("vector", "SC", s1)
        O("vector", lambda e, TM=TM, rb=rstd_base, nb=nmr_base: e.scalar_tensor_tensor(
            STAT[:, nb:nb + 16], TM, -1.0, STAT[:, rb:rb + 16], ALU.mult, ALU.mult), "VE")
        vdrain()

    st8 = {"seq": 0, "v_xl": {}, "te_tp": {}, "v_cp": {}, "v_tg": {}}

    def xln_tile(t, ti, rstd_base, nmr_base, quant, entry_te=None):
        k = st8["seq"]
        st8["seq"] += 1
        sl = k % 2
        xt = X[:, t * 1024:(t + 1) * 1024]
        xl = XLN[:, sl * 1024:(sl + 1) * 1024]
        W("scalar", "VE", st8["v_xl"].get(k - 2))
        s0 = O("scalar", lambda e, xt=xt, xl=xl, t=t: e.activation(
            xl, xt, AF.Identity, bias=stc(nmr_base + t)[:, 0:1],
            scale=stc(rstd_base + t)[:, 0:1]), "SC")
        W("vector", "SC", s0)
        if entry_te:
            W("vector", "TE", entry_te)
        if quant:
            O("vector", lambda e, xl=xl: e.reduce_max(
                stc(SC_TMP + 0), xl, axis=AX.X, apply_absolute_value=True))
            vdrain()
            O("vector", lambda e: e.tensor_scalar(
                stc(SC_TMP + 1), stc(SC_TMP + 0), 1e-5, None, ALU.max))
            vdrain()
            O("vector", lambda e, t=t: e.tensor_scalar(
                stc(SC_INVSX + t % 16), stc(SC_TMP + 1), 1.0 / 127.0, None, ALU.mult))
            O("vector", lambda e: e.reciprocal(stc(SC_TMP + 2), stc(SC_TMP + 1)))
            vdrain()
            v0 = O("vector", lambda e, t=t: e.tensor_scalar(
                stc(SC_SX + t % 16), stc(SC_TMP + 2), 127.0, None, ALU.mult), "VE")
            st8["v_xl"][k] = v0
            W("scalar", "VE", v0)
            W("scalar", "VE", st8["v_tg"].get(k - 1))
            s1 = O("scalar", lambda e, xl=xl, t=t: e.activation(
                TGR[:, 0:1024], xl, AF.Copy, bias=MAGIC,
                scale=stc(SC_SX + t % 16)[:, 0:1]), "SC")
            W("vector", "SC", s1)
            W("vector", "TE", st8["te_tp"].get(k - 2))
            v1 = O("vector", lambda e, sl=sl: e.tensor_scalar(
                abf(A_STG + sl * 1024, 1024), TGR[:, 0:1024], -MAGIC, None, ALU.add), "VE")
            st8["v_tg"][k] = v1
        else:
            W("vector", "TE", st8["te_tp"].get(k - 2))
            v1 = O("vector", lambda e, xl=xl, sl=sl: e.tensor_copy(
                abf(A_STG + sl * 1024, 1024), xl), "VE")
            st8["v_xl"][k] = v1
        W("tensor", "VE", v1)
        W("tensor", "VE", st8["v_cp"].get(k - 1))
        for fc in range(8):
            te_x = O("tensor", lambda e, fc=fc, sl=sl: e.transpose(
                PT8[:, fc * 128:(fc + 1) * 128],
                abf(A_STG + sl * 1024 + fc * 128, 128), IDT[:]),
                "TE" if fc == 7 else None)
        st8["te_tp"][k] = te_x
        W("vector", "TE", st8["te_tp"][k])
        v_c = 0
        for fc in range(8):
            v_c = O("vector", lambda e, fc=fc, ti=ti: e.tensor_copy(
                abf(A_XQT + fc * 512 + ti * 128, 128), PT8[:, fc * 128:(fc + 1) * 128]),
                "VE" if fc == 7 else None)
        st8["v_cp"][k] = v_c
        return v_c

    def bcast4(col0, ibc_lo, psum_guard=0):
        vdrain()
        v_a = O("vector", lambda e, c=col0: e.tensor_copy(
            TGR[:, 1024:1028], STAT[:, c:c + 4]), "VE")
        W("tensor", "VE", v_a)
        W("tensor", "SC", psum_guard)
        for q in range(4):
            te_a = O("tensor", lambda e, q=q: e.transpose(
                P6[:, q * 128:(q + 1) * 128], TGR[:, 1024 + q:1024 + q + 128], IDF[:]),
                "TE" if q == 3 else None)
        W("vector", "TE", te_a)
        for q in range(4):
            O("vector", lambda e, q=q: e.tensor_copy(
                TGR[0:1, 1152 + q * 128:1152 + (q + 1) * 128],
                P6[0:1, q * 128:(q + 1) * 128]))
        v_b = O("vector", lambda e: e.drain(), "VE")
        W("tensor", "VE", v_b)
        te_c = O("tensor", lambda e: e.matmul(
            P6[:, 0:512], ONES[0:1, 0:128], TGR[0:1, 1152:1664],
            start=True, stop=True), "TE")
        W("vector", "TE", te_c)
        return O("vector", lambda e, lo=ibc_lo: e.tensor_copy(
            IBC[:, lo:lo + 512], P6[:, 0:512]), "VE")

    fst = {"call": 0, "te_gu": None, "s_down": 0, "te_dlast": None, "te_blk": None}

    def ffn(rstd_base, nmr_base):
        W("scalar", "VE", C["VE"])
        W("sync", "TE", C["TE"])
        for sn in ("S20", "S21", "SV0", "SV1", "SH0", "SH1"):
            W("sync", sn, C[sn])
            W("vector", sn, C[sn])
        for blk in range(NB):
            v_cp_last = 0
            for ti in range(4):
                t = blk * 4 + ti
                v_cp_last = xln_tile(t, ti, rstd_base, nmr_base, True,
                                     entry_te=(fst["te_blk"] if ti == 0 else None))
            v_ibc = bcast4(SC_INVSX + (blk * 4) % 16, 0, psum_guard=fst["s_down"])
            te_j = {}
            v_pu = {}
            s_g = {}
            v_mm = {}
            for j in range(32):
                p = j % 2
                W("sync", "TE", te_j.get(j - 2, fst["te_dlast"] if j < 2 else None))
                dma("sync", abf(A_WG + p * 1024, 1024), tgd[:, j * 1024:(j + 1) * 1024],
                    f"LW{p}")
                ld_w = dma("sync", abf(A_WU + p * 1024, 1024), tud[:, j * 1024:(j + 1) * 1024],
                           f"LW{p}")
                gb = PA[:, 0:512] if p == 0 else PB[:, 0:512]
                ub = PA[:, 512:1024] if p == 0 else PB[:, 512:1024]
                W("tensor", f"LW{p}", ld_w)
                if j == 0:
                    W("tensor", "VE", v_ibc)
                    W("tensor", "SC", fst["s_down"])
                W("tensor", "VE", v_pu.get(j - 2))
                for ic in range(8):
                    O("tensor", lambda e, ic=ic, p=p, gb=gb: e.matmul(
                        gb, abf(A_WG + p * 1024 + ic * 128, 128),
                        abf(A_XQT + ic * 512, 512), start=(ic == 0), stop=(ic == 7)))
                for ic in range(8):
                    te_jj = O("tensor", lambda e, ic=ic, p=p, ub=ub: e.matmul(
                        ub, abf(A_WU + p * 1024 + ic * 128, 128),
                        abf(A_XQT + ic * 512, 512), start=(ic == 0), stop=(ic == 7)),
                        "TE" if ic == 7 else None)
                te_j[j] = te_jj
                W("vector", "TE", te_j[j])
                W("vector", "SC", s_g.get(j - 1))
                O("vector", lambda e, gb=gb: e.tensor_tensor(
                    TGR[:, 1024:1536], gb, IBC[:, 0:512], ALU.mult))
                O("vector", lambda e, ub=ub: e.tensor_tensor(
                    TGR[:, 0:512], ub, IBC[:, 0:512], ALU.mult))
                v_pu[j] = O("vector", lambda e: e.drain(), "VE")
                W("scalar", "VE", v_pu[j])
                s_g[j] = O("scalar", lambda e: e.activation(
                    TGR[:, 1536:2048], TGR[:, 1024:1536], AF.Silu,
                    scale=stc(SC_WSC + 0)[:, 0:1]), "SC")
                W("vector", "SC", s_g[j])
                O("vector", lambda e, j=j: e.scalar_tensor_tensor(
                    GU[:, j * 512:(j + 1) * 512], TGR[:, 0:512],
                    stc(SC_WSC + 1)[:, 0:1], TGR[:, 1536:2048], ALU.mult, ALU.mult))
                vdrain()
                if j == 0:
                    O("vector", lambda e: e.tensor_copy(CTX[:, :], GU[:, 0:512]))
                    v_mm[j] = O("vector", lambda e: e.tensor_copy(MNA[:, :], GU[:, 0:512]), "VE")
                else:
                    O("vector", lambda e, j=j: e.tensor_tensor(
                        CTX[:, :], CTX[:, :], GU[:, j * 512:(j + 1) * 512], ALU.max))
                    v_mm[j] = O("vector", lambda e, j=j: e.tensor_tensor(
                        MNA[:, :], MNA[:, :], GU[:, j * 512:(j + 1) * 512], ALU.min), "VE")
                vdrain()
            fst["te_gu"] = te_j[31]

            O("vector", lambda e: e.tensor_scalar(MNA[:, :], MNA[:, :], -1.0, None, ALU.mult))
            vdrain()
            O("vector", lambda e: e.tensor_tensor(CTX[:, :], CTX[:, :], MNA[:, :], ALU.max))
            vdrain()
            v_mx = O("vector", lambda e: e.drain(), "VE")
            W("tensor", "VE", v_mx)
            W("tensor", "SC", fst["s_down"])
            for q in range(4):
                te_mx = O("tensor", lambda e, q=q: e.transpose(
                    P6[:, q * 128:(q + 1) * 128], CTX[:, q * 128:(q + 1) * 128], IDF[:]),
                    "TE" if q == 3 else None)
            W("vector", "TE", te_mx)
            for q in range(4):
                O("vector", lambda e, q=q: e.reduce_max(
                    stc(SC_TMP + q), P6[:, q * 128:(q + 1) * 128], axis=AX.X))
            vdrain()
            for q in range(4):
                O("vector", lambda e, q=q: e.tensor_scalar(
                    stc(SC_TMP + 4 + q), stc(SC_TMP + q), 1e-5, None, ALU.max))
            vdrain()
            for q in range(4):
                O("vector", lambda e, q=q: e.tensor_scalar(
                    stc(SC_INVS2 + q), stc(SC_TMP + 4 + q), 1.0 / 127.0, None, ALU.mult))
                O("vector", lambda e, q=q: e.reciprocal(
                    stc(SC_TMP + 8 + q), stc(SC_TMP + 4 + q)))
            vdrain()
            for q in range(4):
                O("vector", lambda e, q=q: e.tensor_scalar(
                    stc(SC_TMP + q), stc(SC_TMP + 8 + q), 127.0, None, ALU.mult))
                O("vector", lambda e, q=q: e.tensor_tensor(
                    stc(SC_S2 + q), stc(SC_INVS2 + q), stc(SC_WSC + 2), ALU.mult))
            v_s2bc = bcast4(SC_TMP, 512, psum_guard=fst["s_down"])

            for hf in range(2):
                v_rnd = {}
                s_t2 = {}
                for j in range(32):
                    jp = j % 2
                    if j == 0:
                        W("vector", "VE", v_s2bc)
                        if fst["te_dlast"]:
                            W("vector", "TE", fst["te_dlast"])
                    W("vector", "SC", s_t2.get(j - 2))
                    O("vector", lambda e, j=j, hf=hf, jp=jp: e.tensor_tensor(
                        TGR[:, 1024 + jp * 256:1280 + jp * 256],
                        GU[:, j * 512 + hf * 256:j * 512 + hf * 256 + 256],
                        IBC[:, 512 + hf * 256:512 + hf * 256 + 256], ALU.mult))
                    v_t1 = O("vector", lambda e: e.drain(), "VE")
                    W("scalar", "VE", v_t1)
                    s_t2[j] = O("scalar", lambda e, jp=jp: e.activation(
                        TGR[:, 1536 + jp * 256:1792 + jp * 256],
                        TGR[:, 1024 + jp * 256:1280 + jp * 256], AF.Copy, bias=MAGIC), "SC")
                    W("vector", "SC", s_t2[j])
                    v_rnd[j] = O("vector", lambda e, j=j, jp=jp: e.tensor_scalar(
                        abf(A_GUQ + j * 256, 256), TGR[:, 1536 + jp * 256:1792 + jp * 256],
                        -MAGIC, None, ALU.add), "VE")
                te_dj = {}
                wdsl = A_WG if hf == 0 else A_WU
                for j in range(32):
                    p = j % 2
                    W("sync", "TE", te_dj.get(
                        j - 2, (fst["te_gu"] if hf == 0 else fst["te_dlast"])
                        if j < 2 else None))
                    lw = dma("sync", abf(wdsl + p * 1024, 1024),
                             tdd[:, j * 1024:(j + 1) * 1024], f"LW{p}")
                    W("tensor", f"LW{p}", lw)
                    W("tensor", "VE", v_rnd[j])
                    if j == 0:
                        W("tensor", "SC", fst["s_down"])
                    O("tensor", lambda e, j=j, p=p, w=wdsl: e.matmul(
                        PA[:, 0:512], abf(A_GUQ + j * 256, 128),
                        abf(w + p * 1024, 512), start=(j == 0), stop=(j == 31)))
                    O("tensor", lambda e, j=j, p=p, w=wdsl: e.matmul(
                        PA[:, 512:1024], abf(A_GUQ + j * 256, 128),
                        abf(w + p * 1024 + 512, 512), start=(j == 0), stop=(j == 31)))
                    O("tensor", lambda e, j=j, p=p, w=wdsl: e.matmul(
                        P[0][:, :], abf(A_GUQ + j * 256 + 128, 128),
                        abf(w + p * 1024, 512), start=(j == 0), stop=(j == 31)))
                    te_dj[j] = O("tensor", lambda e, j=j, p=p, w=wdsl: e.matmul(
                        P[1][:, :], abf(A_GUQ + j * 256 + 128, 128),
                        abf(w + p * 1024 + 512, 512), start=(j == 0), stop=(j == 31)), "TE")
                fst["te_dlast"] = te_dj[31]
                W("scalar", "TE", te_dj[31])
                W("scalar", "VE", v_s2bc)
                v_x = 0
                for tt2 in range(2):
                    tt = hf * 2 + tt2
                    t = blk * 4 + tt
                    if tt2 == 1:
                        W("scalar", "VE", v_x)
                    if tt2 == 0:
                        O("scalar", lambda e, tt=tt: e.activation(
                            TGR[:, 1024:1536], PA[:, 0:512], AF.Identity,
                            scale=stc(SC_S2 + tt)[:, 0:1]))
                        s_e = O("scalar", lambda e, tt=tt: e.activation(
                            TGR[:, 1536:2048], PA[:, 512:1024], AF.Identity,
                            scale=stc(SC_S2 + tt)[:, 0:1]), "SC")
                    else:
                        O("scalar", lambda e, tt=tt: e.activation(
                            TGR[:, 1024:1536], P[0][:, :], AF.Identity,
                            scale=stc(SC_S2 + tt)[:, 0:1]))
                        s_e = O("scalar", lambda e, tt=tt: e.activation(
                            TGR[:, 1536:2048], P[1][:, :], AF.Identity,
                            scale=stc(SC_S2 + tt)[:, 0:1]), "SC")
                    fst["s_down"] = s_e
                    W("vector", "SC", s_e)
                    v_x = O("vector", lambda e, t=t: e.tensor_tensor(
                        X[:, t * 1024:(t + 1) * 1024], X[:, t * 1024:(t + 1) * 1024],
                        TGR[:, 1024:2048], ALU.add), "VE")
                    vdrain()
            fst["te_blk"] = fst["te_dlast"]
        return v_x

    W("vector", "S20", C["S20"])
    W("vector", "S21", C["S21"])
    ln_prepass(SC_RSTD1, SC_NMR1)
    v_last = ffn(SC_RSTD1, SC_NMR1)

    if stop_after == "ffn1":
        W("scalar", "VE", C["VE"])
        for t in range(NT):
            dma("scalar", outd[128 * t:128 * t + 128, :], X[:, t * 1024:(t + 1) * 1024], "S20")
        W("sync", "S20", C["S20"])
        W("gpsimd", "S20", C["S20"])
        prog.emit_block()
        stack.close()
        return nc

    ln_prepass(SC_RSTD2, SC_NMR2)

    W("sync", "TE", C["TE"])
    W("sync", "S20", C["S20"])
    W("sync", "S21", C["S21"])
    ld_wv = 0
    for ic in range(8):
        ld_wv = dma("sync", abf(A_GUQ + ic * 1024, 1024), wvbd[:, ic * 1024:(ic + 1) * 1024])

    st_kq = [0, 0]
    st_v = [0, 0]
    v_qrd = [0, 0]
    W("scalar", "VE", C["VE"])
    te_o = {}
    s_ev = {}
    v_vs = {}
    te_vlast = None
    for blk in range(NB):
        v_cp_last = 0
        W("vector", "S20", st_kq[0])
        W("vector", "S21", st_kq[1])
        for ti in range(4):
            t = blk * 4 + ti
            v_cp_last = xln_tile(t, ti, SC_RSTD2, SC_NMR2, False,
                                 entry_te=(te_vlast if ti == 0 else None))
        for i, oc in enumerate(list(range(8, 16)) + list(range(0, 8))):
            g = blk * 16 + i
            p = g % 2
            W("sync", "TE", te_o.get(g - 2))
            ld_w = dma("sync", abf(A_WG + p * 1024, 1024), wqkb[:, oc * 1024:(oc + 1) * 1024],
                       f"LW{p}")
            W("tensor", f"LW{p}", ld_w)
            if i == 0:
                W("tensor", "VE", v_cp_last)
                if blk == 0:
                    W("tensor", "LD", ld_wv)
                    W("tensor", "SC", fst["s_down"])
            W("tensor", "SC", s_ev.get(g - 2))
            for ic in range(8):
                te_og = O("tensor", lambda e, ic=ic, p=p: e.matmul(
                    P[p][:, :], abf(A_WG + p * 1024 + ic * 128, 128),
                    abf(A_XQT + ic * 512, 512), start=(ic == 0), stop=(ic == 7)),
                    "TE" if ic == 7 else None)
            te_o[g] = te_og
            W("scalar", "TE", te_o[g])
            W("scalar", f"S2{g % 2}", st_kq[g % 2])
            W("scalar", "VE", v_qrd[g % 2])
            sl = g % 2
            s_ev[g] = O("scalar", lambda e, p=p, oc=oc, sl=sl: e.activation(
                abf(A_STG + sl * 1024, 512), P[p][:, :], AF.Identity,
                bias=BQK[:, oc:oc + 1]), "SC")
            if oc >= 8:
                W("scalar", "SC", s_ev[g])
                st_kq[g % 2] = dma(
                    "scalar",
                    kkin[:, (oc - 8) * 2048 + blk * 512:(oc - 8) * 2048 + blk * 512 + 512],
                    abf(A_STG + sl * 1024, 512), f"S2{g % 2}")
            else:
                W("scalar", "SC", s_ev[g])
                st_kq[g % 2] = dma("scalar",
                            qtd[:, oc * 2048 + blk * 512:oc * 2048 + blk * 512 + 512],
                            abf(A_STG + sl * 1024, 512), f"S2{g % 2}")
        for tt in range(4):
            t = blk * 4 + tt
            g2 = blk * 4 + tt
            sl = g2 % 2
            W("vector", f"SV{sl}", st_v[sl])
            v_ms = O("vector", lambda e, sl=sl: e.memset(abf(A_VST + sl * 1040, 1040), 1.0), "VE")
            for oh in range(2):
                pbank = PA if g2 % 2 == 0 else PB
                pb = pbank[:, oh * 512:(oh + 1) * 512]
                W("tensor", "VE", v_vs.get((g2 - 2, oh)))
                if g2 == 0 and oh == 0:
                    W("tensor", "VE", v_ms)
                for ic in range(8):
                    te_v = O("tensor", lambda e, ic=ic, tt=tt, oh=oh, pb=pb: e.matmul(
                        pb, abf(A_XQT + ic * 512 + tt * 128, 128),
                        abf(A_GUQ + ic * 1024 + oh * 512, 512),
                        start=(ic == 0), stop=(ic == 7)), "TE" if ic == 7 else None)
                te_vlast = te_v
                W("vector", "TE", te_v)
                if oh == 0 and g2 > 0:
                    W("vector", "VE", v_ms)
                v_vs[(g2, oh)] = O("vector", lambda e, sl=sl, oh=oh, pb=pb: e.tensor_tensor(
                    abf(A_VST + sl * 1040, 1040).rearrange(
                        "p (h c) -> p h c", c=65)[:, 8 * oh:8 * (oh + 1), 0:64],
                    pb.rearrange("p (h c) -> p h c", c=64),
                    BB[:, oh * 512:(oh + 1) * 512].rearrange("p (h c) -> p h c", c=64),
                    ALU.add), "VE")
            W("scalar", "VE", v_vs[(g2, 1)])
            for pr in range(8):
                st_v[sl] = dma("scalar",
                            vvin[:, pr * 2080 + t * 130:pr * 2080 + t * 130 + 130],
                            abf(A_VST + sl * 1040 + pr * 130, 130), f"SV{sl}")

    W("gpsimd", "S20", C["S20"])
    W("gpsimd", "S21", C["S21"])
    W("gpsimd", "SV0", C["SV0"])
    W("gpsimd", "SV1", C["SV1"])
    O("gpsimd", lambda e: e.collective_compute(
        "AllGather", ALU.bypass, ins=[kvin[:, :]], outs=[kvout[:, :]],
        replica_groups=groups), "CC")

    te_pr = {}
    st_h = [0, 0]
    te_nc_prev = None
    s_exp_prev = None
    vev = {}
    ld_kt = {}
    ld_v = {}
    ld_qt = {}
    for pr in range(8):
        sl = pr % 2
        if pr == 0:
            W("sync", "CC", 1)
            W("sync", "TE", C["TE"])
        W("sync", "TE", te_pr.get(pr - 2))
        for h in range(2):
            dma("sync", abf(A_KT + sl * 4096 + h * 2048, 2048),
                kkout[h * 128:(h + 1) * 128, pr * 2048:(pr + 1) * 2048], f"LK{sl}")
            dma("sync", abf(A_V + sl * 4160 + h * 2080, 2080),
                vvout[h * 128:(h + 1) * 128, pr * 2080:(pr + 1) * 2080], f"LK{sl}")
        ld_v[pr] = C[f"LK{sl}"]
        ld_qt[pr] = dma("sync", abf(A_QT + sl * 2048, 2048), qtd[:, pr * 2048:(pr + 1) * 2048],
                        f"LK{sl}")
        W("vector", f"LK{sl}", ld_qt[pr])
        if te_nc_prev:
            W("vector", "TE", te_nc_prev)
        for h2 in range(2):
            O("vector", lambda e, sl=sl, h2=h2: e.tensor_tensor(
                GU[:, h2 * 2048:(h2 + 1) * 2048], abf(A_KT + sl * 4096 + h2 * 2048, 2048),
                abf(A_KT + sl * 4096 + h2 * 2048, 2048), ALU.mult))
        W("vector", f"LK{sl}", ld_qt[pr])
        v_ksq = O("vector", lambda e, sl=sl: e.tensor_tensor(
            GU[:, 4096:6144], abf(A_QT + sl * 2048, 2048),
            abf(A_QT + sl * 2048, 2048), ALU.mult), "VE")
        W("tensor", "VE", v_ksq)
        v_kacc = 0
        for i8 in range(12):
            te_c2 = O("tensor", lambda e, i8=i8: e.matmul(
                P6[0:2, 0:512], ONES2[:, :], GU[:, i8 * 512:(i8 + 1) * 512],
                start=True, stop=True), "TE")
            W("vector", "TE", te_c2)
            O("vector", lambda e: e.reduce_max(
                STAT[0:2, SC_TMP:SC_TMP + 1], P6[0:2, 0:512], axis=AX.X))
            vdrain()
            dstc = SC_KMX if i8 < 8 else SC_KMX + 1
            if i8 in (0, 8):
                O("vector", lambda e, dstc=dstc: e.tensor_copy(
                    STAT[0:2, dstc:dstc + 1], STAT[0:2, SC_TMP:SC_TMP + 1]))
            else:
                O("vector", lambda e, dstc=dstc: e.tensor_tensor(
                    STAT[0:2, dstc:dstc + 1], STAT[0:2, dstc:dstc + 1],
                    STAT[0:2, SC_TMP:SC_TMP + 1], ALU.max))
            v_kacc = O("vector", lambda e: e.drain(), "VE")
            if i8 < 11:
                W("tensor", "VE", v_kacc)
        if s_exp_prev:
            W("vector", "SC", s_exp_prev)
        O("vector", lambda e, pr=pr: e.tensor_tensor(
            STAT[0:2, SC_TMP:SC_TMP + 1], STAT[0:2, SC_KMX:SC_KMX + 1],
            STAT[0:2, SC_KMX + 1:SC_KMX + 2], ALU.add))
        vdrain()
        v_nc = O("vector", lambda e: e.tensor_scalar(
            STAT[0:2, SC_TMP + 1:SC_TMP + 2], STAT[0:2, SC_TMP:SC_TMP + 1],
            -1.0 / 64.0, 40.0, ALU.mult, ALU.add), "VE")
        W("gpsimd", "VE", v_nc)
        g_nc = dma("gpsimd", TGR[0:1, 1664:1666], STAT[0:2, SC_TMP + 1:SC_TMP + 2], "GD")
        W("vector", "GD", g_nc)
        O("vector", lambda e: e.tensor_reduce(
            TGR[0:1, 1666:1667], TGR[0:1, 1664:1666], AX.X, ALU.min))
        v_nc2 = O("vector", lambda e: e.drain(), "VE")
        W("tensor", "VE", v_nc2)
        te_nc = O("tensor", lambda e: e.matmul(
            P6[0:128, 0:1], ONES[0:1, 0:128], TGR[0:1, 1666:1667],
            start=True, stop=True), "TE")
        te_nc_prev = te_nc
        W("vector", "TE", te_nc)
        v_negc = O("vector", lambda e: e.tensor_copy(
            STAT[:, SC_NEGC:SC_NEGC + 1], P6[0:128, 0:1]), "VE")

        for qb in range(4):
            s_exp = {}
            te_av = {}
            def emit_av(kc):
                h, c16 = kc // 16, kc % 16
                sp = (kc % 2) * 1024
                if kc == 0:
                    prv = vev.get((pr, qb - 1)) if qb > 0 else vev.get((pr - 1, 3))
                    W("tensor", "VE", prv)
                W("tensor", "SC", s_exp[kc])
                O("tensor", lambda e, sl=sl, h=h, c16=c16, sp=sp, kc=kc: e.matmul(
                    P[0][0:65, :], abf(A_V + sl * 4160 + h * 2080 + c16 * 130, 65),
                    abf(A_PT + sp, 512), start=(kc == 0), stop=(kc == 31)))
                te_av[kc] = O("tensor", lambda e, sl=sl, h=h, c16=c16, sp=sp, kc=kc: e.matmul(
                    P[1][0:65, :], abf(A_V + sl * 4160 + h * 2080 + c16 * 130 + 65, 65),
                    abf(A_PT + sp + 512, 512), start=(kc == 0), stop=(kc == 31)), "TE")

            for kc in range(32):
                h, c16 = kc // 16, kc % 16
                sb = PA if kc % 2 == 0 else PB
                if kc == 0:
                    W("tensor", "VE", v_negc)
                    W("tensor", f"LK{sl}", ld_qt[pr])
                W("tensor", "SC", s_exp.get(kc - 2))
                O("tensor", lambda e, sl=sl, h=h, c16=c16, sb=sb, qb=qb: e.matmul(
                    sb[:, 0:512], abf(A_KT + sl * 4096 + h * 2048 + c16 * 128, 128, 0, 64),
                    abf(A_QT + sl * 2048 + qb * 512, 512, 0, 64),
                    start=True, stop=True, tile_position=(0, 0)))
                te_s = O("tensor", lambda e, sl=sl, h=h, c16=c16, sb=sb, qb=qb: e.matmul(
                    sb[:, 512:1024], abf(A_KT + sl * 4096 + h * 2048 + c16 * 128, 128, 64, 128),
                    abf(A_QT + sl * 2048 + qb * 512, 512, 64, 128),
                    start=True, stop=True, tile_position=(64, 0)), "TE")
                W("scalar", "TE", te_s)
                if kc >= 2:
                    W("scalar", "TE", te_av[kc - 2])
                sp = (kc % 2) * 1024
                s_exp[kc] = O("scalar", lambda e, sb=sb, sp=sp: e.activation(
                    abf(A_PT + sp, 1024), sb[:, 0:1024], AF.Exp,
                    scale=ISCALE, bias=stc(SC_NEGC)[:, 0:1]), "SC")
                if kc >= 1:
                    emit_av(kc - 1)
            emit_av(31)
            s_exp_prev = s_exp[31]
            v_last2 = 0
            for hh in range(2):
                if hh == 0:
                    W("vector", "TE", te_av[31])
                v_cx = O("vector", lambda e, hh=hh: e.tensor_copy(
                    CTX[0:65, :], P[hh][0:65, :]), "VE")
                W("gpsimd", "VE", v_cx)
                g_rs = dma("gpsimd", TGR[0:1, 1536:2048], CTX[64:65, 0:512], "GD")
                W("vector", "GD", g_rs)
                O("vector", lambda e: e.reciprocal(
                    TGR[0:1, 1536:2048], TGR[0:1, 1536:2048]))
                v_rc = O("vector", lambda e: e.drain(), "VE")
                W("tensor", "VE", v_rc)
                te_rp = O("tensor", lambda e: e.matmul(
                    P6[0:64, 0:512], ONES[0:1, 0:64], TGR[0:1, 1536:2048],
                    start=True, stop=True), "TE")
                W("vector", "TE", te_rp)
                W("vector", f"SH{hh}", st_h[hh])
                v_h = O("vector", lambda e, hh=hh: e.tensor_tensor(
                    abf(A_HST + hh * 512, 512, 0, 64), CTX[0:64, :], P6[0:64, 0:512],
                    ALU.mult), "VE")
                W("scalar", "VE", v_h)
                st_h[hh] = dma("scalar",
                    htd[64 * hh:64 * (hh + 1),
                        pr * 2048 + qb * 512:pr * 2048 + qb * 512 + 512],
                    abf(A_HST + hh * 512, 512, 0, 64), f"SH{hh}")
                vdrain()
                v_last2 = v_h
            vev[(pr, qb)] = v_last2
        te_pr[pr] = C["TE"]

    W("sync", "TE", C["TE"])
    ld_wo = 0
    for ic in range(8):
        ld_wo = dma("sync", abf(A_GUQ + ic * 1024, 1024), wobd[:, ic * 1024:(ic + 1) * 1024])
    te_op = {}
    s_op = {}
    v_xo = {}
    for t in range(NT):
        p = t % 2
        W("sync", "TE", te_op.get(t - 2))
        if t == 0:
            W("sync", "SH0", C["SH0"])
            W("sync", "SH1", C["SH1"])
        ld_h = 0
        for fc in range(8):
            ld_h = dma("sync", abf(A_QT + p * 1024 + fc * 128, 128),
                       htd[:, fc * 2048 + t * 128:fc * 2048 + t * 128 + 128], f"LK{p}")
        W("tensor", f"LK{p}", ld_h)
        if t == 0:
            W("tensor", "LD", ld_wo)
        if t == 0:
            W("tensor", "SC", C["SC"])
            W("tensor", "VE", C["VE"])
        W("tensor", "VE", v_xo.get(t - 2))
        pbank = PA if p == 0 else PB
        ob0 = pbank[:, 0:512]
        ob1 = pbank[:, 512:1024]
        for fc in range(8):
            O("tensor", lambda e, fc=fc, p=p, ob0=ob0: e.matmul(
                ob0, abf(A_QT + p * 1024 + fc * 128, 128),
                abf(A_GUQ + fc * 1024, 512), start=(fc == 0), stop=(fc == 7)))
            te_ot = O("tensor", lambda e, fc=fc, p=p, ob1=ob1: e.matmul(
                ob1, abf(A_QT + p * 1024 + fc * 128, 128),
                abf(A_GUQ + fc * 1024 + 512, 512), start=(fc == 0), stop=(fc == 7)),
                "TE" if fc == 7 else None)
        te_op[t] = te_ot
        W("scalar", "TE", te_op[t])
        W("scalar", "SC", s_op.get(t - 1, 0))
        W("scalar", "VE", v_xo.get(t - 1))
        s_a = O("scalar", lambda e, t=t: e.activation(
            XLN[:, 0:1024], X[:, t * 1024:(t + 1) * 1024], AF.Identity,
            bias=stc(SC_NMR2 + t)[:, 0:1], scale=stc(SC_RSTD2 + t)[:, 0:1]), "SC")
        s_op[t] = s_a
        W("vector", "SC", s_a)
        for oh in range(2):
            ob = (PA if p == 0 else PB)[:, oh * 512:(oh + 1) * 512]
            O("vector", lambda e, oh=oh, ob=ob: e.tensor_tensor(
                TGR[:, oh * 512:(oh + 1) * 512], ob,
                BB[:, 1024 + oh * 512:1536 + oh * 512], ALU.add))
        vdrain()
        O("vector", lambda e: e.tensor_tensor(
            TGR[:, 1024:2048], TGR[:, 0:1024], XLN[:, 0:1024], ALU.add))
        vdrain()
        v_xo[t] = O("vector", lambda e, t=t: e.tensor_tensor(
            X[:, t * 1024:(t + 1) * 1024], X[:, t * 1024:(t + 1) * 1024],
            TGR[:, 1024:2048], ALU.add), "VE")
        vdrain()
    fst["te_blk"] = te_op[NT - 1]
    fst["s_down"] = s_op[NT - 1]

    if stop_after != "attn":
        ln_prepass(SC_RSTD1, SC_NMR1)
        ffn(SC_RSTD1, SC_NMR1)

    W("scalar", "VE", C["VE"])
    for t in range(NT):
        dma("scalar", outd[128 * t:128 * t + 128, :], X[:, t * 1024:(t + 1) * 1024], "S20")
    for sn in ("S20", "S21", "SV0", "SV1", "SH0", "SH1"):
        W("sync", sn, C[sn])
        W("gpsimd", sn, C[sn])
    prog.emit_block()
    stack.close()
    return nc


def _pm(w, nj, nic):
    return np.ascontiguousarray(
        w.reshape(nj, 128, nic, 128).transpose(3, 0, 2, 1).reshape(128, nj * nic * 128))


def make_inputs(L0_attn_x, L0_attn_w_qkv, L0_attn_b_qkv, L0_attn_w_o, L0_attn_b_o,
                L0_ffn_w_gate, L0_ffn_w_up, L0_ffn_w_down, n_cores=8):
    x = np.ascontiguousarray(np.asarray(L0_attn_x, dtype=np.float32))
    w_qkv = np.asarray(L0_attn_w_qkv, dtype=np.float32)
    b_qkv = np.asarray(L0_attn_b_qkv, dtype=np.float32)
    w_o = np.asarray(L0_attn_w_o, dtype=np.float32)
    b_o = np.asarray(L0_attn_b_o, dtype=np.float32)
    w_gate = np.asarray(L0_ffn_w_gate, dtype=np.float32)
    w_up = np.asarray(L0_ffn_w_up, dtype=np.float32)
    w_down = np.asarray(L0_ffn_w_down, dtype=np.float32)

    shared = {
        "wgpm": _pm(w_gate, 32, 8),
        "wupm": _pm(w_up, 32, 8),
        "wdpm": np.ascontiguousarray(
            w_down.T.reshape(32, 128, 1024).transpose(1, 0, 2).reshape(128, 32 * 1024)),
        "wqkpm": _pm(w_qkv[0:2048], 16, 8),
        "wvpm": np.ascontiguousarray(
            w_qkv[2048:3072].T.reshape(8, 128, 1024).transpose(1, 0, 2).reshape(128, 8 * 1024)),
        "wopm": np.ascontiguousarray(
            w_o.T.reshape(8, 128, 1024).transpose(1, 0, 2).reshape(128, 8 * 1024)),
        "bqkc": np.ascontiguousarray(b_qkv[0:2048].reshape(16, 128).T),
        "bvr": np.ascontiguousarray(b_qkv[2048:3072].reshape(1, 1024)),
        "bor": np.ascontiguousarray(b_o.reshape(1, 1024)),
        "ident": np.eye(128, dtype=ml_dtypes.bfloat16),
        "identf": np.eye(128, dtype=np.float32),
    }
    in_maps = []
    for c in range(n_cores):
        b, h = c // 2, c % 2
        m = dict(shared)
        m["x"] = np.ascontiguousarray(x[b, 2048 * h:2048 * h + 2048])
        in_maps.append(m)
    return in_maps


def kernel(L0_attn_x, L0_attn_w_qkv, L0_attn_b_qkv, L0_attn_w_o, L0_attn_b_o,
           L0_ffn_w_gate, L0_ffn_w_up, L0_ffn_w_down):
    in_maps = make_inputs(L0_attn_x, L0_attn_w_qkv, L0_attn_b_qkv, L0_attn_w_o,
                          L0_attn_b_o, L0_ffn_w_gate, L0_ffn_w_up, L0_ffn_w_down)
    nc = bass.Bass("TRN2", target_bir_lowering=False)
    build(nc)
    res = run_bass_kernel_spmd(nc, in_maps, core_ids=list(range(8)))
    out = np.empty((4, 4096, 1024), dtype=np.float32)
    for c in range(8):
        b, h = c // 2, c % 2
        out[b, 2048 * h:2048 * h + 2048] = res.results[c]["out"]
    return out


# version 07
# speedup vs baseline: 1.4200x; 1.0736x over previous
import os
import contextlib

import numpy as np
import ml_dtypes
import concourse.bass as bass
import concourse.mybir as mybir
from concourse.bass_utils import run_bass_kernel_spmd

F32 = mybir.dt.float32
BF16 = mybir.dt.bfloat16
FP8 = mybir.dt.float8e4
AF = mybir.ActivationFunctionType
ALU = mybir.AluOpType
AX = mybir.AxisListType

MAGIC = float(np.float32(1.5 * 2.0 ** 52)) if os.environ.get("BASS_SIM_FP64") == "1" \
    else float(np.float32(1.5 * 2.0 ** 23))
D = 1024
T = 2048
NT = 16
NB = 4
EPS = 1e-5
ISCALE = 1.0 / 32.0

A_XQT = 0
A_WG = 4096
A_WU = 6144
A_GUQ = 8192
A_VST = 16384
A_STG = 22528
A_KT = 0
A_V = 16384
A_V = 8192
A_QT = 16640
A_PT = 20736
A_HST = 22784

SC_RSTD1, SC_NMR1 = 0, 16
SC_RSTD2, SC_NMR2 = 32, 48
SC_SX, SC_INVSX = 64, 80
SC_S2 = 96
SC_INVS2 = 100
SC_WSC = 104
SC_TMP = 112
SC_QMX = 128
SC_KMX = 144
SC_NEGC = 148


class Prog:
    def __init__(self, nc, stack):
        self.nc = nc
        self.streams = {k: [] for k in ("sync", "tensor", "scalar", "vector", "gpsimd")}
        self.sems = {}
        self.cnt = {}
        self._stack = stack

    def sem(self, name):
        self.sems[name] = self._stack.enter_context(self.nc.semaphore(name=name))
        self.cnt[name] = 0

    def wait(self, eng, sname, val):
        if val is None or val <= 0:
            return
        h = self.sems[sname]
        self.streams[eng].append(lambda e, h=h, v=val: e.wait_ge(h, v))

    def op(self, eng, build, sname=None, n=1):
        if sname is None:
            self.streams[eng].append(build)
            return None
        h = self.sems[sname]
        self.cnt[sname] += n
        v = self.cnt[sname]

        def f(e, build=build, h=h, n=n):
            build(e).then_inc(h, n)

        self.streams[eng].append(f)
        return v

    def emit_block(self):
        with self.nc.Block() as block:
            @block.sync
            def _(e):
                for f in self.streams["sync"]:
                    f(e)

            @block.tensor
            def _(e):
                for f in self.streams["tensor"]:
                    f(e)

            @block.scalar
            def _(e):
                for f in self.streams["scalar"]:
                    f(e)

            @block.vector
            def _(e):
                for f in self.streams["vector"]:
                    f(e)

            @block.gpsimd
            def _(e):
                for f in self.streams["gpsimd"]:
                    f(e)


def build(nc, groups=None, stop_after=None):
    if groups is None:
        groups = [[0, 1], [2, 3], [4, 5], [6, 7]]
    dt = nc.dram_tensor
    xd = dt("x", [T, D], F32, kind="ExternalInput").ap()
    wgpm = dt("wgpm", [128, 32 * 1024], F32, kind="ExternalInput").ap()
    wupm = dt("wupm", [128, 32 * 1024], F32, kind="ExternalInput").ap()
    wdpm = dt("wdpm", [128, 32 * 1024], F32, kind="ExternalInput").ap()
    wqkpm = dt("wqkpm", [128, 16 * 1024], F32, kind="ExternalInput").ap()
    wvpm = dt("wvpm", [128, 8 * 1024], F32, kind="ExternalInput").ap()
    wopm = dt("wopm", [128, 8 * 1024], F32, kind="ExternalInput").ap()
    bqkc = dt("bqkc", [128, 16], F32, kind="ExternalInput").ap()
    bvr = dt("bvr", [1, D], F32, kind="ExternalInput").ap()
    bord = dt("bor", [1, D], F32, kind="ExternalInput").ap()
    identd = dt("ident", [128, 128], BF16, kind="ExternalInput").ap()
    identf = dt("identf", [128, 128], F32, kind="ExternalInput").ap()
    outd = dt("out", [T, D], F32, kind="ExternalOutput").ap()

    tgd = dt("tgs", [128, 32 * 1024], FP8, kind="Internal").ap()
    tud = dt("tus", [128, 32 * 1024], FP8, kind="Internal").ap()
    tdd = dt("tds", [128, 32 * 1024], FP8, kind="Internal").ap()
    wqkb = dt("wqkbs", [128, 16 * 1024], BF16, kind="Internal").ap()
    wvbd = dt("wvbs", [128, 8 * 1024], BF16, kind="Internal").ap()
    wobd = dt("wobs", [128, 8 * 1024], BF16, kind="Internal").ap()
    qtd = dt("qts", [128, 8 * 2048], BF16, kind="Internal").ap()
    htd = dt("hts", [128, 8 * 2048], BF16, kind="Internal").ap()
    kvin = dt("kvin", [128, 8 * 2048 + 8 * 2080], BF16, kind="Internal").ap()
    kvout = dt("kvout", [256, 8 * 2048 + 8 * 2080], BF16, kind="Internal").ap()
    kkin = kvin[:, 0:8 * 2048]
    vvin = kvin[:, 8 * 2048:]
    kkout = kvout[:, 0:8 * 2048]
    vvout = kvout[:, 8 * 2048:]

    stack = contextlib.ExitStack()
    sbuf = lambda name, shape, dty: stack.enter_context(nc.sbuf_tensor(name, shape, dty))
    pst = lambda name, shape, dty=F32: stack.enter_context(nc.psum_tensor(name, shape, dty))

    X = sbuf("X", [128, NT * 1024], F32)
    GU = sbuf("GU", [128, 32 * 512], F32)
    XLN = sbuf("XLN", [128, 1024], F32)
    TGR = sbuf("TGR", [128, 2048], F32)
    CTX = sbuf("CTX", [128, 512], F32)
    MNA = sbuf("MNA", [128, 512], F32)
    STAT = sbuf("STAT", [128, 160], F32)
    IBC = sbuf("IBC", [128, 1024], F32)
    ABF = sbuf("ABF", [128, 23808], BF16)
    AF8 = sbuf("AF8", [128, 7168], FP8)
    BB = sbuf("BB", [128, 2048], BF16)
    IDT = sbuf("IDT", [128, 128], BF16)
    IDF = sbuf("IDF", [128, 128], F32)
    BQK = sbuf("BQK", [128, 16], F32)
    ONES = sbuf("ONES", [128, 128], F32)
    ONES2 = sbuf("ONES2", [128, 2], F32)

    P = [pst(f"P{i}", [128, 512]) for i in range(2)]
    PA = pst("PA", [128, 1024])
    PB = pst("PB", [128, 1024])
    P6 = pst("P6", [128, 512])
    PT8 = pst("PT8", [128, 1024], BF16)

    prog = Prog(nc, stack)
    for s in ("LD", "LW0", "LW1", "LK0", "LK1", "GW0", "GW1", "GM0", "GM1", "LM0", "GD", "GS0", "GS1",
              "S20", "S21", "SV0", "SV1", "SH0", "SH1", "TE", "SC", "VE", "CC"):
        prog.sem(s)
    W, O, C = prog.wait, prog.op, prog.cnt

    def dma(eng, out, in_, sname="LD", **kw):
        return O(eng, lambda e, o=out, i=in_, kw=kw: e.dma_start(out=o, in_=i, **kw),
                 sname, 16)

    def vdrain():
        O("vector", lambda e: e.drain())

    def stc(c):
        return STAT[:, c:c + 1]

    def abf(lo, n, p0=0, p1=128):
        return ABF[p0:p1, lo:lo + n]

    def stg(sl):
        return A_STG if sl == 0 else A_WU

    dma("sync", IDT[:], identd[:])
    dma("sync", IDF[:], identf[:])
    for t in range(NT):
        dma("sync", X[:, t * 1024:(t + 1) * 1024], xd[128 * t:128 * t + 128, :])
    dma("sync", BQK[:], bqkc[:])
    dma("sync", TGR[0:1, 0:1024], bvr[:])
    ld_b = dma("sync", TGR[0:1, 1024:2048], bord[:])
    O("vector", lambda e: e.memset(ONES2[:, :], 0.0))
    vdrain()
    O("vector", lambda e: e.memset(ONES2[0:64, 0:1], 1.0))
    vdrain()
    O("vector", lambda e: e.memset(ONES2[64:128, 1:2], 1.0))
    vdrain()
    v_ones = O("vector", lambda e: e.memset(ONES[:], 1.0), "VE")
    W("tensor", "VE", v_ones)
    W("tensor", "LD", ld_b)
    v_bb = 0
    for h in range(2):
        W("tensor", "VE", v_bb)
        O("tensor", lambda e, h=h: e.matmul(
            P[0][:, :], ONES[0:1, 0:128], TGR[0:1, 512 * h:512 * (h + 1)],
            start=True, stop=True))
        te_b = O("tensor", lambda e, h=h: e.matmul(
            P[1][:, :], ONES[0:1, 0:128], TGR[0:1, 1024 + 512 * h:1536 + 512 * h],
            start=True, stop=True), "TE")
        W("vector", "TE", te_b)
        O("vector", lambda e, h=h: e.tensor_copy(BB[:, 512 * h:512 * (h + 1)], P[0][:, :]))
        v_bb = O("vector", lambda e, h=h: e.tensor_copy(
            BB[:, 1024 + 512 * h:1536 + 512 * h], P[1][:, :]), "VE")
    ve_bias = C["VE"]

    NCOL = 128 * 32 * 1024

    def wprep(src, dst, wsc_col, first_wait):
        v_red = [0] * 8
        for i in range(8):
            p = i % 2
            W("sync", "VE", v_red[i - 2] if i >= 2 else first_wait)
            lv = dma("sync", GU[:, p * 4096:(p + 1) * 4096], src[:, i * 4096:(i + 1) * 4096],
                     f"LW{p}")
            W("vector", f"LW{p}", lv)
            v_red[i] = O("vector", lambda e, i=i, p=p: e.reduce_sum(
                stc(SC_TMP + i), GU[:, p * 4096:(p + 1) * 4096], axis=AX.X,
                apply_absolute_value=True), "VE")
        vdrain()
        O("vector", lambda e: e.reduce_sum(
            stc(SC_TMP + 8), STAT[:, SC_TMP:SC_TMP + 8], axis=AX.X))
        v_s = O("vector", lambda e: e.drain(), "VE")
        W("tensor", "VE", v_s)
        te_s = O("tensor", lambda e: e.matmul(
            P6[0:1, 0:1], stc(SC_TMP + 8), ONES[:, 0:1], start=True, stop=True), "TE")
        W("vector", "TE", te_s)
        O("vector", lambda e: e.tensor_scalar(
            stc(SC_TMP + 9)[0:1, :], P6[0:1, 0:1], 1.0 / NCOL, 1e-5, ALU.mult, ALU.max))
        vdrain()
        v_r = O("vector", lambda e: e.reciprocal(
            stc(SC_TMP + 10)[0:1, :], stc(SC_TMP + 9)[0:1, :]), "VE")
        W("tensor", "VE", v_r)
        O("tensor", lambda e: e.matmul(
            P6[0:128, 0:1], ONES[0:1, 0:128], stc(SC_TMP + 10)[0:1, :],
            start=True, stop=True))
        te_b2 = O("tensor", lambda e: e.matmul(
            P6[0:128, 1:2], ONES[0:1, 0:128], stc(SC_TMP + 9)[0:1, :],
            start=True, stop=True), "TE")
        W("vector", "TE", te_b2)
        O("vector", lambda e: e.tensor_copy(stc(SC_TMP + 11), P6[0:128, 0:1]))
        v_bc = O("vector", lambda e, c=wsc_col: e.tensor_copy(stc(c), P6[0:128, 1:2]), "VE")
        st_prev = [C["GS0"], C["GS1"]]
        s_done = [0] * 8
        v_done = [0] * 8
        for i in range(8):
            p = i % 2
            W("sync", "VE", v_done[i - 2] if i >= 2 else v_bc)
            lv = dma("sync", GU[:, p * 4096:(p + 1) * 4096], src[:, i * 4096:(i + 1) * 4096],
                     f"LW{p}")
            W("scalar", f"LW{p}", lv)
            W("scalar", "VE", v_done[i - 1] if i >= 1 else v_bc)
            s_done[i] = O("scalar", lambda e, p=p: e.activation(
                GU[:, 8192 + p * 4096:8192 + (p + 1) * 4096],
                GU[:, p * 4096:(p + 1) * 4096],
                AF.Copy, bias=MAGIC, scale=stc(SC_TMP + 11)[:, 0:1]), "SC")
            W("vector", "SC", s_done[i])
            W("vector", f"GS{p}", st_prev[p])
            O("vector", lambda e, p=p: e.tensor_scalar(
                abf(p * 4096, 4096), GU[:, 8192 + p * 4096:8192 + (p + 1) * 4096],
                -MAGIC, 1.0, ALU.add, ALU.min))
            vdrain()
            v_done[i] = O("vector", lambda e, p=p: e.tensor_scalar(
                abf(p * 4096, 4096), abf(p * 4096, 4096), -1.0, None, ALU.max), "VE")
            W("gpsimd", "VE", v_done[i])
            st_prev[p] = dma("gpsimd", dst[:, i * 4096:(i + 1) * 4096],
                             abf(p * 4096, 4096), f"GS{p}")
        return v_done[7]

    v_p = wprep(wgpm, tgd, SC_WSC + 0, ve_bias)
    v_p = wprep(wupm, tud, SC_WSC + 1, v_p)
    v_p = wprep(wdpm, tdd, SC_WSC + 2, v_p)
    vdrain()
    O("vector", lambda e: e.tensor_scalar(
        stc(SC_WSC + 2), stc(SC_WSC + 2), 0.5, None, ALU.mult), "VE")

    def wcast(src, dst, ncols, v_first):
        v_prev = [0, 0]
        st_prev = [C["S20"], C["S21"]]
        W("vector", "GS0", C["GS0"])
        W("vector", "GS1", C["GS1"])
        for i in range(ncols // 4096):
            p = i % 2
            W("sync", "VE", v_prev[p] if v_prev[p] else v_first)
            lv = dma("sync", GU[:, p * 4096:(p + 1) * 4096], src[:, i * 4096:(i + 1) * 4096],
                     f"LW{p}")
            W("vector", f"LW{p}", lv)
            W("vector", f"S2{p}", st_prev[p])
            v_prev[p] = O("vector", lambda e, p=p: e.tensor_copy(
                abf(p * 4096, 4096), GU[:, p * 4096:(p + 1) * 4096]), "VE")
            W("scalar", "VE", v_prev[p])
            st_prev[p] = dma("scalar", dst[:, i * 4096:(i + 1) * 4096],
                             abf(p * 4096, 4096), f"S2{p}")
        return v_prev[(ncols // 4096 - 1) % 2]

    v_c = wcast(wqkpm, wqkb, 16 * 1024, C["VE"])
    v_c = wcast(wvpm, wvbd, 8 * 1024, v_c)
    v_c = wcast(wopm, wobd, 8 * 1024, v_c)

    def ln_prepass(rstd_base, nmr_base):
        W("vector", "LD", C["LD"])
        W("scalar", "VE", C["VE"])
        W("scalar", "LW0", C["LW0"])
        W("scalar", "LW1", C["LW1"])
        for t in range(NT):
            O("vector", lambda e, t=t: e.reduce_sum(
                stc(SC_TMP + t), X[:, t * 1024:(t + 1) * 1024], axis=AX.X))
        v0 = O("vector", lambda e: e.drain(), "VE")
        s0 = 0
        s_hist = {}
        for t in range(NT):
            W("scalar", "SC", s_hist.get(t - 4, 0))
            s_hist[t] = O("scalar", lambda e, t=t: e.activation(
                GU[:, (t % 4) * 1024:(t % 4) * 1024 + 1024],
                X[:, t * 1024:(t + 1) * 1024], AF.Square,
                accum_out=stc(nmr_base + t)), "SC")
        s0 = s_hist[NT - 1]
        W("vector", "SC", s0)
        TM = STAT[:, SC_TMP:SC_TMP + 16]
        SQ = STAT[:, nmr_base:nmr_base + 16]
        MM = STAT[:, SC_SX:SC_SX + 16]
        O("vector", lambda e, TM=TM: e.tensor_scalar(TM, TM, 1.0 / D, None, ALU.mult))
        O("vector", lambda e, SQ=SQ: e.tensor_scalar(SQ, SQ, 1.0 / D, None, ALU.mult))
        vdrain()
        O("vector", lambda e, TM=TM, MM=MM: e.tensor_tensor(MM, TM, TM, ALU.mult))
        vdrain()
        O("vector", lambda e, SQ=SQ, MM=MM: e.tensor_tensor(SQ, SQ, MM, ALU.subtract))
        vdrain()
        O("vector", lambda e, SQ=SQ: e.tensor_scalar(SQ, SQ, EPS, None, ALU.add))
        vdrain()
        v1 = O("vector", lambda e, SQ=SQ, MM=MM: e.reciprocal(MM, SQ), "VE")
        W("scalar", "VE", v1)
        s1 = O("scalar", lambda e, MM=MM, rb=rstd_base: e.activation(
            STAT[:, rb:rb + 16], MM, AF.Sqrt), "SC")
        W("vector", "SC", s1)
        O("vector", lambda e, TM=TM, rb=rstd_base, nb=nmr_base: e.scalar_tensor_tensor(
            STAT[:, nb:nb + 16], TM, -1.0, STAT[:, rb:rb + 16], ALU.mult, ALU.mult), "VE")
        vdrain()

    st8 = {"seq": 0, "v_xl": {}, "te_tp": {}, "v_cp": {}, "v_tg": {}}

    def xln_tile(t, ti, rstd_base, nmr_base, quant, entry_te=None):
        k = st8["seq"]
        st8["seq"] += 1
        sl = k % 2
        xt = X[:, t * 1024:(t + 1) * 1024]
        xl = XLN[:, 0:1024]
        W("scalar", "VE", st8["v_xl"].get(k - 1))
        W("scalar", "SC", st8.get("s_last", 0))
        s0 = O("scalar", lambda e, xt=xt, xl=xl, t=t: e.activation(
            xl, xt, AF.Identity, bias=stc(nmr_base + t)[:, 0:1],
            scale=stc(rstd_base + t)[:, 0:1]), "SC")
        W("vector", "SC", s0)
        if entry_te:
            W("vector", "TE", entry_te)
        if quant:
            O("vector", lambda e, xl=xl: e.reduce_max(
                stc(SC_TMP + 0), xl, axis=AX.X, apply_absolute_value=True))
            vdrain()
            O("vector", lambda e: e.tensor_scalar(
                stc(SC_TMP + 1), stc(SC_TMP + 0), 1e-5, None, ALU.max))
            vdrain()
            O("vector", lambda e, t=t: e.tensor_scalar(
                stc(SC_INVSX + t % 16), stc(SC_TMP + 1), 1.0 / 127.0, None, ALU.mult))
            O("vector", lambda e: e.reciprocal(stc(SC_TMP + 2), stc(SC_TMP + 1)))
            vdrain()
            v0 = O("vector", lambda e, t=t: e.tensor_scalar(
                stc(SC_SX + t % 16), stc(SC_TMP + 2), 127.0, None, ALU.mult), "VE")
            st8["v_xl"][k] = v0
            W("scalar", "VE", v0)
            W("scalar", "VE", st8["v_tg"].get(k - 1))
            s1 = O("scalar", lambda e, xl=xl, t=t: e.activation(
                TGR[:, 0:1024], xl, AF.Copy, bias=MAGIC,
                scale=stc(SC_SX + t % 16)[:, 0:1]), "SC")
            st8["s_last"] = s1
            W("vector", "SC", s1)
            W("vector", "TE", st8["te_tp"].get(k - 2))
            v1 = O("vector", lambda e, sl=sl: e.tensor_scalar(
                abf(stg(sl), 1024), TGR[:, 0:1024], -MAGIC, None, ALU.add), "VE")
            st8["v_tg"][k] = v1
        else:
            W("vector", "TE", st8["te_tp"].get(k - 2))
            v1 = O("vector", lambda e, xl=xl, sl=sl: e.tensor_copy(
                abf(stg(sl), 1024), xl), "VE")
            st8["v_xl"][k] = v1
        W("tensor", "VE", v1)
        W("tensor", "VE", st8["v_cp"].get(k - 1))
        for fc in range(8):
            te_x = O("tensor", lambda e, fc=fc, sl=sl: e.transpose(
                PT8[:, fc * 128:(fc + 1) * 128],
                abf(stg(sl) + fc * 128, 128), IDT[:]),
                "TE" if fc == 7 else None)
        st8["te_tp"][k] = te_x
        W("vector", "TE", st8["te_tp"][k])
        v_c = 0
        for fc in range(8):
            v_c = O("vector", lambda e, fc=fc, ti=ti: e.tensor_copy(
                abf(A_XQT + fc * 512 + ti * 128, 128), PT8[:, fc * 128:(fc + 1) * 128]),
                "VE" if fc == 7 else None)
        st8["v_cp"][k] = v_c
        return v_c

    def bcast4(col0, ibc_lo, psum_guard=0):
        vdrain()
        v_a = O("vector", lambda e, c=col0: e.tensor_copy(
            TGR[:, 1024:1028], STAT[:, c:c + 4]), "VE")
        W("tensor", "VE", v_a)
        W("tensor", "SC", psum_guard)
        for q in range(4):
            te_a = O("tensor", lambda e, q=q: e.transpose(
                P6[:, q * 128:(q + 1) * 128], TGR[:, 1024 + q:1024 + q + 128], IDF[:]),
                "TE" if q == 3 else None)
        W("vector", "TE", te_a)
        for q in range(4):
            O("vector", lambda e, q=q: e.tensor_copy(
                TGR[0:1, 1152 + q * 128:1152 + (q + 1) * 128],
                P6[0:1, q * 128:(q + 1) * 128]))
        v_b = O("vector", lambda e: e.drain(), "VE")
        W("tensor", "VE", v_b)
        te_c = O("tensor", lambda e: e.matmul(
            P6[:, 0:512], ONES[0:1, 0:128], TGR[0:1, 1152:1664],
            start=True, stop=True), "TE")
        W("vector", "TE", te_c)
        return O("vector", lambda e, lo=ibc_lo: e.tensor_copy(
            IBC[:, lo:lo + 512], P6[:, 0:512]), "VE")

    fst = {"call": 0, "te_gu": None, "s_down": 0, "te_dlast": None, "te_blk": None,
           "wdj": 0, "te_wd": {}}

    def ffn(rstd_base, nmr_base):
        W("scalar", "VE", C["VE"])
        W("sync", "TE", C["TE"])
        for sn in ("S20", "S21", "SV0", "SV1", "SH0", "SH1", "GS0", "GS1"):
            W("sync", sn, C[sn])
            W("vector", sn, C[sn])
        for blk in range(NB):
            v_cp_last = 0
            for ti in range(4):
                t = blk * 4 + ti
                v_cp_last = xln_tile(t, ti, rstd_base, nmr_base, True,
                                     entry_te=(fst["te_blk"] if ti == 0 else None))
            v_ibc = bcast4(SC_INVSX + (blk * 4) % 16, 0, psum_guard=fst["s_down"])
            te_j = {}
            v_pu = {}
            s_g = {}
            v_mm = {}
            for j in range(32):
                p = j % 2
                W("sync", "TE", te_j.get(j - 2, fst["te_dlast"] if j < 2 else None))
                W("gpsimd", "TE", te_j.get(j - 2, fst["te_dlast"] if j < 2 else None))
                ld_w = dma("sync", AF8[:, p * 1024:(p + 1) * 1024],
                           tgd[:, j * 1024:(j + 1) * 1024], f"LW{p}")
                ld_u = dma("gpsimd", AF8[:, 2048 + p * 1024:2048 + (p + 1) * 1024],
                           tud[:, j * 1024:(j + 1) * 1024], f"GW{p}")
                gb = PA[:, 0:512] if p == 0 else PB[:, 0:512]
                ub = PA[:, 512:1024] if p == 0 else PB[:, 512:1024]
                W("tensor", f"LW{p}", ld_w)
                W("tensor", f"GW{p}", ld_u)
                if j == 0:
                    W("tensor", "VE", v_ibc)
                    W("tensor", "SC", fst["s_down"])
                W("tensor", "VE", v_pu.get(j - 2))
                for ic in range(8):
                    O("tensor", lambda e, ic=ic, p=p, gb=gb: e.matmul(
                        gb, AF8[:, p * 1024 + ic * 128:p * 1024 + ic * 128 + 128],
                        abf(A_XQT + ic * 512, 512), start=(ic == 0), stop=(ic == 7)))
                for ic in range(8):
                    te_jj = O("tensor", lambda e, ic=ic, p=p, ub=ub: e.matmul(
                        ub, AF8[:, 2048 + p * 1024 + ic * 128:2048 + p * 1024 + ic * 128 + 128],
                        abf(A_XQT + ic * 512, 512), start=(ic == 0), stop=(ic == 7)),
                        "TE" if ic == 7 else None)
                te_j[j] = te_jj
                W("vector", "TE", te_j[j])
                W("vector", "SC", s_g.get(j - 1))
                O("vector", lambda e, gb=gb: e.tensor_tensor(
                    TGR[:, 1024:1536], gb, IBC[:, 0:512], ALU.mult))
                O("vector", lambda e, ub=ub: e.tensor_tensor(
                    TGR[:, 0:512], ub, IBC[:, 0:512], ALU.mult))
                v_pu[j] = O("vector", lambda e: e.drain(), "VE")
                W("scalar", "VE", v_pu[j])
                s_g[j] = O("scalar", lambda e: e.activation(
                    TGR[:, 1536:2048], TGR[:, 1024:1536], AF.Silu,
                    scale=stc(SC_WSC + 0)[:, 0:1]), "SC")
                W("vector", "SC", s_g[j])
                O("vector", lambda e, j=j: e.scalar_tensor_tensor(
                    GU[:, j * 512:(j + 1) * 512], TGR[:, 0:512],
                    stc(SC_WSC + 1)[:, 0:1], TGR[:, 1536:2048], ALU.mult, ALU.mult))
                vdrain()
                if j == 0:
                    O("vector", lambda e: e.tensor_copy(CTX[:, :], GU[:, 0:512]))
                    v_mm[j] = O("vector", lambda e: e.tensor_copy(MNA[:, :], GU[:, 0:512]), "VE")
                else:
                    O("vector", lambda e, j=j: e.tensor_tensor(
                        CTX[:, :], CTX[:, :], GU[:, j * 512:(j + 1) * 512], ALU.max))
                    v_mm[j] = O("vector", lambda e, j=j: e.tensor_tensor(
                        MNA[:, :], MNA[:, :], GU[:, j * 512:(j + 1) * 512], ALU.min), "VE")
                vdrain()
            fst["te_gu"] = te_j[31]

            O("vector", lambda e: e.tensor_scalar(MNA[:, :], MNA[:, :], -1.0, None, ALU.mult))
            vdrain()
            O("vector", lambda e: e.tensor_tensor(CTX[:, :], CTX[:, :], MNA[:, :], ALU.max))
            vdrain()
            v_mx = O("vector", lambda e: e.drain(), "VE")
            W("tensor", "VE", v_mx)
            W("tensor", "SC", fst["s_down"])
            for q in range(4):
                te_mx = O("tensor", lambda e, q=q: e.transpose(
                    P6[:, q * 128:(q + 1) * 128], CTX[:, q * 128:(q + 1) * 128], IDF[:]),
                    "TE" if q == 3 else None)
            W("vector", "TE", te_mx)
            for q in range(4):
                O("vector", lambda e, q=q: e.reduce_max(
                    stc(SC_TMP + q), P6[:, q * 128:(q + 1) * 128], axis=AX.X))
            vdrain()
            for q in range(4):
                O("vector", lambda e, q=q: e.tensor_scalar(
                    stc(SC_TMP + 4 + q), stc(SC_TMP + q), 1e-5, None, ALU.max))
            vdrain()
            for q in range(4):
                O("vector", lambda e, q=q: e.tensor_scalar(
                    stc(SC_INVS2 + q), stc(SC_TMP + 4 + q), 1.0 / 127.0, None, ALU.mult))
                O("vector", lambda e, q=q: e.reciprocal(
                    stc(SC_TMP + 8 + q), stc(SC_TMP + 4 + q)))
            vdrain()
            for q in range(4):
                O("vector", lambda e, q=q: e.tensor_scalar(
                    stc(SC_TMP + q), stc(SC_TMP + 8 + q), 127.0, None, ALU.mult))
                O("vector", lambda e, q=q: e.tensor_tensor(
                    stc(SC_S2 + q), stc(SC_INVS2 + q), stc(SC_WSC + 2), ALU.mult))
            v_s2bc = bcast4(SC_TMP, 512, psum_guard=fst["s_down"])

            for hf in range(2):
                v_rnd = {}
                s_t2 = {}
                for j in range(32):
                    jp = j % 2
                    if j == 0:
                        W("vector", "VE", v_s2bc)
                        if fst["te_dlast"]:
                            W("vector", "TE", fst["te_dlast"])
                    W("vector", "SC", s_t2.get(j - 2))
                    O("vector", lambda e, j=j, hf=hf, jp=jp: e.tensor_tensor(
                        TGR[:, 1024 + jp * 256:1280 + jp * 256],
                        GU[:, j * 512 + hf * 256:j * 512 + hf * 256 + 256],
                        IBC[:, 512 + hf * 256:512 + hf * 256 + 256], ALU.mult))
                    v_t1 = O("vector", lambda e: e.drain(), "VE")
                    W("scalar", "VE", v_t1)
                    s_t2[j] = O("scalar", lambda e, jp=jp: e.activation(
                        TGR[:, 1536 + jp * 256:1792 + jp * 256],
                        TGR[:, 1024 + jp * 256:1280 + jp * 256], AF.Copy, bias=MAGIC), "SC")
                    W("vector", "SC", s_t2[j])
                    v_rnd[j] = O("vector", lambda e, j=j, jp=jp: e.tensor_scalar(
                        abf(A_GUQ + j * 256, 256), TGR[:, 1536 + jp * 256:1792 + jp * 256],
                        -MAGIC, None, ALU.add), "VE")
                te_dj = {}
                for j in range(32):
                    sd = fst["wdj"] % 3
                    fst["wdj"] += 1
                    qn = "sync" if sd == 0 else "gpsimd"
                    sn = "LM0" if sd == 0 else f"GM{sd - 1}"
                    W(qn, "TE", fst["te_wd"].get(fst["wdj"] - 1 - 3,
                                                 fst["te_gu"] if j < 3 and hf == 0 else
                                                 (fst["te_dlast"] if j < 3 else None)))
                    lw = dma(qn, AF8[:, 4096 + sd * 1024:4096 + (sd + 1) * 1024],
                             tdd[:, j * 1024:(j + 1) * 1024], sn)
                    W("tensor", sn, lw)
                    W("tensor", "VE", v_rnd[j])
                    if j == 0:
                        W("tensor", "SC", fst["s_down"])
                    wb = 4096 + sd * 1024
                    O("tensor", lambda e, j=j, wb=wb: e.matmul(
                        PA[:, 0:512], abf(A_GUQ + j * 256, 128),
                        AF8[:, wb:wb + 512], start=(j == 0), stop=(j == 31)))
                    O("tensor", lambda e, j=j, wb=wb: e.matmul(
                        PA[:, 512:1024], abf(A_GUQ + j * 256, 128),
                        AF8[:, wb + 512:wb + 1024], start=(j == 0), stop=(j == 31)))
                    O("tensor", lambda e, j=j, wb=wb: e.matmul(
                        P[0][:, :], abf(A_GUQ + j * 256 + 128, 128),
                        AF8[:, wb:wb + 512], start=(j == 0), stop=(j == 31)))
                    te_dj[j] = O("tensor", lambda e, j=j, wb=wb: e.matmul(
                        P[1][:, :], abf(A_GUQ + j * 256 + 128, 128),
                        AF8[:, wb + 512:wb + 1024], start=(j == 0), stop=(j == 31)), "TE")
                    fst["te_wd"][fst["wdj"] - 1] = te_dj[j]
                fst["te_dlast"] = te_dj[31]
                W("scalar", "TE", te_dj[31])
                W("scalar", "VE", v_s2bc)
                v_x = 0
                for tt2 in range(2):
                    tt = hf * 2 + tt2
                    t = blk * 4 + tt
                    if tt2 == 1:
                        W("scalar", "VE", v_x)
                    if tt2 == 0:
                        O("scalar", lambda e, tt=tt: e.activation(
                            TGR[:, 1024:1536], PA[:, 0:512], AF.Identity,
                            scale=stc(SC_S2 + tt)[:, 0:1]))
                        s_e = O("scalar", lambda e, tt=tt: e.activation(
                            TGR[:, 1536:2048], PA[:, 512:1024], AF.Identity,
                            scale=stc(SC_S2 + tt)[:, 0:1]), "SC")
                    else:
                        O("scalar", lambda e, tt=tt: e.activation(
                            TGR[:, 1024:1536], P[0][:, :], AF.Identity,
                            scale=stc(SC_S2 + tt)[:, 0:1]))
                        s_e = O("scalar", lambda e, tt=tt: e.activation(
                            TGR[:, 1536:2048], P[1][:, :], AF.Identity,
                            scale=stc(SC_S2 + tt)[:, 0:1]), "SC")
                    fst["s_down"] = s_e
                    W("vector", "SC", s_e)
                    v_x = O("vector", lambda e, t=t: e.tensor_tensor(
                        X[:, t * 1024:(t + 1) * 1024], X[:, t * 1024:(t + 1) * 1024],
                        TGR[:, 1024:2048], ALU.add), "VE")
                    vdrain()
            fst["te_blk"] = fst["te_dlast"]
        return v_x

    W("vector", "S20", C["S20"])
    W("vector", "S21", C["S21"])
    ln_prepass(SC_RSTD1, SC_NMR1)
    v_last = ffn(SC_RSTD1, SC_NMR1)

    if stop_after == "ffn1":
        W("scalar", "VE", C["VE"])
        for t in range(NT):
            dma("scalar", outd[128 * t:128 * t + 128, :], X[:, t * 1024:(t + 1) * 1024], "S20")
        W("sync", "S20", C["S20"])
        W("gpsimd", "S20", C["S20"])
        prog.emit_block()
        stack.close()
        return nc

    ln_prepass(SC_RSTD2, SC_NMR2)

    W("sync", "TE", C["TE"])
    W("sync", "S20", C["S20"])
    W("sync", "S21", C["S21"])
    ld_wv = 0
    for ic in range(8):
        ld_wv = dma("sync", abf(A_GUQ + ic * 1024, 1024), wvbd[:, ic * 1024:(ic + 1) * 1024])

    st_kq = [0, 0]
    st_v = [0, 0]
    v_qrd = [0, 0]
    W("scalar", "VE", C["VE"])
    te_o = {}
    s_ev = {}
    v_vs = {}
    te_vlast = None
    for blk in range(NB):
        v_cp_last = 0
        W("vector", "S20", st_kq[0])
        W("vector", "S21", st_kq[1])
        for ti in range(4):
            t = blk * 4 + ti
            v_cp_last = xln_tile(t, ti, SC_RSTD2, SC_NMR2, False,
                                 entry_te=(te_vlast if ti == 0 else None))
        for i, oc in enumerate(list(range(8, 16)) + list(range(0, 8))):
            g = blk * 16 + i
            p = g % 2
            W("sync", "TE", te_o.get(g - 2))
            ld_w = dma("sync", abf(A_WG + p * 1024, 1024), wqkb[:, oc * 1024:(oc + 1) * 1024],
                       f"LW{p}")
            W("tensor", f"LW{p}", ld_w)
            if i == 0:
                W("tensor", "VE", v_cp_last)
                if blk == 0:
                    W("tensor", "LD", ld_wv)
                    W("tensor", "SC", fst["s_down"])
            W("tensor", "SC", s_ev.get(g - 2))
            for ic in range(8):
                te_og = O("tensor", lambda e, ic=ic, p=p: e.matmul(
                    P[p][:, :], abf(A_WG + p * 1024 + ic * 128, 128),
                    abf(A_XQT + ic * 512, 512), start=(ic == 0), stop=(ic == 7)),
                    "TE" if ic == 7 else None)
            te_o[g] = te_og
            W("scalar", "TE", te_o[g])
            W("scalar", f"S2{g % 2}", st_kq[g % 2])
            W("scalar", "VE", v_qrd[g % 2])
            sl = g % 2
            s_ev[g] = O("scalar", lambda e, p=p, oc=oc, sl=sl: e.activation(
                abf(stg(sl), 512), P[p][:, :], AF.Identity,
                bias=BQK[:, oc:oc + 1]), "SC")
            if oc >= 8:
                W("scalar", "SC", s_ev[g])
                st_kq[g % 2] = dma(
                    "scalar",
                    kkin[:, (oc - 8) * 2048 + blk * 512:(oc - 8) * 2048 + blk * 512 + 512],
                    abf(stg(sl), 512), f"S2{g % 2}")
            else:
                W("scalar", "SC", s_ev[g])
                st_kq[g % 2] = dma("scalar",
                            qtd[:, oc * 2048 + blk * 512:oc * 2048 + blk * 512 + 512],
                            abf(stg(sl), 512), f"S2{g % 2}")
        for tt in range(4):
            t = blk * 4 + tt
            g2 = blk * 4 + tt
            sl = g2 % 2
            W("vector", f"SV{sl}", st_v[sl])
            v_ms = O("vector", lambda e, sl=sl: e.memset(abf(A_VST + sl * 1040, 1040), 1.0), "VE")
            for oh in range(2):
                pbank = PA if g2 % 2 == 0 else PB
                pb = pbank[:, oh * 512:(oh + 1) * 512]
                W("tensor", "VE", v_vs.get((g2 - 2, oh)))
                if g2 == 0 and oh == 0:
                    W("tensor", "VE", v_ms)
                for ic in range(8):
                    te_v = O("tensor", lambda e, ic=ic, tt=tt, oh=oh, pb=pb: e.matmul(
                        pb, abf(A_XQT + ic * 512 + tt * 128, 128),
                        abf(A_GUQ + ic * 1024 + oh * 512, 512),
                        start=(ic == 0), stop=(ic == 7)), "TE" if ic == 7 else None)
                te_vlast = te_v
                W("vector", "TE", te_v)
                if oh == 0 and g2 > 0:
                    W("vector", "VE", v_ms)
                v_vs[(g2, oh)] = O("vector", lambda e, sl=sl, oh=oh, pb=pb: e.tensor_tensor(
                    abf(A_VST + sl * 1040, 1040).rearrange(
                        "p (h c) -> p h c", c=65)[:, 8 * oh:8 * (oh + 1), 0:64],
                    pb.rearrange("p (h c) -> p h c", c=64),
                    BB[:, oh * 512:(oh + 1) * 512].rearrange("p (h c) -> p h c", c=64),
                    ALU.add), "VE")
            W("scalar", "VE", v_vs[(g2, 1)])
            for pr in range(8):
                st_v[sl] = dma("scalar",
                            vvin[:, pr * 2080 + t * 130:pr * 2080 + t * 130 + 130],
                            abf(A_VST + sl * 1040 + pr * 130, 130), f"SV{sl}")

    W("gpsimd", "S20", C["S20"])
    W("gpsimd", "S21", C["S21"])
    W("gpsimd", "SV0", C["SV0"])
    W("gpsimd", "SV1", C["SV1"])
    O("gpsimd", lambda e: e.collective_compute(
        "AllGather", ALU.bypass, ins=[kvin[:, :]], outs=[kvout[:, :]],
        replica_groups=groups), "CC")

    te_pr = {}
    st_h = [0, 0]
    te_nc_prev = None
    s_exp_prev = None
    vev = {}
    ld_kt = {}
    ld_v = {}
    ld_qt = {}
    for pr in range(8):
        sl = pr % 2
        if pr == 0:
            W("sync", "CC", 1)
            W("sync", "TE", C["TE"])
        W("sync", "TE", te_pr.get(pr - 2))
        for h in range(2):
            dma("sync", abf(A_KT + sl * 4096 + h * 2048, 2048),
                kkout[h * 128:(h + 1) * 128, pr * 2048:(pr + 1) * 2048], f"LK{sl}")
            dma("sync", abf(A_V + sl * 4160 + h * 2080, 2080),
                vvout[h * 128:(h + 1) * 128, pr * 2080:(pr + 1) * 2080], f"LK{sl}")
        ld_v[pr] = C[f"LK{sl}"]
        ld_qt[pr] = dma("sync", abf(A_QT + sl * 2048, 2048), qtd[:, pr * 2048:(pr + 1) * 2048],
                        f"LK{sl}")
        W("vector", f"LK{sl}", ld_qt[pr])
        if te_nc_prev:
            W("vector", "TE", te_nc_prev)
        for h2 in range(2):
            O("vector", lambda e, sl=sl, h2=h2: e.tensor_tensor(
                GU[:, h2 * 2048:(h2 + 1) * 2048], abf(A_KT + sl * 4096 + h2 * 2048, 2048),
                abf(A_KT + sl * 4096 + h2 * 2048, 2048), ALU.mult))
        W("vector", f"LK{sl}", ld_qt[pr])
        v_ksq = O("vector", lambda e, sl=sl: e.tensor_tensor(
            GU[:, 4096:6144], abf(A_QT + sl * 2048, 2048),
            abf(A_QT + sl * 2048, 2048), ALU.mult), "VE")
        W("tensor", "VE", v_ksq)
        v_kacc = 0
        for i8 in range(12):
            te_c2 = O("tensor", lambda e, i8=i8: e.matmul(
                P6[0:2, 0:512], ONES2[:, :], GU[:, i8 * 512:(i8 + 1) * 512],
                start=True, stop=True), "TE")
            W("vector", "TE", te_c2)
            O("vector", lambda e: e.reduce_max(
                STAT[0:2, SC_TMP:SC_TMP + 1], P6[0:2, 0:512], axis=AX.X))
            vdrain()
            dstc = SC_KMX if i8 < 8 else SC_KMX + 1
            if i8 in (0, 8):
                O("vector", lambda e, dstc=dstc: e.tensor_copy(
                    STAT[0:2, dstc:dstc + 1], STAT[0:2, SC_TMP:SC_TMP + 1]))
            else:
                O("vector", lambda e, dstc=dstc: e.tensor_tensor(
                    STAT[0:2, dstc:dstc + 1], STAT[0:2, dstc:dstc + 1],
                    STAT[0:2, SC_TMP:SC_TMP + 1], ALU.max))
            v_kacc = O("vector", lambda e: e.drain(), "VE")
            if i8 < 11:
                W("tensor", "VE", v_kacc)
        if s_exp_prev:
            W("vector", "SC", s_exp_prev)
        O("vector", lambda e, pr=pr: e.tensor_tensor(
            STAT[0:2, SC_TMP:SC_TMP + 1], STAT[0:2, SC_KMX:SC_KMX + 1],
            STAT[0:2, SC_KMX + 1:SC_KMX + 2], ALU.add))
        vdrain()
        v_nc = O("vector", lambda e: e.tensor_scalar(
            STAT[0:2, SC_TMP + 1:SC_TMP + 2], STAT[0:2, SC_TMP:SC_TMP + 1],
            -1.0 / 64.0, 40.0, ALU.mult, ALU.add), "VE")
        W("gpsimd", "VE", v_nc)
        g_nc = dma("gpsimd", TGR[0:1, 1664:1666], STAT[0:2, SC_TMP + 1:SC_TMP + 2], "GD")
        W("vector", "GD", g_nc)
        O("vector", lambda e: e.tensor_reduce(
            TGR[0:1, 1666:1667], TGR[0:1, 1664:1666], AX.X, ALU.min))
        v_nc2 = O("vector", lambda e: e.drain(), "VE")
        W("tensor", "VE", v_nc2)
        te_nc = O("tensor", lambda e: e.matmul(
            P6[0:128, 0:1], ONES[0:1, 0:128], TGR[0:1, 1666:1667],
            start=True, stop=True), "TE")
        te_nc_prev = te_nc
        W("vector", "TE", te_nc)
        v_negc = O("vector", lambda e: e.tensor_copy(
            STAT[:, SC_NEGC:SC_NEGC + 1], P6[0:128, 0:1]), "VE")

        for qb in range(4):
            s_exp = {}
            te_av = {}
            def emit_av(kc):
                h, c16 = kc // 16, kc % 16
                sp = (kc % 2) * 1024
                if kc == 0:
                    prv = vev.get((pr, qb - 1)) if qb > 0 else vev.get((pr - 1, 3))
                    W("tensor", "VE", prv)
                W("tensor", "SC", s_exp[kc])
                O("tensor", lambda e, sl=sl, h=h, c16=c16, sp=sp, kc=kc: e.matmul(
                    P[0][0:65, :], abf(A_V + sl * 4160 + h * 2080 + c16 * 130, 65),
                    abf(A_PT + sp, 512), start=(kc == 0), stop=(kc == 31)))
                te_av[kc] = O("tensor", lambda e, sl=sl, h=h, c16=c16, sp=sp, kc=kc: e.matmul(
                    P[1][0:65, :], abf(A_V + sl * 4160 + h * 2080 + c16 * 130 + 65, 65),
                    abf(A_PT + sp + 512, 512), start=(kc == 0), stop=(kc == 31)), "TE")

            for kc in range(32):
                h, c16 = kc // 16, kc % 16
                sb = PA if kc % 2 == 0 else PB
                if kc == 0:
                    W("tensor", "VE", v_negc)
                    W("tensor", f"LK{sl}", ld_qt[pr])
                W("tensor", "SC", s_exp.get(kc - 2))
                O("tensor", lambda e, sl=sl, h=h, c16=c16, sb=sb, qb=qb: e.matmul(
                    sb[:, 0:512], abf(A_KT + sl * 4096 + h * 2048 + c16 * 128, 128, 0, 64),
                    abf(A_QT + sl * 2048 + qb * 512, 512, 0, 64),
                    start=True, stop=True, tile_position=(0, 0)))
                te_s = O("tensor", lambda e, sl=sl, h=h, c16=c16, sb=sb, qb=qb: e.matmul(
                    sb[:, 512:1024], abf(A_KT + sl * 4096 + h * 2048 + c16 * 128, 128, 64, 128),
                    abf(A_QT + sl * 2048 + qb * 512, 512, 64, 128),
                    start=True, stop=True, tile_position=(64, 0)), "TE")
                W("scalar", "TE", te_s)
                if kc >= 2:
                    W("scalar", "TE", te_av[kc - 2])
                sp = (kc % 2) * 1024
                s_exp[kc] = O("scalar", lambda e, sb=sb, sp=sp: e.activation(
                    abf(A_PT + sp, 1024), sb[:, 0:1024], AF.Exp,
                    scale=ISCALE, bias=stc(SC_NEGC)[:, 0:1]), "SC")
                if kc >= 1:
                    emit_av(kc - 1)
            emit_av(31)
            s_exp_prev = s_exp[31]
            v_last2 = 0
            for hh in range(2):
                if hh == 0:
                    W("vector", "TE", te_av[31])
                v_cx = O("vector", lambda e, hh=hh: e.tensor_copy(
                    CTX[0:65, :], P[hh][0:65, :]), "VE")
                W("gpsimd", "VE", v_cx)
                g_rs = dma("gpsimd", TGR[0:1, 1536:2048], CTX[64:65, 0:512], "GD")
                W("vector", "GD", g_rs)
                O("vector", lambda e: e.reciprocal(
                    TGR[0:1, 1536:2048], TGR[0:1, 1536:2048]))
                v_rc = O("vector", lambda e: e.drain(), "VE")
                W("tensor", "VE", v_rc)
                te_rp = O("tensor", lambda e: e.matmul(
                    P6[0:64, 0:512], ONES[0:1, 0:64], TGR[0:1, 1536:2048],
                    start=True, stop=True), "TE")
                W("vector", "TE", te_rp)
                W("vector", f"SH{hh}", st_h[hh])
                v_h = O("vector", lambda e, hh=hh: e.tensor_tensor(
                    abf(A_HST + hh * 512, 512, 0, 64), CTX[0:64, :], P6[0:64, 0:512],
                    ALU.mult), "VE")
                W("scalar", "VE", v_h)
                st_h[hh] = dma("scalar",
                    htd[64 * hh:64 * (hh + 1),
                        pr * 2048 + qb * 512:pr * 2048 + qb * 512 + 512],
                    abf(A_HST + hh * 512, 512, 0, 64), f"SH{hh}")
                vdrain()
                v_last2 = v_h
            vev[(pr, qb)] = v_last2
        te_pr[pr] = C["TE"]

    W("sync", "TE", C["TE"])
    ld_wo = 0
    for ic in range(8):
        ld_wo = dma("sync", abf(A_GUQ + ic * 1024, 1024), wobd[:, ic * 1024:(ic + 1) * 1024])
    te_op = {}
    s_op = {}
    v_xo = {}
    for t in range(NT):
        p = t % 2
        W("sync", "TE", te_op.get(t - 2))
        if t == 0:
            W("sync", "SH0", C["SH0"])
            W("sync", "SH1", C["SH1"])
        ld_h = 0
        for fc in range(8):
            ld_h = dma("sync", abf(A_QT + p * 1024 + fc * 128, 128),
                       htd[:, fc * 2048 + t * 128:fc * 2048 + t * 128 + 128], f"LK{p}")
        W("tensor", f"LK{p}", ld_h)
        if t == 0:
            W("tensor", "LD", ld_wo)
        if t == 0:
            W("tensor", "SC", C["SC"])
            W("tensor", "VE", C["VE"])
        W("tensor", "VE", v_xo.get(t - 2))
        pbank = PA if p == 0 else PB
        ob0 = pbank[:, 0:512]
        ob1 = pbank[:, 512:1024]
        for fc in range(8):
            O("tensor", lambda e, fc=fc, p=p, ob0=ob0: e.matmul(
                ob0, abf(A_QT + p * 1024 + fc * 128, 128),
                abf(A_GUQ + fc * 1024, 512), start=(fc == 0), stop=(fc == 7)))
            te_ot = O("tensor", lambda e, fc=fc, p=p, ob1=ob1: e.matmul(
                ob1, abf(A_QT + p * 1024 + fc * 128, 128),
                abf(A_GUQ + fc * 1024 + 512, 512), start=(fc == 0), stop=(fc == 7)),
                "TE" if fc == 7 else None)
        te_op[t] = te_ot
        W("scalar", "TE", te_op[t])
        W("scalar", "SC", s_op.get(t - 1, 0))
        W("scalar", "VE", v_xo.get(t - 1))
        s_a = O("scalar", lambda e, t=t: e.activation(
            XLN[:, 0:1024], X[:, t * 1024:(t + 1) * 1024], AF.Identity,
            bias=stc(SC_NMR2 + t)[:, 0:1], scale=stc(SC_RSTD2 + t)[:, 0:1]), "SC")
        s_op[t] = s_a
        W("vector", "SC", s_a)
        for oh in range(2):
            ob = (PA if p == 0 else PB)[:, oh * 512:(oh + 1) * 512]
            O("vector", lambda e, oh=oh, ob=ob: e.tensor_tensor(
                TGR[:, oh * 512:(oh + 1) * 512], ob,
                BB[:, 1024 + oh * 512:1536 + oh * 512], ALU.add))
        vdrain()
        O("vector", lambda e: e.tensor_tensor(
            TGR[:, 1024:2048], TGR[:, 0:1024], XLN[:, 0:1024], ALU.add))
        vdrain()
        v_xo[t] = O("vector", lambda e, t=t: e.tensor_tensor(
            X[:, t * 1024:(t + 1) * 1024], X[:, t * 1024:(t + 1) * 1024],
            TGR[:, 1024:2048], ALU.add), "VE")
        vdrain()
    fst["te_blk"] = te_op[NT - 1]
    fst["s_down"] = s_op[NT - 1]

    if stop_after != "attn":
        ln_prepass(SC_RSTD1, SC_NMR1)
        ffn(SC_RSTD1, SC_NMR1)

    W("scalar", "VE", C["VE"])
    for t in range(NT):
        dma("scalar", outd[128 * t:128 * t + 128, :], X[:, t * 1024:(t + 1) * 1024], "S20")
    for sn in ("S20", "S21", "SV0", "SV1", "SH0", "SH1", "GS0", "GS1"):
        W("sync", sn, C[sn])
        W("gpsimd", sn, C[sn])
    prog.emit_block()
    stack.close()
    return nc


def _pm(w, nj, nic):
    return np.ascontiguousarray(
        w.reshape(nj, 128, nic, 128).transpose(3, 0, 2, 1).reshape(128, nj * nic * 128))


def make_inputs(L0_attn_x, L0_attn_w_qkv, L0_attn_b_qkv, L0_attn_w_o, L0_attn_b_o,
                L0_ffn_w_gate, L0_ffn_w_up, L0_ffn_w_down, n_cores=8):
    x = np.ascontiguousarray(np.asarray(L0_attn_x, dtype=np.float32))
    w_qkv = np.asarray(L0_attn_w_qkv, dtype=np.float32)
    b_qkv = np.asarray(L0_attn_b_qkv, dtype=np.float32)
    w_o = np.asarray(L0_attn_w_o, dtype=np.float32)
    b_o = np.asarray(L0_attn_b_o, dtype=np.float32)
    w_gate = np.asarray(L0_ffn_w_gate, dtype=np.float32)
    w_up = np.asarray(L0_ffn_w_up, dtype=np.float32)
    w_down = np.asarray(L0_ffn_w_down, dtype=np.float32)

    shared = {
        "wgpm": _pm(w_gate, 32, 8),
        "wupm": _pm(w_up, 32, 8),
        "wdpm": np.ascontiguousarray(
            w_down.T.reshape(32, 128, 1024).transpose(1, 0, 2).reshape(128, 32 * 1024)),
        "wqkpm": _pm(w_qkv[0:2048], 16, 8),
        "wvpm": np.ascontiguousarray(
            w_qkv[2048:3072].T.reshape(8, 128, 1024).transpose(1, 0, 2).reshape(128, 8 * 1024)),
        "wopm": np.ascontiguousarray(
            w_o.T.reshape(8, 128, 1024).transpose(1, 0, 2).reshape(128, 8 * 1024)),
        "bqkc": np.ascontiguousarray(b_qkv[0:2048].reshape(16, 128).T),
        "bvr": np.ascontiguousarray(b_qkv[2048:3072].reshape(1, 1024)),
        "bor": np.ascontiguousarray(b_o.reshape(1, 1024)),
        "ident": np.eye(128, dtype=ml_dtypes.bfloat16),
        "identf": np.eye(128, dtype=np.float32),
    }
    in_maps = []
    for c in range(n_cores):
        b, h = c // 2, c % 2
        m = dict(shared)
        m["x"] = np.ascontiguousarray(x[b, 2048 * h:2048 * h + 2048])
        in_maps.append(m)
    return in_maps


def kernel(L0_attn_x, L0_attn_w_qkv, L0_attn_b_qkv, L0_attn_w_o, L0_attn_b_o,
           L0_ffn_w_gate, L0_ffn_w_up, L0_ffn_w_down):
    in_maps = make_inputs(L0_attn_x, L0_attn_w_qkv, L0_attn_b_qkv, L0_attn_w_o,
                          L0_attn_b_o, L0_ffn_w_gate, L0_ffn_w_up, L0_ffn_w_down)
    nc = bass.Bass("TRN2", target_bir_lowering=False)
    build(nc)
    res = run_bass_kernel_spmd(nc, in_maps, core_ids=list(range(8)))
    out = np.empty((4, 4096, 1024), dtype=np.float32)
    for c in range(8):
        b, h = c // 2, c % 2
        out[b, 2048 * h:2048 * h + 2048] = res.results[c]["out"]
    return out


# version 11
# speedup vs baseline: 1.4365x; 1.0116x over previous
import os
import contextlib

import numpy as np
import ml_dtypes
import concourse.bass as bass
import concourse.mybir as mybir
from concourse.bass_utils import run_bass_kernel_spmd

F32 = mybir.dt.float32
BF16 = mybir.dt.bfloat16
FP8 = mybir.dt.float8e4
AF = mybir.ActivationFunctionType
ALU = mybir.AluOpType
AX = mybir.AxisListType

MAGIC = float(np.float32(1.5 * 2.0 ** 52)) if os.environ.get("BASS_SIM_FP64") == "1" \
    else float(np.float32(1.5 * 2.0 ** 23))
D = 1024
T = 2048
NT = 16
NB = 4
EPS = 1e-5
ISCALE = 1.0 / 32.0

A_XQT = 0
A_WG = 4096
A_WU = 6144
A_GUQ = 8192
A_VST = 16384
A_STG = 22528
A_KT = 0
A_V = 16384
A_V = 8192
A_QT = 16640
A_PT = 20736
A_HST = 22784

SC_RSTD1, SC_NMR1 = 0, 16
SC_RSTD2, SC_NMR2 = 32, 48
SC_SX, SC_INVSX = 64, 80
SC_S2 = 96
SC_INVS2 = 100
SC_WSC = 104
SC_TMP = 112
SC_QMX = 128
SC_KMX = 144
SC_NEGC = 148


class Prog:
    def __init__(self, nc, stack):
        self.nc = nc
        self.streams = {k: [] for k in ("sync", "tensor", "scalar", "vector", "gpsimd")}
        self.sems = {}
        self.cnt = {}
        self._stack = stack

    def sem(self, name):
        self.sems[name] = self._stack.enter_context(self.nc.semaphore(name=name))
        self.cnt[name] = 0

    def wait(self, eng, sname, val):
        if val is None or val <= 0:
            return
        h = self.sems[sname]
        self.streams[eng].append(lambda e, h=h, v=val: e.wait_ge(h, v))

    def op(self, eng, build, sname=None, n=1):
        if sname is None:
            self.streams[eng].append(build)
            return None
        h = self.sems[sname]
        self.cnt[sname] += n
        v = self.cnt[sname]

        def f(e, build=build, h=h, n=n):
            build(e).then_inc(h, n)

        self.streams[eng].append(f)
        return v

    def emit_block(self):
        with self.nc.Block() as block:
            @block.sync
            def _(e):
                for f in self.streams["sync"]:
                    f(e)

            @block.tensor
            def _(e):
                for f in self.streams["tensor"]:
                    f(e)

            @block.scalar
            def _(e):
                for f in self.streams["scalar"]:
                    f(e)

            @block.vector
            def _(e):
                for f in self.streams["vector"]:
                    f(e)

            @block.gpsimd
            def _(e):
                for f in self.streams["gpsimd"]:
                    f(e)


def build(nc, groups=None, stop_after=None):
    if groups is None:
        groups = [[0, 1], [2, 3], [4, 5], [6, 7]]
    dt = nc.dram_tensor
    xd = dt("x", [T, D], F32, kind="ExternalInput").ap()
    wgpm = dt("wgpm", [128, 32 * 1024], F32, kind="ExternalInput").ap()
    wupm = dt("wupm", [128, 32 * 1024], F32, kind="ExternalInput").ap()
    wdpm = dt("wdpm", [128, 32 * 1024], F32, kind="ExternalInput").ap()
    wqkpm = dt("wqkpm", [128, 16 * 1024], F32, kind="ExternalInput").ap()
    wvpm = dt("wvpm", [128, 8 * 1024], F32, kind="ExternalInput").ap()
    wopm = dt("wopm", [128, 8 * 1024], F32, kind="ExternalInput").ap()
    bqkc = dt("bqkc", [128, 16], F32, kind="ExternalInput").ap()
    bvr = dt("bvr", [1, D], F32, kind="ExternalInput").ap()
    bord = dt("bor", [1, D], F32, kind="ExternalInput").ap()
    identd = dt("ident", [128, 128], BF16, kind="ExternalInput").ap()
    identf = dt("identf", [128, 128], F32, kind="ExternalInput").ap()
    outd = dt("out", [T, D], F32, kind="ExternalOutput").ap()

    tgd = dt("tgs", [128, 32 * 1024], FP8, kind="Internal").ap()
    tud = dt("tus", [128, 32 * 1024], FP8, kind="Internal").ap()
    tdd = dt("tds", [128, 32 * 1024], FP8, kind="Internal").ap()
    wqkb = dt("wqkbs", [128, 16 * 1024], BF16, kind="Internal").ap()
    wvbd = dt("wvbs", [128, 8 * 1024], BF16, kind="Internal").ap()
    wobd = dt("wobs", [128, 8 * 1024], BF16, kind="Internal").ap()
    qtd = dt("qts", [128, 8 * 2048], BF16, kind="Internal").ap()
    htd = dt("hts", [128, 8 * 2048], BF16, kind="Internal").ap()
    kvin = dt("kvin", [128, 8 * 2048 + 8 * 2080], BF16, kind="Internal").ap()
    kvout = dt("kvout", [256, 8 * 2048 + 8 * 2080], BF16, kind="Internal").ap()
    kkin = kvin[:, 0:8 * 2048]
    vvin = kvin[:, 8 * 2048:]
    kkout = kvout[:, 0:8 * 2048]
    vvout = kvout[:, 8 * 2048:]

    stack = contextlib.ExitStack()
    sbuf = lambda name, shape, dty: stack.enter_context(nc.sbuf_tensor(name, shape, dty))
    pst = lambda name, shape, dty=F32: stack.enter_context(nc.psum_tensor(name, shape, dty))

    X = sbuf("X", [128, NT * 1024], F32)
    GU = sbuf("GU", [128, 32 * 512], F32)
    XLN = sbuf("XLN", [128, 1024], F32)
    TGR = sbuf("TGR", [128, 2048], F32)
    CTX = sbuf("CTX", [128, 512], F32)
    MNA = sbuf("MNA", [128, 512], F32)
    STAT = sbuf("STAT", [128, 160], F32)
    IBC = sbuf("IBC", [128, 1024], F32)
    ABF = sbuf("ABF", [128, 23808], BF16)
    AF8 = sbuf("AF8", [128, 8192], FP8)
    BB = sbuf("BB", [128, 1024], BF16)
    IDT = sbuf("IDT", [128, 128], BF16)
    IDF = sbuf("IDF", [128, 128], F32)
    BQK = sbuf("BQK", [128, 16], F32)
    ONES = sbuf("ONES", [128, 128], F32)
    ONES2 = sbuf("ONES2", [128, 2], F32)

    P = [pst(f"P{i}", [128, 512]) for i in range(2)]
    PA = pst("PA", [128, 1024])
    PB = pst("PB", [128, 1024])
    P6 = pst("P6", [128, 512])
    PT8 = pst("PT8", [128, 1024], BF16)

    prog = Prog(nc, stack)
    for s in ("LD", "LW0", "LW1", "LK0", "LK1", "GW0", "GW1", "GM0", "GM1", "LM0", "LB", "LC0", "LC1", "GD", "GS0", "GS1",
              "S20", "S21", "SV0", "SV1", "SH0", "SH1", "TE", "SC", "VE", "CC"):
        prog.sem(s)
    W, O, C = prog.wait, prog.op, prog.cnt

    def dma(eng, out, in_, sname="LD", **kw):
        return O(eng, lambda e, o=out, i=in_, kw=kw: e.dma_start(out=o, in_=i, **kw),
                 sname, 16)

    def dma2(eng, out, in_, sname, n):
        h = n // 2
        O(eng, lambda e, o=out, i=in_, h=h: e.dma_start(out=o[:, 0:h], in_=i[:, 0:h]),
          sname, 16)
        return O(eng, lambda e, o=out, i=in_, h=h, n=n: e.dma_start(
            out=o[:, h:n], in_=i[:, h:n]), sname, 16)

    def vdrain():
        O("vector", lambda e: e.drain())

    def stc(c):
        return STAT[:, c:c + 1]

    def abf(lo, n, p0=0, p1=128):
        return ABF[p0:p1, lo:lo + n]

    def stg(sl):
        return A_STG if sl == 0 else A_WU

    dma("sync", IDT[:], identd[:])
    dma("sync", IDF[:], identf[:])
    for t in range(NT):
        dma("sync", X[:, t * 1024:(t + 1) * 1024], xd[128 * t:128 * t + 128, :])
    dma("sync", BQK[:], bqkc[:])
    O("vector", lambda e: e.memset(ONES2[:, :], 0.0))
    vdrain()
    O("vector", lambda e: e.memset(ONES2[0:64, 0:1], 1.0))
    vdrain()
    O("vector", lambda e: e.memset(ONES2[64:128, 1:2], 1.0))
    vdrain()
    v_ones = O("vector", lambda e: e.memset(ONES[:], 1.0), "VE")

    def bias_bcast(row_dram, ve_guard, te_guard=None):
        lb = dma("sync", TGR[0:1, 0:1024], row_dram, "LB")
        W("tensor", "LB", lb)
        W("tensor", "VE", ve_guard)
        if te_guard:
            W("tensor", "SC", te_guard)
        v_b = 0
        for h in range(2):
            if h == 1:
                W("tensor", "VE", v_b)
            te_b = O("tensor", lambda e, h=h: e.matmul(
                P[0][:, :], ONES[0:1, 0:128], TGR[0:1, 512 * h:512 * (h + 1)],
                start=True, stop=True), "TE")
            W("vector", "TE", te_b)
            v_b = O("vector", lambda e, h=h: e.tensor_copy(
                BB[:, 512 * h:512 * (h + 1)], P[0][:, :]), "VE")
        return v_b

    v_bb = bias_bcast(bvr[:], v_ones)
    ve_bias = C["VE"]

    NCOL = 128 * 32 * 1024

    def wprep(src, dst, wsc_col, first_wait, sc_first=0):
        v_red = [0] * 8
        for i in range(8):
            p = i % 2
            if i < 2:
                W("sync", "VE", first_wait)
                W("sync", "SC", sc_first)
            else:
                W("sync", "VE", v_red[i - 2])
            lv = dma2("sync", GU[:, p * 4096:(p + 1) * 4096],
                      src[:, i * 4096:(i + 1) * 4096], f"LW{p}", 4096)
            W("vector", f"LW{p}", lv)
            v_red[i] = O("vector", lambda e, i=i, p=p: e.reduce_sum(
                stc(SC_TMP + i), GU[:, p * 4096:(p + 1) * 4096], axis=AX.X,
                apply_absolute_value=True), "VE")
        vdrain()
        O("vector", lambda e: e.reduce_sum(
            stc(SC_TMP + 8), STAT[:, SC_TMP:SC_TMP + 8], axis=AX.X))
        v_s = O("vector", lambda e: e.drain(), "VE")
        W("tensor", "VE", v_s)
        te_s = O("tensor", lambda e: e.matmul(
            P6[0:1, 0:1], stc(SC_TMP + 8), ONES[:, 0:1], start=True, stop=True), "TE")
        W("vector", "TE", te_s)
        O("vector", lambda e: e.tensor_scalar(
            stc(SC_TMP + 9)[0:1, :], P6[0:1, 0:1], 1.0 / NCOL, 1e-5, ALU.mult, ALU.max))
        vdrain()
        v_r = O("vector", lambda e: e.reciprocal(
            stc(SC_TMP + 10)[0:1, :], stc(SC_TMP + 9)[0:1, :]), "VE")
        W("tensor", "VE", v_r)
        O("tensor", lambda e: e.matmul(
            P6[0:128, 0:1], ONES[0:1, 0:128], stc(SC_TMP + 10)[0:1, :],
            start=True, stop=True))
        te_b2 = O("tensor", lambda e: e.matmul(
            P6[0:128, 1:2], ONES[0:1, 0:128], stc(SC_TMP + 9)[0:1, :],
            start=True, stop=True), "TE")
        W("vector", "TE", te_b2)
        O("vector", lambda e: e.tensor_copy(stc(SC_TMP + 11), P6[0:128, 0:1]))
        v_bc = O("vector", lambda e, c=wsc_col: e.tensor_copy(stc(c), P6[0:128, 1:2]), "VE")
        st_prev = [C["GS0"], C["GS1"]]
        s_done = [0] * 8
        v_done = [0] * 8
        for i in range(8):
            p = i % 2
            W("scalar", "VE", v_done[i - 2] if i >= 2 else v_bc)
            lv = dma2("scalar", GU[:, p * 4096:(p + 1) * 4096],
                      src[:, i * 4096:(i + 1) * 4096], f"LC{p}", 4096)
            W("scalar", f"LC{p}", lv)
            W("scalar", "VE", v_done[i - 1] if i >= 1 else v_bc)
            s_done[i] = O("scalar", lambda e, p=p: e.activation(
                GU[:, 8192 + p * 4096:8192 + (p + 1) * 4096],
                GU[:, p * 4096:(p + 1) * 4096],
                AF.Copy, bias=MAGIC, scale=stc(SC_TMP + 11)[:, 0:1]), "SC")
            W("vector", "SC", s_done[i])
            W("vector", f"GS{p}", st_prev[p])
            O("vector", lambda e, p=p: e.tensor_scalar(
                abf(p * 4096, 4096), GU[:, 8192 + p * 4096:8192 + (p + 1) * 4096],
                -MAGIC, 1.0, ALU.add, ALU.min))
            vdrain()
            v_done[i] = O("vector", lambda e, p=p: e.tensor_scalar(
                AF8[:, p * 4096:(p + 1) * 4096], abf(p * 4096, 4096),
                -1.0, None, ALU.max), "VE")
            W("scalar", "VE", v_done[i])
            st_prev[p] = dma("scalar", dst[:, i * 4096:(i + 1) * 4096],
                             AF8[:, p * 4096:(p + 1) * 4096], f"GS{p}")
        return v_done[7], s_done[7]

    v_p, s_p = wprep(wgpm, tgd, SC_WSC + 0, ve_bias)
    v_p, s_p = wprep(wupm, tud, SC_WSC + 1, 0, sc_first=s_p)
    v_p, s_p = wprep(wdpm, tdd, SC_WSC + 2, 0, sc_first=s_p)
    vdrain()
    O("vector", lambda e: e.tensor_scalar(
        stc(SC_WSC + 2), stc(SC_WSC + 2), 0.5, None, ALU.mult), "VE")

    def wcast(src, dst, ncols, v_first, sc_first=0):
        v_prev = [0, 0]
        st_prev = [C["S20"], C["S21"]]
        W("vector", "GS0", C["GS0"])
        W("vector", "GS1", C["GS1"])
        for i in range(ncols // 4096):
            p = i % 2
            if v_prev[p]:
                W("sync", "VE", v_prev[p])
            else:
                W("sync", "VE", v_first)
                W("sync", "SC", sc_first)
            lv = dma2("sync", GU[:, p * 4096:(p + 1) * 4096],
                      src[:, i * 4096:(i + 1) * 4096], f"LW{p}", 4096)
            W("vector", f"LW{p}", lv)
            W("vector", f"S2{p}", st_prev[p])
            v_prev[p] = O("vector", lambda e, p=p: e.tensor_copy(
                abf(p * 4096, 4096), GU[:, p * 4096:(p + 1) * 4096]), "VE")
            W("scalar", "VE", v_prev[p])
            st_prev[p] = dma("scalar", dst[:, i * 4096:(i + 1) * 4096],
                             abf(p * 4096, 4096), f"S2{p}")
        return v_prev[(ncols // 4096 - 1) % 2]

    v_c = wcast(wqkpm, wqkb, 16 * 1024, 0, sc_first=s_p)
    v_c = wcast(wvpm, wvbd, 8 * 1024, v_c)
    v_c = wcast(wopm, wobd, 8 * 1024, v_c)

    def ln_prepass(rstd_base, nmr_base):
        W("vector", "LD", C["LD"])
        W("scalar", "LD", C["LD"])
        W("scalar", "VE", C["VE"])
        W("scalar", "LW0", C["LW0"])
        W("scalar", "LW1", C["LW1"])
        for t in range(NT):
            O("vector", lambda e, t=t: e.reduce_sum(
                stc(SC_TMP + t), X[:, t * 1024:(t + 1) * 1024], axis=AX.X))
        v0 = O("vector", lambda e: e.drain(), "VE")
        s0 = 0
        s_hist = {}
        for t in range(NT):
            W("scalar", "SC", s_hist.get(t - 4, 0))
            s_hist[t] = O("scalar", lambda e, t=t: e.activation(
                GU[:, (t % 4) * 1024:(t % 4) * 1024 + 1024],
                X[:, t * 1024:(t + 1) * 1024], AF.Square,
                accum_out=stc(nmr_base + t)), "SC")
        s0 = s_hist[NT - 1]
        W("vector", "SC", s0)
        TM = STAT[:, SC_TMP:SC_TMP + 16]
        SQ = STAT[:, nmr_base:nmr_base + 16]
        MM = STAT[:, SC_SX:SC_SX + 16]
        O("vector", lambda e, TM=TM: e.tensor_scalar(TM, TM, 1.0 / D, None, ALU.mult))
        O("vector", lambda e, SQ=SQ: e.tensor_scalar(SQ, SQ, 1.0 / D, None, ALU.mult))
        vdrain()
        O("vector", lambda e, TM=TM, MM=MM: e.tensor_tensor(MM, TM, TM, ALU.mult))
        vdrain()
        O("vector", lambda e, SQ=SQ, MM=MM: e.tensor_tensor(SQ, SQ, MM, ALU.subtract))
        vdrain()
        O("vector", lambda e, SQ=SQ: e.tensor_scalar(SQ, SQ, EPS, None, ALU.add))
        vdrain()
        v1 = O("vector", lambda e, SQ=SQ, MM=MM: e.reciprocal(MM, SQ), "VE")
        W("scalar", "VE", v1)
        s1 = O("scalar", lambda e, MM=MM, rb=rstd_base: e.activation(
            STAT[:, rb:rb + 16], MM, AF.Sqrt), "SC")
        W("vector", "SC", s1)
        O("vector", lambda e, TM=TM, rb=rstd_base, nb=nmr_base: e.scalar_tensor_tensor(
            STAT[:, nb:nb + 16], TM, -1.0, STAT[:, rb:rb + 16], ALU.mult, ALU.mult), "VE")
        vdrain()

    st8 = {"seq": 0, "v_xl": {}, "te_tp": {}, "v_cp": {}, "v_tg": {}}

    def xln_tile(t, ti, rstd_base, nmr_base, quant, entry_te=None):
        k = st8["seq"]
        st8["seq"] += 1
        sl = k % 2
        xt = X[:, t * 1024:(t + 1) * 1024]
        xl = XLN[:, 0:1024]
        W("scalar", "VE", st8["v_xl"].get(k - 1))
        W("scalar", "SC", st8.get("s_last", 0))
        s0 = O("scalar", lambda e, xt=xt, xl=xl, t=t: e.activation(
            xl, xt, AF.Identity, bias=stc(nmr_base + t)[:, 0:1],
            scale=stc(rstd_base + t)[:, 0:1]), "SC")
        W("vector", "SC", s0)
        if entry_te:
            W("vector", "TE", entry_te)
        if quant:
            O("vector", lambda e, xl=xl: e.reduce_max(
                stc(SC_TMP + 0), xl, axis=AX.X, apply_absolute_value=True))
            vdrain()
            O("vector", lambda e: e.tensor_scalar(
                stc(SC_TMP + 1), stc(SC_TMP + 0), 1e-5, None, ALU.max))
            vdrain()
            O("vector", lambda e, t=t: e.tensor_scalar(
                stc(SC_INVSX + t % 16), stc(SC_TMP + 1), 1.0 / 127.0, None, ALU.mult))
            O("vector", lambda e: e.reciprocal(stc(SC_TMP + 2), stc(SC_TMP + 1)))
            vdrain()
            v0 = O("vector", lambda e, t=t: e.tensor_scalar(
                stc(SC_SX + t % 16), stc(SC_TMP + 2), 127.0, None, ALU.mult), "VE")
            st8["v_xl"][k] = v0
            W("scalar", "VE", v0)
            W("scalar", "VE", st8["v_tg"].get(k - 1))
            s1 = O("scalar", lambda e, xl=xl, t=t: e.activation(
                TGR[:, 0:1024], xl, AF.Copy, bias=MAGIC,
                scale=stc(SC_SX + t % 16)[:, 0:1]), "SC")
            st8["s_last"] = s1
            W("vector", "SC", s1)
            W("vector", "TE", st8["te_tp"].get(k - 2))
            v1 = O("vector", lambda e, sl=sl: e.tensor_scalar(
                abf(stg(sl), 1024), TGR[:, 0:1024], -MAGIC, None, ALU.add), "VE")
            st8["v_tg"][k] = v1
        else:
            W("vector", "TE", st8["te_tp"].get(k - 2))
            v1 = O("vector", lambda e, xl=xl, sl=sl: e.tensor_copy(
                abf(stg(sl), 1024), xl), "VE")
            st8["v_xl"][k] = v1
        W("tensor", "VE", v1)
        W("tensor", "VE", st8["v_cp"].get(k - 1))
        for fc in range(8):
            te_x = O("tensor", lambda e, fc=fc, sl=sl: e.transpose(
                PT8[:, fc * 128:(fc + 1) * 128],
                abf(stg(sl) + fc * 128, 128), IDT[:]),
                "TE" if fc == 7 else None)
        st8["te_tp"][k] = te_x
        W("vector", "TE", st8["te_tp"][k])
        v_c = 0
        for fc in range(8):
            v_c = O("vector", lambda e, fc=fc, ti=ti: e.tensor_copy(
                abf(A_XQT + fc * 512 + ti * 128, 128), PT8[:, fc * 128:(fc + 1) * 128]),
                "VE" if fc == 7 else None)
        st8["v_cp"][k] = v_c
        return v_c

    def bcast4(col0, ibc_lo, psum_guard=0):
        vdrain()
        v_a = O("vector", lambda e, c=col0: e.tensor_copy(
            TGR[:, 1024:1028], STAT[:, c:c + 4]), "VE")
        W("tensor", "VE", v_a)
        W("tensor", "SC", psum_guard)
        for q in range(4):
            te_a = O("tensor", lambda e, q=q: e.transpose(
                P6[:, q * 128:(q + 1) * 128], TGR[:, 1024 + q:1024 + q + 128], IDF[:]),
                "TE" if q == 3 else None)
        W("vector", "TE", te_a)
        for q in range(4):
            O("vector", lambda e, q=q: e.tensor_copy(
                TGR[0:1, 1152 + q * 128:1152 + (q + 1) * 128],
                P6[0:1, q * 128:(q + 1) * 128]))
        v_b = O("vector", lambda e: e.drain(), "VE")
        W("tensor", "VE", v_b)
        te_c = O("tensor", lambda e: e.matmul(
            P6[:, 0:512], ONES[0:1, 0:128], TGR[0:1, 1152:1664],
            start=True, stop=True), "TE")
        W("vector", "TE", te_c)
        return O("vector", lambda e, lo=ibc_lo: e.tensor_copy(
            IBC[:, lo:lo + 512], P6[:, 0:512]), "VE")

    fst = {"call": 0, "te_gu": None, "s_down": 0, "te_dlast": None, "te_blk": None,
           "wdj": 0, "te_wd": {}}

    def ffn(rstd_base, nmr_base):
        W("scalar", "VE", C["VE"])
        W("scalar", "LD", C["LD"])
        W("vector", "LD", C["LD"])
        W("sync", "TE", C["TE"])
        for sn in ("S20", "S21", "SV0", "SV1", "SH0", "SH1", "GS0", "GS1"):
            W("sync", sn, C[sn])
            W("vector", sn, C[sn])
        for blk in range(NB):
            v_cp_last = 0
            for ti in range(4):
                t = blk * 4 + ti
                v_cp_last = xln_tile(t, ti, rstd_base, nmr_base, True,
                                     entry_te=(fst["te_blk"] if ti == 0 else None))
            v_ibc = bcast4(SC_INVSX + (blk * 4) % 16, 0, psum_guard=fst["s_down"])
            te_j = {}
            v_pu = {}
            s_g = {}
            v_mm = {}
            for j in range(32):
                p = j % 2
                W("sync", "TE", te_j.get(j - 2, fst["te_dlast"] if j < 2 else None))
                ld_w = dma2("sync", AF8[:, p * 1024:(p + 1) * 1024],
                            tgd[:, j * 1024:(j + 1) * 1024], f"LW{p}", 1024)
                ld_u = dma2("sync", AF8[:, 2048 + p * 1024:2048 + (p + 1) * 1024],
                            tud[:, j * 1024:(j + 1) * 1024], f"GW{p}", 1024)
                gb = PA[:, 0:512] if p == 0 else PB[:, 0:512]
                ub = PA[:, 512:1024] if p == 0 else PB[:, 512:1024]
                W("tensor", f"LW{p}", ld_w)
                W("tensor", f"GW{p}", ld_u)
                if j == 0:
                    W("tensor", "VE", v_ibc)
                    W("tensor", "SC", fst["s_down"])
                W("tensor", "VE", v_pu.get(j - 2))
                for ic in range(8):
                    O("tensor", lambda e, ic=ic, p=p, gb=gb: e.matmul(
                        gb, AF8[:, p * 1024 + ic * 128:p * 1024 + ic * 128 + 128],
                        abf(A_XQT + ic * 512, 512), start=(ic == 0), stop=(ic == 7)))
                for ic in range(8):
                    te_jj = O("tensor", lambda e, ic=ic, p=p, ub=ub: e.matmul(
                        ub, AF8[:, 2048 + p * 1024 + ic * 128:2048 + p * 1024 + ic * 128 + 128],
                        abf(A_XQT + ic * 512, 512), start=(ic == 0), stop=(ic == 7)),
                        "TE" if ic == 7 else None)
                te_j[j] = te_jj
                W("vector", "TE", te_j[j])
                W("vector", "SC", s_g.get(j - 1))
                O("vector", lambda e, gb=gb: e.tensor_tensor(
                    TGR[:, 1024:1536], gb, IBC[:, 0:512], ALU.mult))
                O("vector", lambda e, ub=ub: e.tensor_tensor(
                    TGR[:, 0:512], ub, IBC[:, 0:512], ALU.mult))
                v_pu[j] = O("vector", lambda e: e.drain(), "VE")
                W("scalar", "VE", v_pu[j])
                s_g[j] = O("scalar", lambda e: e.activation(
                    TGR[:, 1536:2048], TGR[:, 1024:1536], AF.Silu,
                    scale=stc(SC_WSC + 0)[:, 0:1]), "SC")
                W("vector", "SC", s_g[j])
                v_gu = O("vector", lambda e, j=j: e.scalar_tensor_tensor(
                    GU[:, j * 512:(j + 1) * 512], TGR[:, 0:512],
                    stc(SC_WSC + 1)[:, 0:1], TGR[:, 1536:2048], ALU.mult, ALU.mult), "VE")
                W("scalar", "VE", v_gu)
                s_ab = O("scalar", lambda e, j=j, p=p: e.activation(
                    MNA[:, :] if p == 0 else CTX[:, :], GU[:, j * 512:(j + 1) * 512],
                    AF.Abs), "SC")
                W("vector", "SC", s_ab)
                if j == 0:
                    v_mm[j] = O("vector", lambda e: e.tensor_copy(
                        TGR[:, 512:1024], MNA[:, :]), "VE")
                else:
                    v_mm[j] = O("vector", lambda e, j=j, p=p: e.tensor_tensor(
                        TGR[:, 512:1024], TGR[:, 512:1024],
                        MNA[:, :] if p == 0 else CTX[:, :], ALU.max), "VE")
                vdrain()
            fst["te_gu"] = te_j[31]

            O("vector", lambda e: e.tensor_copy(CTX[:, :], TGR[:, 512:1024]))
            vdrain()
            v_mx = O("vector", lambda e: e.drain(), "VE")
            W("tensor", "VE", v_mx)
            W("tensor", "SC", fst["s_down"])
            for q in range(4):
                te_mx = O("tensor", lambda e, q=q: e.transpose(
                    P6[:, q * 128:(q + 1) * 128], CTX[:, q * 128:(q + 1) * 128], IDF[:]),
                    "TE" if q == 3 else None)
            W("vector", "TE", te_mx)
            for q in range(4):
                O("vector", lambda e, q=q: e.reduce_max(
                    stc(SC_TMP + q), P6[:, q * 128:(q + 1) * 128], axis=AX.X))
            vdrain()
            for q in range(4):
                O("vector", lambda e, q=q: e.tensor_scalar(
                    stc(SC_TMP + 4 + q), stc(SC_TMP + q), 1e-5, None, ALU.max))
            vdrain()
            for q in range(4):
                O("vector", lambda e, q=q: e.tensor_scalar(
                    stc(SC_INVS2 + q), stc(SC_TMP + 4 + q), 1.0 / 127.0, None, ALU.mult))
                O("vector", lambda e, q=q: e.reciprocal(
                    stc(SC_TMP + 8 + q), stc(SC_TMP + 4 + q)))
            vdrain()
            for q in range(4):
                O("vector", lambda e, q=q: e.tensor_scalar(
                    stc(SC_TMP + q), stc(SC_TMP + 8 + q), 127.0, None, ALU.mult))
                O("vector", lambda e, q=q: e.tensor_tensor(
                    stc(SC_S2 + q), stc(SC_INVS2 + q), stc(SC_WSC + 2), ALU.mult))
            v_s2bc = bcast4(SC_TMP, 512, psum_guard=fst["s_down"])

            for hf in range(2):
                v_rnd = {}
                s_t2 = {}
                for j in range(32):
                    jp = j % 2
                    if j == 0:
                        W("vector", "VE", v_s2bc)
                        if fst["te_dlast"]:
                            W("vector", "TE", fst["te_dlast"])
                    W("vector", "SC", s_t2.get(j - 2))
                    O("vector", lambda e, j=j, hf=hf, jp=jp: e.tensor_tensor(
                        TGR[:, 1024 + jp * 256:1280 + jp * 256],
                        GU[:, j * 512 + hf * 256:j * 512 + hf * 256 + 256],
                        IBC[:, 512 + hf * 256:512 + hf * 256 + 256], ALU.mult))
                    v_t1 = O("vector", lambda e: e.drain(), "VE")
                    W("scalar", "VE", v_t1)
                    s_t2[j] = O("scalar", lambda e, jp=jp: e.activation(
                        TGR[:, 1536 + jp * 256:1792 + jp * 256],
                        TGR[:, 1024 + jp * 256:1280 + jp * 256], AF.Copy, bias=MAGIC), "SC")
                    W("vector", "SC", s_t2[j])
                    v_rnd[j] = O("vector", lambda e, j=j, jp=jp: e.tensor_scalar(
                        abf(A_GUQ + j * 256, 256), TGR[:, 1536 + jp * 256:1792 + jp * 256],
                        -MAGIC, None, ALU.add), "VE")
                te_dj = {}
                for j in range(32):
                    sd = fst["wdj"] % 3
                    fst["wdj"] += 1
                    qn = "sync"
                    sn = "LM0" if sd == 0 else f"GM{sd - 1}"
                    W(qn, "TE", fst["te_wd"].get(fst["wdj"] - 1 - 3,
                                                 fst["te_gu"] if j < 3 and hf == 0 else
                                                 (fst["te_dlast"] if j < 3 else None)))
                    lw = dma2(qn, AF8[:, 4096 + sd * 1024:4096 + (sd + 1) * 1024],
                              tdd[:, j * 1024:(j + 1) * 1024], sn, 1024)
                    W("tensor", sn, lw)
                    W("tensor", "VE", v_rnd[j])
                    if j == 0:
                        W("tensor", "SC", fst["s_down"])
                    wb = 4096 + sd * 1024
                    O("tensor", lambda e, j=j, wb=wb: e.matmul(
                        PA[:, 0:512], abf(A_GUQ + j * 256, 128),
                        AF8[:, wb:wb + 512], start=(j == 0), stop=(j == 31)))
                    O("tensor", lambda e, j=j, wb=wb: e.matmul(
                        PA[:, 512:1024], abf(A_GUQ + j * 256, 128),
                        AF8[:, wb + 512:wb + 1024], start=(j == 0), stop=(j == 31)))
                    O("tensor", lambda e, j=j, wb=wb: e.matmul(
                        P[0][:, :], abf(A_GUQ + j * 256 + 128, 128),
                        AF8[:, wb:wb + 512], start=(j == 0), stop=(j == 31)))
                    te_dj[j] = O("tensor", lambda e, j=j, wb=wb: e.matmul(
                        P[1][:, :], abf(A_GUQ + j * 256 + 128, 128),
                        AF8[:, wb + 512:wb + 1024], start=(j == 0), stop=(j == 31)), "TE")
                    fst["te_wd"][fst["wdj"] - 1] = te_dj[j]
                fst["te_dlast"] = te_dj[31]
                W("scalar", "TE", te_dj[31])
                W("scalar", "VE", v_s2bc)
                v_x = 0
                for tt2 in range(2):
                    tt = hf * 2 + tt2
                    t = blk * 4 + tt
                    if tt2 == 1:
                        W("scalar", "VE", v_x)
                    if tt2 == 0:
                        O("scalar", lambda e, tt=tt: e.activation(
                            TGR[:, 1024:1536], PA[:, 0:512], AF.Identity,
                            scale=stc(SC_S2 + tt)[:, 0:1]))
                        s_e = O("scalar", lambda e, tt=tt: e.activation(
                            TGR[:, 1536:2048], PA[:, 512:1024], AF.Identity,
                            scale=stc(SC_S2 + tt)[:, 0:1]), "SC")
                    else:
                        O("scalar", lambda e, tt=tt: e.activation(
                            TGR[:, 1024:1536], P[0][:, :], AF.Identity,
                            scale=stc(SC_S2 + tt)[:, 0:1]))
                        s_e = O("scalar", lambda e, tt=tt: e.activation(
                            TGR[:, 1536:2048], P[1][:, :], AF.Identity,
                            scale=stc(SC_S2 + tt)[:, 0:1]), "SC")
                    fst["s_down"] = s_e
                    W("vector", "SC", s_e)
                    v_x = O("vector", lambda e, t=t: e.tensor_tensor(
                        X[:, t * 1024:(t + 1) * 1024], X[:, t * 1024:(t + 1) * 1024],
                        TGR[:, 1024:2048], ALU.add), "VE")
                    vdrain()
            fst["te_blk"] = fst["te_dlast"]
        return v_x

    W("vector", "S20", C["S20"])
    W("vector", "S21", C["S21"])
    ln_prepass(SC_RSTD1, SC_NMR1)
    v_last = ffn(SC_RSTD1, SC_NMR1)

    if stop_after == "ffn1":
        W("scalar", "VE", C["VE"])
        for t in range(NT):
            dma("scalar", outd[128 * t:128 * t + 128, :], X[:, t * 1024:(t + 1) * 1024], "S20")
        W("sync", "S20", C["S20"])
        W("gpsimd", "S20", C["S20"])
        prog.emit_block()
        stack.close()
        return nc

    ln_prepass(SC_RSTD2, SC_NMR2)

    W("sync", "TE", C["TE"])
    W("sync", "S20", C["S20"])
    W("sync", "S21", C["S21"])
    ld_wv = 0
    for ic in range(8):
        ld_wv = dma("sync", abf(A_GUQ + ic * 1024, 1024), wvbd[:, ic * 1024:(ic + 1) * 1024])

    st_kq = [0, 0]
    st_v = [0, 0]
    v_qrd = [0, 0]
    W("scalar", "VE", C["VE"])
    te_o = {}
    s_ev = {}
    v_vs = {}
    te_vlast = None
    for blk in range(NB):
        v_cp_last = 0
        W("vector", "S20", st_kq[0])
        W("vector", "S21", st_kq[1])
        for ti in range(4):
            t = blk * 4 + ti
            v_cp_last = xln_tile(t, ti, SC_RSTD2, SC_NMR2, False,
                                 entry_te=(te_vlast if ti == 0 else None))
        for i, oc in enumerate(list(range(8, 16)) + list(range(0, 8))):
            g = blk * 16 + i
            p = g % 2
            W("sync", "TE", te_o.get(g - 2))
            ld_w = dma2("sync", ABF[:, A_WG + p * 1024:A_WG + (p + 1) * 1024],
                        wqkb[:, oc * 1024:(oc + 1) * 1024], f"LW{p}", 1024)
            W("tensor", f"LW{p}", ld_w)
            if i == 0:
                W("tensor", "VE", v_cp_last)
                if blk == 0:
                    W("tensor", "LD", ld_wv)
                    W("tensor", "SC", fst["s_down"])
            W("tensor", "SC", s_ev.get(g - 2))
            for ic in range(8):
                te_og = O("tensor", lambda e, ic=ic, p=p: e.matmul(
                    P[p][:, :], abf(A_WG + p * 1024 + ic * 128, 128),
                    abf(A_XQT + ic * 512, 512), start=(ic == 0), stop=(ic == 7)),
                    "TE" if ic == 7 else None)
            te_o[g] = te_og
            W("scalar", "TE", te_o[g])
            W("scalar", f"S2{g % 2}", st_kq[g % 2])
            W("scalar", "VE", v_qrd[g % 2])
            sl = g % 2
            s_ev[g] = O("scalar", lambda e, p=p, oc=oc, sl=sl: e.activation(
                abf(stg(sl), 512), P[p][:, :], AF.Identity,
                bias=BQK[:, oc:oc + 1]), "SC")
            if oc >= 8:
                W("scalar", "SC", s_ev[g])
                st_kq[g % 2] = dma(
                    "scalar",
                    kkin[:, (oc - 8) * 2048 + blk * 512:(oc - 8) * 2048 + blk * 512 + 512],
                    abf(stg(sl), 512), f"S2{g % 2}")
            else:
                W("scalar", "SC", s_ev[g])
                st_kq[g % 2] = dma("scalar",
                            qtd[:, oc * 2048 + blk * 512:oc * 2048 + blk * 512 + 512],
                            abf(stg(sl), 512), f"S2{g % 2}")
        for tt in range(4):
            t = blk * 4 + tt
            g2 = blk * 4 + tt
            sl = g2 % 2
            W("vector", f"SV{sl}", st_v[sl])
            v_ms = O("vector", lambda e, sl=sl: e.memset(abf(A_VST + sl * 1040, 1040), 1.0), "VE")
            for oh in range(2):
                pbank = PA if g2 % 2 == 0 else PB
                pb = pbank[:, oh * 512:(oh + 1) * 512]
                W("tensor", "VE", v_vs.get((g2 - 2, oh)))
                if g2 == 0 and oh == 0:
                    W("tensor", "VE", v_ms)
                for ic in range(8):
                    te_v = O("tensor", lambda e, ic=ic, tt=tt, oh=oh, pb=pb: e.matmul(
                        pb, abf(A_XQT + ic * 512 + tt * 128, 128),
                        abf(A_GUQ + ic * 1024 + oh * 512, 512),
                        start=(ic == 0), stop=(ic == 7)), "TE" if ic == 7 else None)
                te_vlast = te_v
                W("vector", "TE", te_v)
                if oh == 0 and g2 > 0:
                    W("vector", "VE", v_ms)
                v_vs[(g2, oh)] = O("vector", lambda e, sl=sl, oh=oh, pb=pb: e.tensor_tensor(
                    abf(A_VST + sl * 1040, 1040).rearrange(
                        "p (h c) -> p h c", c=65)[:, 8 * oh:8 * (oh + 1), 0:64],
                    pb.rearrange("p (h c) -> p h c", c=64),
                    BB[:, oh * 512:(oh + 1) * 512].rearrange("p (h c) -> p h c", c=64),
                    ALU.add), "VE")
            W("scalar", "VE", v_vs[(g2, 1)])
            for pr in range(8):
                st_v[sl] = dma("scalar",
                            vvin[:, pr * 2080 + t * 130:pr * 2080 + t * 130 + 130],
                            abf(A_VST + sl * 1040 + pr * 130, 130), f"SV{sl}")

    W("gpsimd", "S20", C["S20"])
    W("gpsimd", "S21", C["S21"])
    W("gpsimd", "SV0", C["SV0"])
    W("gpsimd", "SV1", C["SV1"])
    O("gpsimd", lambda e: e.collective_compute(
        "AllGather", ALU.bypass, ins=[kvin[:, :]], outs=[kvout[:, :]],
        replica_groups=groups), "CC")

    te_pr = {}
    vhd = {}
    st_h = [0, 0]
    te_nc_prev = None
    s_exp_prev = None
    vev = {}
    ld_kt = {}
    ld_v = {}
    ld_qt = {}
    for pr in range(8):
        sl = pr % 2
        if pr == 0:
            W("sync", "CC", 1)
            W("sync", "TE", C["TE"])
        W("sync", "TE", te_pr.get(pr - 2))
        for h in range(2):
            dma2("sync", ABF[:, A_KT + sl * 4096 + h * 2048:A_KT + sl * 4096 + (h + 1) * 2048],
                 kkout[h * 128:(h + 1) * 128, pr * 2048:(pr + 1) * 2048], f"LK{sl}", 2048)
            dma2("sync", ABF[:, A_V + sl * 4160 + h * 2080:A_V + sl * 4160 + h * 2080 + 2080],
                 vvout[h * 128:(h + 1) * 128, pr * 2080:(pr + 1) * 2080], f"LK{sl}", 2080)
        ld_v[pr] = C[f"LK{sl}"]
        ld_qt[pr] = dma2("sync", ABF[:, A_QT + sl * 2048:A_QT + (sl + 1) * 2048],
                         qtd[:, pr * 2048:(pr + 1) * 2048], f"LK{sl}", 2048)
        W("vector", f"LK{sl}", ld_qt[pr])
        if te_nc_prev:
            W("vector", "TE", te_nc_prev)
        for h2 in range(2):
            O("vector", lambda e, sl=sl, h2=h2: e.tensor_tensor(
                GU[:, h2 * 2048:(h2 + 1) * 2048], abf(A_KT + sl * 4096 + h2 * 2048, 2048),
                abf(A_KT + sl * 4096 + h2 * 2048, 2048), ALU.mult))
        W("vector", f"LK{sl}", ld_qt[pr])
        v_ksq = O("vector", lambda e, sl=sl: e.tensor_tensor(
            GU[:, 4096:6144], abf(A_QT + sl * 2048, 2048),
            abf(A_QT + sl * 2048, 2048), ALU.mult), "VE")
        W("tensor", "VE", v_ksq)
        prv0 = vev.get((pr - 1, 3))
        W("tensor", "VE", prv0)
        v_k8 = {}
        for i8 in range(12):
            bk = (PA if i8 % 4 < 2 else PB)[0:2, (i8 % 2) * 512:(i8 % 2) * 512 + 512]
            W("tensor", "VE", v_k8.get(i8 - 4))
            te_c2 = O("tensor", lambda e, i8=i8, bk=bk: e.matmul(
                bk, ONES2[:, :], GU[:, i8 * 512:(i8 + 1) * 512],
                start=True, stop=True), "TE")
            W("vector", "TE", te_c2)
            O("vector", lambda e, bk=bk: e.reduce_max(
                STAT[0:2, SC_TMP:SC_TMP + 1], bk, axis=AX.X))
            vdrain()
            dstc = SC_KMX if i8 < 8 else SC_KMX + 1
            if i8 in (0, 8):
                O("vector", lambda e, dstc=dstc: e.tensor_copy(
                    STAT[0:2, dstc:dstc + 1], STAT[0:2, SC_TMP:SC_TMP + 1]))
            else:
                O("vector", lambda e, dstc=dstc: e.tensor_tensor(
                    STAT[0:2, dstc:dstc + 1], STAT[0:2, dstc:dstc + 1],
                    STAT[0:2, SC_TMP:SC_TMP + 1], ALU.max))
            v_k8[i8] = O("vector", lambda e: e.drain(), "VE")
        v_kacc = v_k8[11]
        if s_exp_prev:
            W("vector", "SC", s_exp_prev)
        O("vector", lambda e, pr=pr: e.tensor_tensor(
            STAT[0:2, SC_TMP:SC_TMP + 1], STAT[0:2, SC_KMX:SC_KMX + 1],
            STAT[0:2, SC_KMX + 1:SC_KMX + 2], ALU.add))
        vdrain()
        v_nc = O("vector", lambda e: e.tensor_scalar(
            STAT[0:2, SC_TMP + 1:SC_TMP + 2], STAT[0:2, SC_TMP:SC_TMP + 1],
            -1.0 / 64.0, 40.0, ALU.mult, ALU.add), "VE")
        W("scalar", "VE", v_nc)
        g_nc = dma("scalar", TGR[0:1, 0:2], STAT[0:2, SC_TMP + 1:SC_TMP + 2], "GD")
        W("vector", "GD", g_nc)
        O("vector", lambda e: e.tensor_reduce(
            TGR[0:1, 2:3], TGR[0:1, 0:2], AX.X, ALU.min))
        v_nc2 = O("vector", lambda e: e.drain(), "VE")
        W("tensor", "VE", v_nc2)
        te_nc = O("tensor", lambda e: e.matmul(
            P6[0:128, 0:1], ONES[0:1, 0:128], TGR[0:1, 2:3],
            start=True, stop=True), "TE")
        te_nc_prev = te_nc
        W("vector", "TE", te_nc)
        v_negc = O("vector", lambda e: e.tensor_copy(
            STAT[:, SC_NEGC:SC_NEGC + 1], P6[0:128, 0:1]), "VE")

        for qb in range(4):
            s_exp = {}
            te_av = {}
            def emit_av(kc):
                h, c16 = kc // 16, kc % 16
                sp = (kc % 2) * 1024
                if kc == 0:
                    prv = vev.get((pr, qb - 1)) if qb > 0 else vev.get((pr - 1, 3))
                    W("tensor", "VE", prv)
                W("tensor", "SC", s_exp[kc])
                O("tensor", lambda e, sl=sl, h=h, c16=c16, sp=sp, kc=kc: e.matmul(
                    P[0][0:65, :], abf(A_V + sl * 4160 + h * 2080 + c16 * 130, 65),
                    abf(A_PT + sp, 512), start=(kc == 0), stop=(kc == 31)))
                te_av[kc] = O("tensor", lambda e, sl=sl, h=h, c16=c16, sp=sp, kc=kc: e.matmul(
                    P[1][0:65, :], abf(A_V + sl * 4160 + h * 2080 + c16 * 130 + 65, 65),
                    abf(A_PT + sp + 512, 512), start=(kc == 0), stop=(kc == 31)), "TE")

            for kc in range(32):
                h, c16 = kc // 16, kc % 16
                sb = PA if kc % 2 == 0 else PB
                if kc == 0:
                    W("tensor", "VE", v_negc)
                    W("tensor", f"LK{sl}", ld_qt[pr])
                W("tensor", "SC", s_exp.get(kc - 2))
                O("tensor", lambda e, sl=sl, h=h, c16=c16, sb=sb, qb=qb: e.matmul(
                    sb[:, 0:512], abf(A_KT + sl * 4096 + h * 2048 + c16 * 128, 128, 0, 64),
                    abf(A_QT + sl * 2048 + qb * 512, 512, 0, 64),
                    start=True, stop=True, tile_position=(0, 0)))
                te_s = O("tensor", lambda e, sl=sl, h=h, c16=c16, sb=sb, qb=qb: e.matmul(
                    sb[:, 512:1024], abf(A_KT + sl * 4096 + h * 2048 + c16 * 128, 128, 64, 128),
                    abf(A_QT + sl * 2048 + qb * 512, 512, 64, 128),
                    start=True, stop=True, tile_position=(64, 0)), "TE")
                W("scalar", "TE", te_s)
                if kc >= 3:
                    W("scalar", "TE", te_av[kc - 2])
                sp = (kc % 2) * 1024
                s_exp[kc] = O("scalar", lambda e, sb=sb, sp=sp: e.activation(
                    abf(A_PT + sp, 1024), sb[:, 0:1024], AF.Exp,
                    scale=ISCALE, bias=stc(SC_NEGC)[:, 0:1]), "SC")
                if kc >= 1:
                    emit_av(kc - 1)
            emit_av(31)
            s_exp_prev = s_exp[31]
            W("vector", "TE", te_av[31])
            W("vector", "VE", vhd.get((pr, qb - 1), vhd.get((pr - 1, 3))))
            O("vector", lambda e: e.tensor_copy(CTX[0:65, :], P[0][0:65, :]))
            v_cx = O("vector", lambda e: e.tensor_copy(MNA[0:65, :], P[1][0:65, :]), "VE")
            vev[(pr, qb)] = v_cx
            W("scalar", "VE", v_cx)
            g_rs = dma("scalar", TGR[0:1, 1024:1536], CTX[64:65, 0:512], "GD")
            g_rs2 = dma("scalar", TGR[0:1, 1536:2048], MNA[64:65, 0:512], "GD")
            W("vector", "GD", g_rs2)
            O("vector", lambda e: e.reciprocal(TGR[0:1, 1024:1536], TGR[0:1, 1024:1536]))
            O("vector", lambda e: e.reciprocal(TGR[0:1, 1536:2048], TGR[0:1, 1536:2048]))
            v_rc = O("vector", lambda e: e.drain(), "VE")
            v_h = 0
            for hh in range(2):
                if hh == 0:
                    W("tensor", "VE", v_rc)
                else:
                    W("tensor", "VE", v_h)
                te_rp = O("tensor", lambda e, hh=hh: e.matmul(
                    P6[0:64, 0:512], ONES[0:1, 0:64],
                    TGR[0:1, 1024 + hh * 512:1536 + hh * 512],
                    start=True, stop=True), "TE")
                W("vector", "TE", te_rp)
                W("vector", f"SH{hh}", st_h[hh])
                src = CTX if hh == 0 else MNA
                v_h = O("vector", lambda e, hh=hh, src=src: e.tensor_tensor(
                    abf(A_HST + hh * 512, 512, 0, 64), src[0:64, :], P6[0:64, 0:512],
                    ALU.mult), "VE")
                W("scalar", "VE", v_h)
                st_h[hh] = dma("scalar",
                    htd[64 * hh:64 * (hh + 1),
                        pr * 2048 + qb * 512:pr * 2048 + qb * 512 + 512],
                    abf(A_HST + hh * 512, 512, 0, 64), f"SH{hh}")
            vhd[(pr, qb)] = v_h
        te_pr[pr] = C["TE"]

    W("sync", "TE", C["TE"])
    W("sync", "GD", C["GD"])
    v_bo = bias_bcast(bord[:], C["VE"], te_guard=C["SC"])
    ld_wo = 0
    for ic in range(8):
        ld_wo = dma("sync", abf(A_GUQ + ic * 1024, 1024), wobd[:, ic * 1024:(ic + 1) * 1024])
    te_op = {}
    s_op = {}
    v_xo = {}
    for t in range(NT):
        p = t % 2
        W("sync", "TE", te_op.get(t - 2))
        if t == 0:
            W("sync", "SH0", C["SH0"])
            W("sync", "SH1", C["SH1"])
        ld_h = 0
        for fc in range(8):
            ld_h = dma("sync", abf(A_QT + p * 1024 + fc * 128, 128),
                       htd[:, fc * 2048 + t * 128:fc * 2048 + t * 128 + 128], f"LK{p}")
        W("tensor", f"LK{p}", ld_h)
        if t == 0:
            W("tensor", "LD", ld_wo)
        if t == 0:
            W("tensor", "SC", C["SC"])
            W("tensor", "VE", C["VE"])
        W("tensor", "VE", v_xo.get(t - 2))
        pbank = PA if p == 0 else PB
        ob0 = pbank[:, 0:512]
        ob1 = pbank[:, 512:1024]
        for fc in range(8):
            O("tensor", lambda e, fc=fc, p=p, ob0=ob0: e.matmul(
                ob0, abf(A_QT + p * 1024 + fc * 128, 128),
                abf(A_GUQ + fc * 1024, 512), start=(fc == 0), stop=(fc == 7)))
            te_ot = O("tensor", lambda e, fc=fc, p=p, ob1=ob1: e.matmul(
                ob1, abf(A_QT + p * 1024 + fc * 128, 128),
                abf(A_GUQ + fc * 1024 + 512, 512), start=(fc == 0), stop=(fc == 7)),
                "TE" if fc == 7 else None)
        te_op[t] = te_ot
        W("scalar", "TE", te_op[t])
        W("scalar", "SC", s_op.get(t - 1, 0))
        W("scalar", "VE", v_xo.get(t - 1))
        s_a = O("scalar", lambda e, t=t: e.activation(
            XLN[:, 0:1024], X[:, t * 1024:(t + 1) * 1024], AF.Identity,
            bias=stc(SC_NMR2 + t)[:, 0:1], scale=stc(SC_RSTD2 + t)[:, 0:1]), "SC")
        s_op[t] = s_a
        W("vector", "SC", s_a)
        if t == 0:
            W("vector", "VE", v_bo)
        for oh in range(2):
            ob = (PA if p == 0 else PB)[:, oh * 512:(oh + 1) * 512]
            O("vector", lambda e, oh=oh, ob=ob: e.tensor_tensor(
                TGR[:, oh * 512:(oh + 1) * 512], ob,
                BB[:, oh * 512:(oh + 1) * 512], ALU.add))
        vdrain()
        O("vector", lambda e: e.tensor_tensor(
            TGR[:, 1024:2048], TGR[:, 0:1024], XLN[:, 0:1024], ALU.add))
        vdrain()
        v_xo[t] = O("vector", lambda e, t=t: e.tensor_tensor(
            X[:, t * 1024:(t + 1) * 1024], X[:, t * 1024:(t + 1) * 1024],
            TGR[:, 1024:2048], ALU.add), "VE")
        vdrain()
    fst["te_blk"] = te_op[NT - 1]
    fst["s_down"] = s_op[NT - 1]

    if stop_after != "attn":
        ln_prepass(SC_RSTD1, SC_NMR1)
        ffn(SC_RSTD1, SC_NMR1)

    W("scalar", "VE", C["VE"])
    for t in range(NT):
        dma("scalar", outd[128 * t:128 * t + 128, :], X[:, t * 1024:(t + 1) * 1024], "S20")
    for sn in ("S20", "S21", "SV0", "SV1", "SH0", "SH1", "GS0", "GS1"):
        W("sync", sn, C[sn])
        W("gpsimd", sn, C[sn])
    prog.emit_block()
    stack.close()
    return nc


def _pm(w, nj, nic):
    return np.ascontiguousarray(
        w.reshape(nj, 128, nic, 128).transpose(3, 0, 2, 1).reshape(128, nj * nic * 128))


def make_inputs(L0_attn_x, L0_attn_w_qkv, L0_attn_b_qkv, L0_attn_w_o, L0_attn_b_o,
                L0_ffn_w_gate, L0_ffn_w_up, L0_ffn_w_down, n_cores=8):
    x = np.ascontiguousarray(np.asarray(L0_attn_x, dtype=np.float32))
    w_qkv = np.asarray(L0_attn_w_qkv, dtype=np.float32)
    b_qkv = np.asarray(L0_attn_b_qkv, dtype=np.float32)
    w_o = np.asarray(L0_attn_w_o, dtype=np.float32)
    b_o = np.asarray(L0_attn_b_o, dtype=np.float32)
    w_gate = np.asarray(L0_ffn_w_gate, dtype=np.float32)
    w_up = np.asarray(L0_ffn_w_up, dtype=np.float32)
    w_down = np.asarray(L0_ffn_w_down, dtype=np.float32)

    shared = {
        "wgpm": _pm(w_gate, 32, 8),
        "wupm": _pm(w_up, 32, 8),
        "wdpm": np.ascontiguousarray(
            w_down.T.reshape(32, 128, 1024).transpose(1, 0, 2).reshape(128, 32 * 1024)),
        "wqkpm": _pm(w_qkv[0:2048], 16, 8),
        "wvpm": np.ascontiguousarray(
            w_qkv[2048:3072].T.reshape(8, 128, 1024).transpose(1, 0, 2).reshape(128, 8 * 1024)),
        "wopm": np.ascontiguousarray(
            w_o.T.reshape(8, 128, 1024).transpose(1, 0, 2).reshape(128, 8 * 1024)),
        "bqkc": np.ascontiguousarray(b_qkv[0:2048].reshape(16, 128).T),
        "bvr": np.ascontiguousarray(b_qkv[2048:3072].reshape(1, 1024)),
        "bor": np.ascontiguousarray(b_o.reshape(1, 1024)),
        "ident": np.eye(128, dtype=ml_dtypes.bfloat16),
        "identf": np.eye(128, dtype=np.float32),
    }
    in_maps = []
    for c in range(n_cores):
        b, h = c // 2, c % 2
        m = dict(shared)
        m["x"] = np.ascontiguousarray(x[b, 2048 * h:2048 * h + 2048])
        in_maps.append(m)
    return in_maps


def kernel(L0_attn_x, L0_attn_w_qkv, L0_attn_b_qkv, L0_attn_w_o, L0_attn_b_o,
           L0_ffn_w_gate, L0_ffn_w_up, L0_ffn_w_down):
    in_maps = make_inputs(L0_attn_x, L0_attn_w_qkv, L0_attn_b_qkv, L0_attn_w_o,
                          L0_attn_b_o, L0_ffn_w_gate, L0_ffn_w_up, L0_ffn_w_down)
    nc = bass.Bass("TRN2", target_bir_lowering=False)
    build(nc)
    res = run_bass_kernel_spmd(nc, in_maps, core_ids=list(range(8)))
    out = np.empty((4, 4096, 1024), dtype=np.float32)
    for c in range(8):
        b, h = c // 2, c % 2
        out[b, 2048 * h:2048 * h + 2048] = res.results[c]["out"]
    return out


# version 12
# speedup vs baseline: 1.4571x; 1.0143x over previous
import os
import contextlib

import numpy as np
import ml_dtypes
import concourse.bass as bass
import concourse.mybir as mybir
from concourse.bass_utils import run_bass_kernel_spmd

F32 = mybir.dt.float32
BF16 = mybir.dt.bfloat16
FP8 = mybir.dt.float8e4
AF = mybir.ActivationFunctionType
ALU = mybir.AluOpType
AX = mybir.AxisListType

MAGIC = float(np.float32(1.5 * 2.0 ** 52)) if os.environ.get("BASS_SIM_FP64") == "1" \
    else float(np.float32(1.5 * 2.0 ** 23))
D = 1024
T = 2048
NT = 16
NB = 4
EPS = 1e-5
ISCALE = 1.0 / 32.0

A_XQT = 0
A_WG = 4096
A_WU = 6144
A_GUQ = 8192
A_VST = 16384
A_STG = 22528
A_KT = 0
A_V = 16384
A_V = 8192
A_QT = 16640
A_PT = 20736
A_HST = 22784

SC_RSTD1, SC_NMR1 = 0, 16
SC_RSTD2, SC_NMR2 = 32, 48
SC_SX, SC_INVSX = 64, 80
SC_S2 = 96
SC_INVS2 = 100
SC_WSC = 104
SC_TMP = 112
SC_QMX = 128
SC_KMX = 144
SC_NEGC = 148


class Prog:
    def __init__(self, nc, stack):
        self.nc = nc
        self.streams = {k: [] for k in ("sync", "tensor", "scalar", "vector", "gpsimd")}
        self.sems = {}
        self.cnt = {}
        self._stack = stack

    def sem(self, name):
        self.sems[name] = self._stack.enter_context(self.nc.semaphore(name=name))
        self.cnt[name] = 0

    def wait(self, eng, sname, val):
        if val is None or val <= 0:
            return
        h = self.sems[sname]
        self.streams[eng].append(lambda e, h=h, v=val: e.wait_ge(h, v))

    def op(self, eng, build, sname=None, n=1):
        if sname is None:
            self.streams[eng].append(build)
            return None
        h = self.sems[sname]
        self.cnt[sname] += n
        v = self.cnt[sname]

        def f(e, build=build, h=h, n=n):
            build(e).then_inc(h, n)

        self.streams[eng].append(f)
        return v

    def emit_block(self):
        with self.nc.Block() as block:
            @block.sync
            def _(e):
                for f in self.streams["sync"]:
                    f(e)

            @block.tensor
            def _(e):
                for f in self.streams["tensor"]:
                    f(e)

            @block.scalar
            def _(e):
                for f in self.streams["scalar"]:
                    f(e)

            @block.vector
            def _(e):
                for f in self.streams["vector"]:
                    f(e)

            @block.gpsimd
            def _(e):
                for f in self.streams["gpsimd"]:
                    f(e)


def build(nc, groups=None, stop_after=None):
    if groups is None:
        groups = [[0, 1], [2, 3], [4, 5], [6, 7]]
    dt = nc.dram_tensor
    xd = dt("x", [T, D], F32, kind="ExternalInput").ap()
    wgpm = dt("wgpm", [128, 32 * 1024], F32, kind="ExternalInput").ap()
    wupm = dt("wupm", [128, 32 * 1024], F32, kind="ExternalInput").ap()
    wdpm = dt("wdpm", [128, 32 * 1024], F32, kind="ExternalInput").ap()
    wqkpm = dt("wqkpm", [128, 16 * 1024], F32, kind="ExternalInput").ap()
    wvpm = dt("wvpm", [128, 8 * 1024], F32, kind="ExternalInput").ap()
    wopm = dt("wopm", [128, 8 * 1024], F32, kind="ExternalInput").ap()
    bqkc = dt("bqkc", [128, 16], F32, kind="ExternalInput").ap()
    bvr = dt("bvr", [1, D], F32, kind="ExternalInput").ap()
    bord = dt("bor", [1, D], F32, kind="ExternalInput").ap()
    identd = dt("ident", [128, 128], BF16, kind="ExternalInput").ap()
    identf = dt("identf", [128, 128], F32, kind="ExternalInput").ap()
    outd = dt("out", [T, D], F32, kind="ExternalOutput").ap()

    tgd = dt("tgs", [128, 32 * 1024], FP8, kind="Internal").ap()
    tud = dt("tus", [128, 32 * 1024], FP8, kind="Internal").ap()
    tdd = dt("tds", [128, 32 * 1024], FP8, kind="Internal").ap()
    wqkb = dt("wqkbs", [128, 16 * 1024], BF16, kind="Internal").ap()
    wvbd = dt("wvbs", [128, 8 * 1024], BF16, kind="Internal").ap()
    wobd = dt("wobs", [128, 8 * 1024], BF16, kind="Internal").ap()
    qtd = dt("qts", [128, 8 * 2048], BF16, kind="Internal").ap()
    htd = dt("hts", [128, 8 * 2048], BF16, kind="Internal").ap()
    kvin = dt("kvin", [128, 8 * 2048 + 8 * 2080], BF16, kind="Internal").ap()
    kvout = dt("kvout", [256, 8 * 2048 + 8 * 2080], BF16, kind="Internal").ap()
    kkin = kvin[:, 0:8 * 2048]
    vvin = kvin[:, 8 * 2048:]
    kkout = kvout[:, 0:8 * 2048]
    vvout = kvout[:, 8 * 2048:]

    stack = contextlib.ExitStack()
    sbuf = lambda name, shape, dty: stack.enter_context(nc.sbuf_tensor(name, shape, dty))
    pst = lambda name, shape, dty=F32: stack.enter_context(nc.psum_tensor(name, shape, dty))

    X = sbuf("X", [128, NT * 1024], F32)
    GU = sbuf("GU", [128, 32 * 512], F32)
    XLN = sbuf("XLN", [128, 1024], F32)
    TGR = sbuf("TGR", [128, 2048], F32)
    CTX = sbuf("CTX", [128, 512], F32)
    MNA = sbuf("MNA", [128, 512], F32)
    STAT = sbuf("STAT", [128, 160], F32)
    IBC = sbuf("IBC", [128, 1024], F32)
    ABF = sbuf("ABF", [128, 23808], BF16)
    AF8 = sbuf("AF8", [128, 8192], FP8)
    BB = sbuf("BB", [128, 1024], BF16)
    IDT = sbuf("IDT", [128, 128], BF16)
    IDF = sbuf("IDF", [128, 128], F32)
    BQK = sbuf("BQK", [128, 16], F32)
    ONES = sbuf("ONES", [128, 128], F32)
    ONES2 = sbuf("ONES2", [128, 2], F32)

    P = [pst(f"P{i}", [128, 512]) for i in range(2)]
    PA = pst("PA", [128, 1024])
    PB = pst("PB", [128, 1024])
    P6 = pst("P6", [128, 512])
    PT8 = pst("PT8", [128, 1024], BF16)

    prog = Prog(nc, stack)
    for s in ("LD", "LW0", "LW1", "LK0", "LK1", "GW0", "GW1", "GM0", "GM1", "LM0", "LB", "LC0", "LC1", "GD", "GS0", "GS1",
              "S20", "S21", "SV0", "SV1", "SH0", "SH1", "TE", "SC", "VE", "CC"):
        prog.sem(s)
    W, O, C = prog.wait, prog.op, prog.cnt

    def dma(eng, out, in_, sname="LD", **kw):
        return O(eng, lambda e, o=out, i=in_, kw=kw: e.dma_start(out=o, in_=i, **kw),
                 sname, 16)

    def dma2(eng, out, in_, sname, n):
        h = n // 2
        O(eng, lambda e, o=out, i=in_, h=h: e.dma_start(out=o[:, 0:h], in_=i[:, 0:h]),
          sname, 16)
        return O(eng, lambda e, o=out, i=in_, h=h, n=n: e.dma_start(
            out=o[:, h:n], in_=i[:, h:n]), sname, 16)

    def vdrain():
        O("vector", lambda e: e.drain())

    def stc(c):
        return STAT[:, c:c + 1]

    def abf(lo, n, p0=0, p1=128):
        return ABF[p0:p1, lo:lo + n]

    def stg(sl):
        return A_STG if sl == 0 else A_WU

    dma("sync", IDT[:], identd[:])
    dma("sync", IDF[:], identf[:])
    for t in range(NT):
        dma("sync", X[:, t * 1024:(t + 1) * 1024], xd[128 * t:128 * t + 128, :])
    dma("sync", BQK[:], bqkc[:])
    O("vector", lambda e: e.memset(ONES2[:, :], 0.0))
    vdrain()
    O("vector", lambda e: e.memset(ONES2[0:64, 0:1], 1.0))
    vdrain()
    O("vector", lambda e: e.memset(ONES2[64:128, 1:2], 1.0))
    vdrain()
    v_ones = O("vector", lambda e: e.memset(ONES[:], 1.0), "VE")

    def bias_bcast(row_dram, ve_guard, te_guard=None):
        lb = dma("sync", TGR[0:1, 0:1024], row_dram, "LB")
        W("tensor", "LB", lb)
        W("tensor", "VE", ve_guard)
        if te_guard:
            W("tensor", "SC", te_guard)
        v_b = 0
        for h in range(2):
            if h == 1:
                W("tensor", "VE", v_b)
            te_b = O("tensor", lambda e, h=h: e.matmul(
                P[0][:, :], ONES[0:1, 0:128], TGR[0:1, 512 * h:512 * (h + 1)],
                start=True, stop=True), "TE")
            W("vector", "TE", te_b)
            v_b = O("vector", lambda e, h=h: e.tensor_copy(
                BB[:, 512 * h:512 * (h + 1)], P[0][:, :]), "VE")
        return v_b

    v_bb = bias_bcast(bvr[:], v_ones)
    ve_bias = C["VE"]

    NCOL = 128 * 32 * 1024

    def wprep(src, dst, wsc_col, first_wait, sc_first=0):
        v_red = [0] * 8
        for i in range(8):
            p = i % 2
            if i < 2:
                W("sync", "VE", first_wait)
                W("sync", "SC", sc_first)
            else:
                W("sync", "VE", v_red[i - 2])
            lv = dma2("sync", GU[:, p * 4096:(p + 1) * 4096],
                      src[:, i * 4096:(i + 1) * 4096], f"LW{p}", 4096)
            W("vector", f"LW{p}", lv)
            v_red[i] = O("vector", lambda e, i=i, p=p: e.reduce_sum(
                stc(SC_TMP + i), GU[:, p * 4096:(p + 1) * 4096], axis=AX.X,
                apply_absolute_value=True), "VE")
        vdrain()
        O("vector", lambda e: e.reduce_sum(
            stc(SC_TMP + 8), STAT[:, SC_TMP:SC_TMP + 8], axis=AX.X))
        v_s = O("vector", lambda e: e.drain(), "VE")
        W("tensor", "VE", v_s)
        te_s = O("tensor", lambda e: e.matmul(
            P6[0:1, 0:1], stc(SC_TMP + 8), ONES[:, 0:1], start=True, stop=True), "TE")
        W("vector", "TE", te_s)
        O("vector", lambda e: e.tensor_scalar(
            stc(SC_TMP + 9)[0:1, :], P6[0:1, 0:1], 1.0 / NCOL, 1e-5, ALU.mult, ALU.max))
        vdrain()
        v_r = O("vector", lambda e: e.reciprocal(
            stc(SC_TMP + 10)[0:1, :], stc(SC_TMP + 9)[0:1, :]), "VE")
        W("tensor", "VE", v_r)
        O("tensor", lambda e: e.matmul(
            P6[0:128, 0:1], ONES[0:1, 0:128], stc(SC_TMP + 10)[0:1, :],
            start=True, stop=True))
        te_b2 = O("tensor", lambda e: e.matmul(
            P6[0:128, 1:2], ONES[0:1, 0:128], stc(SC_TMP + 9)[0:1, :],
            start=True, stop=True), "TE")
        W("vector", "TE", te_b2)
        O("vector", lambda e: e.tensor_copy(stc(SC_TMP + 11), P6[0:128, 0:1]))
        v_bc = O("vector", lambda e, c=wsc_col: e.tensor_copy(stc(c), P6[0:128, 1:2]), "VE")
        st_prev = [C["GS0"], C["GS1"]]
        s_done = [0] * 8
        v_done = [0] * 8
        for i in range(8):
            p = i % 2
            W("scalar", "VE", v_done[i - 2] if i >= 2 else v_bc)
            lv = dma2("scalar", GU[:, p * 4096:(p + 1) * 4096],
                      src[:, i * 4096:(i + 1) * 4096], f"LC{p}", 4096)
            W("scalar", f"LC{p}", lv)
            W("scalar", "VE", v_done[i - 1] if i >= 1 else v_bc)
            s_done[i] = O("scalar", lambda e, p=p: e.activation(
                GU[:, 8192 + p * 4096:8192 + (p + 1) * 4096],
                GU[:, p * 4096:(p + 1) * 4096],
                AF.Copy, bias=MAGIC, scale=stc(SC_TMP + 11)[:, 0:1]), "SC")
            W("vector", "SC", s_done[i])
            W("vector", f"GS{p}", st_prev[p])
            O("vector", lambda e, p=p: e.tensor_scalar(
                abf(p * 4096, 4096), GU[:, 8192 + p * 4096:8192 + (p + 1) * 4096],
                -MAGIC, 1.0, ALU.add, ALU.min))
            vdrain()
            v_done[i] = O("vector", lambda e, p=p: e.tensor_scalar(
                AF8[:, p * 4096:(p + 1) * 4096], abf(p * 4096, 4096),
                -1.0, None, ALU.max), "VE")
            W("scalar", "VE", v_done[i])
            st_prev[p] = dma("scalar", dst[:, i * 4096:(i + 1) * 4096],
                             AF8[:, p * 4096:(p + 1) * 4096], f"GS{p}")
        return v_done[7], s_done[7]

    v_p, s_p = wprep(wgpm, tgd, SC_WSC + 0, ve_bias)
    v_p, s_p = wprep(wupm, tud, SC_WSC + 1, 0, sc_first=s_p)
    v_p, s_p = wprep(wdpm, tdd, SC_WSC + 2, 0, sc_first=s_p)
    vdrain()
    O("vector", lambda e: e.tensor_scalar(
        stc(SC_WSC + 2), stc(SC_WSC + 2), 0.5, None, ALU.mult), "VE")

    def wcast(src, dst, ncols, v_first, sc_first=0):
        v_prev = [0, 0]
        st_prev = [C["S20"], C["S21"]]
        W("vector", "GS0", C["GS0"])
        W("vector", "GS1", C["GS1"])
        for i in range(ncols // 4096):
            p = i % 2
            if v_prev[p]:
                W("sync", "VE", v_prev[p])
            else:
                W("sync", "VE", v_first)
                W("sync", "SC", sc_first)
            lv = dma2("sync", GU[:, p * 4096:(p + 1) * 4096],
                      src[:, i * 4096:(i + 1) * 4096], f"LW{p}", 4096)
            W("vector", f"LW{p}", lv)
            W("vector", f"S2{p}", st_prev[p])
            v_prev[p] = O("vector", lambda e, p=p: e.tensor_copy(
                abf(p * 4096, 4096), GU[:, p * 4096:(p + 1) * 4096]), "VE")
            W("scalar", "VE", v_prev[p])
            st_prev[p] = dma("scalar", dst[:, i * 4096:(i + 1) * 4096],
                             abf(p * 4096, 4096), f"S2{p}")
        return v_prev[(ncols // 4096 - 1) % 2]

    v_c = wcast(wqkpm, wqkb, 16 * 1024, 0, sc_first=s_p)
    v_c = wcast(wvpm, wvbd, 8 * 1024, v_c)
    v_c = wcast(wopm, wobd, 8 * 1024, v_c)

    def ln_prepass(rstd_base, nmr_base):
        W("vector", "LD", C["LD"])
        W("scalar", "LD", C["LD"])
        W("scalar", "VE", C["VE"])
        W("scalar", "LW0", C["LW0"])
        W("scalar", "LW1", C["LW1"])
        for t in range(NT):
            O("vector", lambda e, t=t: e.reduce_sum(
                stc(SC_TMP + t), X[:, t * 1024:(t + 1) * 1024], axis=AX.X))
        v0 = O("vector", lambda e: e.drain(), "VE")
        s0 = 0
        s_hist = {}
        for t in range(NT):
            W("scalar", "SC", s_hist.get(t - 4, 0))
            s_hist[t] = O("scalar", lambda e, t=t: e.activation(
                GU[:, (t % 4) * 1024:(t % 4) * 1024 + 1024],
                X[:, t * 1024:(t + 1) * 1024], AF.Square,
                accum_out=stc(nmr_base + t)), "SC")
        s0 = s_hist[NT - 1]
        W("vector", "SC", s0)
        TM = STAT[:, SC_TMP:SC_TMP + 16]
        SQ = STAT[:, nmr_base:nmr_base + 16]
        MM = STAT[:, SC_SX:SC_SX + 16]
        O("vector", lambda e, TM=TM: e.tensor_scalar(TM, TM, 1.0 / D, None, ALU.mult))
        O("vector", lambda e, SQ=SQ: e.tensor_scalar(SQ, SQ, 1.0 / D, None, ALU.mult))
        vdrain()
        O("vector", lambda e, TM=TM, MM=MM: e.tensor_tensor(MM, TM, TM, ALU.mult))
        vdrain()
        O("vector", lambda e, SQ=SQ, MM=MM: e.tensor_tensor(SQ, SQ, MM, ALU.subtract))
        vdrain()
        O("vector", lambda e, SQ=SQ: e.tensor_scalar(SQ, SQ, EPS, None, ALU.add))
        vdrain()
        v1 = O("vector", lambda e, SQ=SQ, MM=MM: e.reciprocal(MM, SQ), "VE")
        W("scalar", "VE", v1)
        s1 = O("scalar", lambda e, MM=MM, rb=rstd_base: e.activation(
            STAT[:, rb:rb + 16], MM, AF.Sqrt), "SC")
        W("vector", "SC", s1)
        O("vector", lambda e, TM=TM, rb=rstd_base, nb=nmr_base: e.scalar_tensor_tensor(
            STAT[:, nb:nb + 16], TM, -1.0, STAT[:, rb:rb + 16], ALU.mult, ALU.mult), "VE")
        vdrain()

    st8 = {"seq": 0, "v_xl": {}, "te_tp": {}, "v_cp": {}, "v_tg": {}}

    def xln_tile(t, ti, rstd_base, nmr_base, quant, entry_te=None):
        k = st8["seq"]
        st8["seq"] += 1
        sl = k % 2
        xt = X[:, t * 1024:(t + 1) * 1024]
        xl = XLN[:, 0:1024]
        W("scalar", "VE", st8["v_xl"].get(k - 1))
        W("scalar", "SC", st8.get("s_last", 0))
        s0 = O("scalar", lambda e, xt=xt, xl=xl, t=t: e.activation(
            xl, xt, AF.Identity, bias=stc(nmr_base + t)[:, 0:1],
            scale=stc(rstd_base + t)[:, 0:1]), "SC")
        W("vector", "SC", s0)
        if entry_te:
            W("vector", "TE", entry_te)
        if quant:
            O("vector", lambda e, xl=xl: e.reduce_max(
                stc(SC_TMP + 0), xl, axis=AX.X, apply_absolute_value=True))
            vdrain()
            O("vector", lambda e: e.tensor_scalar(
                stc(SC_TMP + 1), stc(SC_TMP + 0), 1e-5, None, ALU.max))
            vdrain()
            O("vector", lambda e, t=t: e.tensor_scalar(
                stc(SC_INVSX + t % 16), stc(SC_TMP + 1), 1.0 / 127.0, None, ALU.mult))
            O("vector", lambda e: e.reciprocal(stc(SC_TMP + 2), stc(SC_TMP + 1)))
            vdrain()
            v0 = O("vector", lambda e, t=t: e.tensor_scalar(
                stc(SC_SX + t % 16), stc(SC_TMP + 2), 127.0, None, ALU.mult), "VE")
            st8["v_xl"][k] = v0
            W("scalar", "VE", v0)
            W("scalar", "VE", st8["v_tg"].get(k - 1))
            s1 = O("scalar", lambda e, xl=xl, t=t: e.activation(
                TGR[:, 0:1024], xl, AF.Copy, bias=MAGIC,
                scale=stc(SC_SX + t % 16)[:, 0:1]), "SC")
            st8["s_last"] = s1
            W("vector", "SC", s1)
            W("vector", "TE", st8["te_tp"].get(k - 2))
            v1 = O("vector", lambda e, sl=sl: e.tensor_scalar(
                abf(stg(sl), 1024), TGR[:, 0:1024], -MAGIC, None, ALU.add), "VE")
            st8["v_tg"][k] = v1
        else:
            W("vector", "TE", st8["te_tp"].get(k - 2))
            v1 = O("vector", lambda e, xl=xl, sl=sl: e.tensor_copy(
                abf(stg(sl), 1024), xl), "VE")
            st8["v_xl"][k] = v1
        W("tensor", "VE", v1)
        W("tensor", "VE", st8["v_cp"].get(k - 1))
        for fc in range(8):
            te_x = O("tensor", lambda e, fc=fc, sl=sl: e.transpose(
                PT8[:, fc * 128:(fc + 1) * 128],
                abf(stg(sl) + fc * 128, 128), IDT[:]),
                "TE" if fc == 7 else None)
        st8["te_tp"][k] = te_x
        W("vector", "TE", st8["te_tp"][k])
        v_c = 0
        for fc in range(8):
            v_c = O("vector", lambda e, fc=fc, ti=ti: e.tensor_copy(
                abf(A_XQT + fc * 512 + ti * 128, 128), PT8[:, fc * 128:(fc + 1) * 128]),
                "VE" if fc == 7 else None)
        st8["v_cp"][k] = v_c
        return v_c

    def bcast4(col0, ibc_lo, psum_guard=0):
        vdrain()
        v_a = O("vector", lambda e, c=col0: e.tensor_copy(
            TGR[:, 1024:1028], STAT[:, c:c + 4]), "VE")
        W("tensor", "VE", v_a)
        W("tensor", "SC", psum_guard)
        for q in range(4):
            te_a = O("tensor", lambda e, q=q: e.transpose(
                P6[:, q * 128:(q + 1) * 128], TGR[:, 1024 + q:1024 + q + 128], IDF[:]),
                "TE" if q == 3 else None)
        W("vector", "TE", te_a)
        for q in range(4):
            O("vector", lambda e, q=q: e.tensor_copy(
                TGR[0:1, 1152 + q * 128:1152 + (q + 1) * 128],
                P6[0:1, q * 128:(q + 1) * 128]))
        v_b = O("vector", lambda e: e.drain(), "VE")
        W("tensor", "VE", v_b)
        te_c = O("tensor", lambda e: e.matmul(
            P6[:, 0:512], ONES[0:1, 0:128], TGR[0:1, 1152:1664],
            start=True, stop=True), "TE")
        W("vector", "TE", te_c)
        return O("vector", lambda e, lo=ibc_lo: e.tensor_copy(
            IBC[:, lo:lo + 512], P6[:, 0:512]), "VE")

    fst = {"call": 0, "te_gu": None, "s_down": 0, "te_dlast": None, "te_blk": None,
           "wdj": 0, "te_wd": {}}

    def ffn(rstd_base, nmr_base):
        W("scalar", "VE", C["VE"])
        W("scalar", "LD", C["LD"])
        W("vector", "LD", C["LD"])
        W("sync", "TE", C["TE"])
        for sn in ("S20", "S21", "SV0", "SV1", "SH0", "SH1", "GS0", "GS1"):
            W("sync", sn, C[sn])
            W("vector", sn, C[sn])
        for blk in range(NB):
            v_cp_last = 0
            for ti in range(4):
                t = blk * 4 + ti
                v_cp_last = xln_tile(t, ti, rstd_base, nmr_base, True,
                                     entry_te=(fst["te_blk"] if ti == 0 else None))
            v_ibc = bcast4(SC_INVSX + (blk * 4) % 16, 0, psum_guard=fst["s_down"])
            te_j = {}
            v_pu = {}
            s_g = {}
            v_mm = {}
            for j in range(32):
                p = j % 2
                W("sync", "TE", te_j.get(j - 2, fst["te_dlast"] if j < 2 else None))
                ld_w = dma2("sync", AF8[:, p * 1024:(p + 1) * 1024],
                            tgd[:, j * 1024:(j + 1) * 1024], f"LW{p}", 1024)
                ld_u = dma2("sync", AF8[:, 2048 + p * 1024:2048 + (p + 1) * 1024],
                            tud[:, j * 1024:(j + 1) * 1024], f"GW{p}", 1024)
                gb = PA[:, 0:512] if p == 0 else PB[:, 0:512]
                ub = PA[:, 512:1024] if p == 0 else PB[:, 512:1024]
                W("tensor", f"LW{p}", ld_w)
                W("tensor", f"GW{p}", ld_u)
                if j == 0:
                    W("tensor", "VE", v_ibc)
                    W("tensor", "SC", fst["s_down"])
                W("tensor", "VE", v_pu.get(j - 2))
                for ic in range(8):
                    O("tensor", lambda e, ic=ic, p=p, gb=gb: e.matmul(
                        gb, AF8[:, p * 1024 + ic * 128:p * 1024 + ic * 128 + 128],
                        abf(A_XQT + ic * 512, 512), start=(ic == 0), stop=(ic == 7)))
                for ic in range(8):
                    te_jj = O("tensor", lambda e, ic=ic, p=p, ub=ub: e.matmul(
                        ub, AF8[:, 2048 + p * 1024 + ic * 128:2048 + p * 1024 + ic * 128 + 128],
                        abf(A_XQT + ic * 512, 512), start=(ic == 0), stop=(ic == 7)),
                        "TE" if ic == 7 else None)
                te_j[j] = te_jj
                W("vector", "TE", te_j[j])
                W("vector", "SC", s_g.get(j - 1))
                O("vector", lambda e, gb=gb: e.tensor_tensor(
                    TGR[:, 1024:1536], gb, IBC[:, 0:512], ALU.mult))
                O("vector", lambda e, ub=ub: e.tensor_tensor(
                    TGR[:, 0:512], ub, IBC[:, 0:512], ALU.mult))
                v_pu[j] = O("vector", lambda e: e.drain(), "VE")
                W("scalar", "VE", v_pu[j])
                s_g[j] = O("scalar", lambda e: e.activation(
                    TGR[:, 1536:2048], TGR[:, 1024:1536], AF.Silu,
                    scale=stc(SC_WSC + 0)[:, 0:1]), "SC")
                W("vector", "SC", s_g[j])
                O("vector", lambda e, j=j: e.scalar_tensor_tensor(
                    GU[:, j * 512:(j + 1) * 512], TGR[:, 0:512],
                    stc(SC_WSC + 1)[:, 0:1], TGR[:, 1536:2048], ALU.mult, ALU.mult))
                vdrain()
                if j == 0:
                    O("vector", lambda e: e.tensor_copy(CTX[:, :], GU[:, 0:512]))
                    v_mm[j] = O("vector", lambda e: e.tensor_copy(MNA[:, :], GU[:, 0:512]), "VE")
                else:
                    O("vector", lambda e, j=j: e.tensor_tensor(
                        CTX[:, :], CTX[:, :], GU[:, j * 512:(j + 1) * 512], ALU.max))
                    v_mm[j] = O("vector", lambda e, j=j: e.tensor_tensor(
                        MNA[:, :], MNA[:, :], GU[:, j * 512:(j + 1) * 512], ALU.min), "VE")
                vdrain()
            fst["te_gu"] = te_j[31]

            O("vector", lambda e: e.tensor_scalar(MNA[:, :], MNA[:, :], -1.0, None, ALU.mult))
            vdrain()
            O("vector", lambda e: e.tensor_tensor(CTX[:, :], CTX[:, :], MNA[:, :], ALU.max))
            vdrain()
            v_mx = O("vector", lambda e: e.drain(), "VE")
            W("tensor", "VE", v_mx)
            W("tensor", "SC", fst["s_down"])
            for q in range(4):
                te_mx = O("tensor", lambda e, q=q: e.transpose(
                    P6[:, q * 128:(q + 1) * 128], CTX[:, q * 128:(q + 1) * 128], IDF[:]),
                    "TE" if q == 3 else None)
            W("vector", "TE", te_mx)
            for q in range(4):
                O("vector", lambda e, q=q: e.reduce_max(
                    stc(SC_TMP + q), P6[:, q * 128:(q + 1) * 128], axis=AX.X))
            vdrain()
            for q in range(4):
                O("vector", lambda e, q=q: e.tensor_scalar(
                    stc(SC_TMP + 4 + q), stc(SC_TMP + q), 1e-5, None, ALU.max))
            vdrain()
            for q in range(4):
                O("vector", lambda e, q=q: e.tensor_scalar(
                    stc(SC_INVS2 + q), stc(SC_TMP + 4 + q), 1.0 / 127.0, None, ALU.mult))
                O("vector", lambda e, q=q: e.reciprocal(
                    stc(SC_TMP + 8 + q), stc(SC_TMP + 4 + q)))
            vdrain()
            for q in range(4):
                O("vector", lambda e, q=q: e.tensor_scalar(
                    stc(SC_TMP + q), stc(SC_TMP + 8 + q), 127.0, None, ALU.mult))
                O("vector", lambda e, q=q: e.tensor_tensor(
                    stc(SC_S2 + q), stc(SC_INVS2 + q), stc(SC_WSC + 2), ALU.mult))
            v_s2bc = bcast4(SC_TMP, 512, psum_guard=fst["s_down"])

            for hf in range(2):
                v_rnd = {}
                s_t2 = {}
                for j in range(32):
                    jp = j % 2
                    if j == 0:
                        W("vector", "VE", v_s2bc)
                        if fst["te_dlast"]:
                            W("vector", "TE", fst["te_dlast"])
                    W("vector", "SC", s_t2.get(j - 2))
                    O("vector", lambda e, j=j, hf=hf, jp=jp: e.tensor_tensor(
                        TGR[:, 1024 + jp * 256:1280 + jp * 256],
                        GU[:, j * 512 + hf * 256:j * 512 + hf * 256 + 256],
                        IBC[:, 512 + hf * 256:512 + hf * 256 + 256], ALU.mult))
                    v_t1 = O("vector", lambda e: e.drain(), "VE")
                    W("scalar", "VE", v_t1)
                    s_t2[j] = O("scalar", lambda e, jp=jp: e.activation(
                        TGR[:, 1536 + jp * 256:1792 + jp * 256],
                        TGR[:, 1024 + jp * 256:1280 + jp * 256], AF.Copy, bias=MAGIC), "SC")
                    W("vector", "SC", s_t2[j])
                    v_rnd[j] = O("vector", lambda e, j=j, jp=jp: e.tensor_scalar(
                        abf(A_GUQ + j * 256, 256), TGR[:, 1536 + jp * 256:1792 + jp * 256],
                        -MAGIC, None, ALU.add), "VE")
                te_dj = {}
                for j in range(32):
                    sd = fst["wdj"] % 3
                    fst["wdj"] += 1
                    qn = "sync"
                    sn = "LM0" if sd == 0 else f"GM{sd - 1}"
                    W(qn, "TE", fst["te_wd"].get(fst["wdj"] - 1 - 3,
                                                 fst["te_gu"] if j < 3 and hf == 0 else
                                                 (fst["te_dlast"] if j < 3 else None)))
                    lw = dma2(qn, AF8[:, 4096 + sd * 1024:4096 + (sd + 1) * 1024],
                              tdd[:, j * 1024:(j + 1) * 1024], sn, 1024)
                    W("tensor", sn, lw)
                    W("tensor", "VE", v_rnd[j])
                    if j == 0:
                        W("tensor", "SC", fst["s_down"])
                    wb = 4096 + sd * 1024
                    O("tensor", lambda e, j=j, wb=wb: e.matmul(
                        PA[:, 0:512], abf(A_GUQ + j * 256, 128),
                        AF8[:, wb:wb + 512], start=(j == 0), stop=(j == 31)))
                    O("tensor", lambda e, j=j, wb=wb: e.matmul(
                        PA[:, 512:1024], abf(A_GUQ + j * 256, 128),
                        AF8[:, wb + 512:wb + 1024], start=(j == 0), stop=(j == 31)))
                    O("tensor", lambda e, j=j, wb=wb: e.matmul(
                        P[0][:, :], abf(A_GUQ + j * 256 + 128, 128),
                        AF8[:, wb:wb + 512], start=(j == 0), stop=(j == 31)))
                    te_dj[j] = O("tensor", lambda e, j=j, wb=wb: e.matmul(
                        P[1][:, :], abf(A_GUQ + j * 256 + 128, 128),
                        AF8[:, wb + 512:wb + 1024], start=(j == 0), stop=(j == 31)), "TE")
                    fst["te_wd"][fst["wdj"] - 1] = te_dj[j]
                fst["te_dlast"] = te_dj[31]
                W("scalar", "TE", te_dj[31])
                W("scalar", "VE", v_s2bc)
                v_x = 0
                for tt2 in range(2):
                    tt = hf * 2 + tt2
                    t = blk * 4 + tt
                    if tt2 == 1:
                        W("scalar", "VE", v_x)
                    if tt2 == 0:
                        O("scalar", lambda e, tt=tt: e.activation(
                            TGR[:, 1024:1536], PA[:, 0:512], AF.Identity,
                            scale=stc(SC_S2 + tt)[:, 0:1]))
                        s_e = O("scalar", lambda e, tt=tt: e.activation(
                            TGR[:, 1536:2048], PA[:, 512:1024], AF.Identity,
                            scale=stc(SC_S2 + tt)[:, 0:1]), "SC")
                    else:
                        O("scalar", lambda e, tt=tt: e.activation(
                            TGR[:, 1024:1536], P[0][:, :], AF.Identity,
                            scale=stc(SC_S2 + tt)[:, 0:1]))
                        s_e = O("scalar", lambda e, tt=tt: e.activation(
                            TGR[:, 1536:2048], P[1][:, :], AF.Identity,
                            scale=stc(SC_S2 + tt)[:, 0:1]), "SC")
                    fst["s_down"] = s_e
                    W("vector", "SC", s_e)
                    v_x = O("vector", lambda e, t=t: e.tensor_tensor(
                        X[:, t * 1024:(t + 1) * 1024], X[:, t * 1024:(t + 1) * 1024],
                        TGR[:, 1024:2048], ALU.add), "VE")
                    vdrain()
            fst["te_blk"] = fst["te_dlast"]
        return v_x

    W("vector", "S20", C["S20"])
    W("vector", "S21", C["S21"])
    ln_prepass(SC_RSTD1, SC_NMR1)
    v_last = ffn(SC_RSTD1, SC_NMR1)

    if stop_after == "ffn1":
        W("scalar", "VE", C["VE"])
        for t in range(NT):
            dma("scalar", outd[128 * t:128 * t + 128, :], X[:, t * 1024:(t + 1) * 1024], "S20")
        W("sync", "S20", C["S20"])
        W("gpsimd", "S20", C["S20"])
        prog.emit_block()
        stack.close()
        return nc

    ln_prepass(SC_RSTD2, SC_NMR2)

    W("sync", "TE", C["TE"])
    W("sync", "S20", C["S20"])
    W("sync", "S21", C["S21"])
    ld_wv = 0
    for ic in range(8):
        ld_wv = dma("sync", abf(A_GUQ + ic * 1024, 1024), wvbd[:, ic * 1024:(ic + 1) * 1024])

    st_kq = [0, 0]
    st_v = [0, 0]
    v_qrd = [0, 0]
    W("scalar", "VE", C["VE"])
    te_o = {}
    s_ev = {}
    v_vs = {}
    te_vlast = None
    for blk in range(NB):
        v_cp_last = 0
        W("vector", "S20", st_kq[0])
        W("vector", "S21", st_kq[1])
        for ti in range(4):
            t = blk * 4 + ti
            v_cp_last = xln_tile(t, ti, SC_RSTD2, SC_NMR2, False,
                                 entry_te=(te_vlast if ti == 0 else None))
        for i, oc in enumerate(list(range(8, 16)) + list(range(0, 8))):
            g = blk * 16 + i
            p = g % 2
            W("sync", "TE", te_o.get(g - 2))
            ld_w = dma2("sync", ABF[:, A_WG + p * 1024:A_WG + (p + 1) * 1024],
                        wqkb[:, oc * 1024:(oc + 1) * 1024], f"LW{p}", 1024)
            W("tensor", f"LW{p}", ld_w)
            if i == 0:
                W("tensor", "VE", v_cp_last)
                if blk == 0:
                    W("tensor", "LD", ld_wv)
                    W("tensor", "SC", fst["s_down"])
            W("tensor", "SC", s_ev.get(g - 2))
            for ic in range(8):
                te_og = O("tensor", lambda e, ic=ic, p=p: e.matmul(
                    P[p][:, :], abf(A_WG + p * 1024 + ic * 128, 128),
                    abf(A_XQT + ic * 512, 512), start=(ic == 0), stop=(ic == 7)),
                    "TE" if ic == 7 else None)
            te_o[g] = te_og
            W("scalar", "TE", te_o[g])
            W("scalar", f"S2{g % 2}", st_kq[g % 2])
            W("scalar", "VE", v_qrd[g % 2])
            sl = g % 2
            s_ev[g] = O("scalar", lambda e, p=p, oc=oc, sl=sl: e.activation(
                abf(stg(sl), 512), P[p][:, :], AF.Identity,
                bias=BQK[:, oc:oc + 1]), "SC")
            if oc >= 8:
                W("scalar", "SC", s_ev[g])
                st_kq[g % 2] = dma(
                    "scalar",
                    kkin[:, (oc - 8) * 2048 + blk * 512:(oc - 8) * 2048 + blk * 512 + 512],
                    abf(stg(sl), 512), f"S2{g % 2}")
            else:
                W("scalar", "SC", s_ev[g])
                st_kq[g % 2] = dma("scalar",
                            qtd[:, oc * 2048 + blk * 512:oc * 2048 + blk * 512 + 512],
                            abf(stg(sl), 512), f"S2{g % 2}")
        for tt in range(4):
            t = blk * 4 + tt
            g2 = blk * 4 + tt
            sl = g2 % 2
            W("vector", f"SV{sl}", st_v[sl])
            v_ms = O("vector", lambda e, sl=sl: e.memset(abf(A_VST + sl * 1040, 1040), 1.0), "VE")
            for oh in range(2):
                pbank = PA if g2 % 2 == 0 else PB
                pb = pbank[:, oh * 512:(oh + 1) * 512]
                W("tensor", "VE", v_vs.get((g2 - 2, oh)))
                if g2 == 0 and oh == 0:
                    W("tensor", "VE", v_ms)
                for ic in range(8):
                    te_v = O("tensor", lambda e, ic=ic, tt=tt, oh=oh, pb=pb: e.matmul(
                        pb, abf(A_XQT + ic * 512 + tt * 128, 128),
                        abf(A_GUQ + ic * 1024 + oh * 512, 512),
                        start=(ic == 0), stop=(ic == 7)), "TE" if ic == 7 else None)
                te_vlast = te_v
                W("vector", "TE", te_v)
                if oh == 0 and g2 > 0:
                    W("vector", "VE", v_ms)
                v_vs[(g2, oh)] = O("vector", lambda e, sl=sl, oh=oh, pb=pb: e.tensor_tensor(
                    abf(A_VST + sl * 1040, 1040).rearrange(
                        "p (h c) -> p h c", c=65)[:, 8 * oh:8 * (oh + 1), 0:64],
                    pb.rearrange("p (h c) -> p h c", c=64),
                    BB[:, oh * 512:(oh + 1) * 512].rearrange("p (h c) -> p h c", c=64),
                    ALU.add), "VE")
            W("scalar", "VE", v_vs[(g2, 1)])
            for pr in range(8):
                st_v[sl] = dma("scalar",
                            vvin[:, pr * 2080 + t * 130:pr * 2080 + t * 130 + 130],
                            abf(A_VST + sl * 1040 + pr * 130, 130), f"SV{sl}")

    W("gpsimd", "S20", C["S20"])
    W("gpsimd", "S21", C["S21"])
    W("gpsimd", "SV0", C["SV0"])
    W("gpsimd", "SV1", C["SV1"])
    O("gpsimd", lambda e: e.collective_compute(
        "AllGather", ALU.bypass, ins=[kvin[:, :]], outs=[kvout[:, :]],
        replica_groups=groups), "CC")

    te_pr = {}
    vhd = {}
    st_h = [0, 0]
    te_nc_prev = None
    s_exp_prev = None
    vev = {}
    ld_kt = {}
    ld_v = {}
    ld_qt = {}
    for pr in range(8):
        sl = pr % 2
        if pr == 0:
            W("sync", "CC", 1)
            W("sync", "TE", C["TE"])
        W("sync", "TE", te_pr.get(pr - 2))
        for h in range(2):
            dma2("sync", ABF[:, A_KT + sl * 4096 + h * 2048:A_KT + sl * 4096 + (h + 1) * 2048],
                 kkout[h * 128:(h + 1) * 128, pr * 2048:(pr + 1) * 2048], f"LK{sl}", 2048)
            dma2("sync", ABF[:, A_V + sl * 4160 + h * 2080:A_V + sl * 4160 + h * 2080 + 2080],
                 vvout[h * 128:(h + 1) * 128, pr * 2080:(pr + 1) * 2080], f"LK{sl}", 2080)
        ld_v[pr] = C[f"LK{sl}"]
        ld_qt[pr] = dma2("sync", ABF[:, A_QT + sl * 2048:A_QT + (sl + 1) * 2048],
                         qtd[:, pr * 2048:(pr + 1) * 2048], f"LK{sl}", 2048)
        W("vector", f"LK{sl}", ld_qt[pr])
        if te_nc_prev:
            W("vector", "TE", te_nc_prev)
        for h2 in range(2):
            O("vector", lambda e, sl=sl, h2=h2: e.tensor_tensor(
                GU[:, h2 * 2048:(h2 + 1) * 2048], abf(A_KT + sl * 4096 + h2 * 2048, 2048),
                abf(A_KT + sl * 4096 + h2 * 2048, 2048), ALU.mult))
        W("vector", f"LK{sl}", ld_qt[pr])
        v_ksq = O("vector", lambda e, sl=sl: e.tensor_tensor(
            GU[:, 4096:6144], abf(A_QT + sl * 2048, 2048),
            abf(A_QT + sl * 2048, 2048), ALU.mult), "VE")
        W("tensor", "VE", v_ksq)
        prv0 = vev.get((pr - 1, 3))
        W("tensor", "VE", prv0)
        v_k8 = {}
        for i8 in range(12):
            bk = (PA if i8 % 4 < 2 else PB)[0:2, (i8 % 2) * 512:(i8 % 2) * 512 + 512]
            W("tensor", "VE", v_k8.get(i8 - 4))
            te_c2 = O("tensor", lambda e, i8=i8, bk=bk: e.matmul(
                bk, ONES2[:, :], GU[:, i8 * 512:(i8 + 1) * 512],
                start=True, stop=True), "TE")
            W("vector", "TE", te_c2)
            O("vector", lambda e, bk=bk: e.reduce_max(
                STAT[0:2, SC_TMP:SC_TMP + 1], bk, axis=AX.X))
            vdrain()
            dstc = SC_KMX if i8 < 8 else SC_KMX + 1
            if i8 in (0, 8):
                O("vector", lambda e, dstc=dstc: e.tensor_copy(
                    STAT[0:2, dstc:dstc + 1], STAT[0:2, SC_TMP:SC_TMP + 1]))
            else:
                O("vector", lambda e, dstc=dstc: e.tensor_tensor(
                    STAT[0:2, dstc:dstc + 1], STAT[0:2, dstc:dstc + 1],
                    STAT[0:2, SC_TMP:SC_TMP + 1], ALU.max))
            v_k8[i8] = O("vector", lambda e: e.drain(), "VE")
        v_kacc = v_k8[11]
        if s_exp_prev:
            W("vector", "SC", s_exp_prev)
        O("vector", lambda e, pr=pr: e.tensor_tensor(
            STAT[0:2, SC_TMP:SC_TMP + 1], STAT[0:2, SC_KMX:SC_KMX + 1],
            STAT[0:2, SC_KMX + 1:SC_KMX + 2], ALU.add))
        vdrain()
        v_nc = O("vector", lambda e: e.tensor_scalar(
            STAT[0:2, SC_TMP + 1:SC_TMP + 2], STAT[0:2, SC_TMP:SC_TMP + 1],
            -1.0 / 64.0, 40.0, ALU.mult, ALU.add), "VE")
        W("scalar", "VE", v_nc)
        g_nc = dma("scalar", TGR[0:1, 0:2], STAT[0:2, SC_TMP + 1:SC_TMP + 2], "GD")
        W("vector", "GD", g_nc)
        O("vector", lambda e: e.tensor_reduce(
            TGR[0:1, 2:3], TGR[0:1, 0:2], AX.X, ALU.min))
        v_nc2 = O("vector", lambda e: e.drain(), "VE")
        W("tensor", "VE", v_nc2)
        te_nc = O("tensor", lambda e: e.matmul(
            P6[0:128, 0:1], ONES[0:1, 0:128], TGR[0:1, 2:3],
            start=True, stop=True), "TE")
        te_nc_prev = te_nc
        W("vector", "TE", te_nc)
        v_negc = O("vector", lambda e: e.tensor_copy(
            STAT[:, SC_NEGC:SC_NEGC + 1], P6[0:128, 0:1]), "VE")

        for qb in range(4):
            s_exp = {}
            te_av = {}
            def emit_av(kc):
                h, c16 = kc // 16, kc % 16
                sp = (kc % 2) * 1024
                if kc == 0:
                    prv = vev.get((pr, qb - 1)) if qb > 0 else vev.get((pr - 1, 3))
                    W("tensor", "VE", prv)
                W("tensor", "SC", s_exp[kc])
                O("tensor", lambda e, sl=sl, h=h, c16=c16, sp=sp, kc=kc: e.matmul(
                    P[0][0:65, :], abf(A_V + sl * 4160 + h * 2080 + c16 * 130, 65),
                    abf(A_PT + sp, 512), start=(kc == 0), stop=(kc == 31)))
                te_av[kc] = O("tensor", lambda e, sl=sl, h=h, c16=c16, sp=sp, kc=kc: e.matmul(
                    P[1][0:65, :], abf(A_V + sl * 4160 + h * 2080 + c16 * 130 + 65, 65),
                    abf(A_PT + sp + 512, 512), start=(kc == 0), stop=(kc == 31)), "TE")

            for kc in range(32):
                h, c16 = kc // 16, kc % 16
                sb = PA if kc % 2 == 0 else PB
                if kc == 0:
                    W("tensor", "VE", v_negc)
                    W("tensor", f"LK{sl}", ld_qt[pr])
                W("tensor", "SC", s_exp.get(kc - 2))
                O("tensor", lambda e, sl=sl, h=h, c16=c16, sb=sb, qb=qb: e.matmul(
                    sb[:, 0:512], abf(A_KT + sl * 4096 + h * 2048 + c16 * 128, 128, 0, 64),
                    abf(A_QT + sl * 2048 + qb * 512, 512, 0, 64),
                    start=True, stop=True, tile_position=(0, 0)))
                te_s = O("tensor", lambda e, sl=sl, h=h, c16=c16, sb=sb, qb=qb: e.matmul(
                    sb[:, 512:1024], abf(A_KT + sl * 4096 + h * 2048 + c16 * 128, 128, 64, 128),
                    abf(A_QT + sl * 2048 + qb * 512, 512, 64, 128),
                    start=True, stop=True, tile_position=(64, 0)), "TE")
                W("scalar", "TE", te_s)
                if kc >= 3:
                    W("scalar", "TE", te_av[kc - 2])
                sp = (kc % 2) * 1024
                s_exp[kc] = O("scalar", lambda e, sb=sb, sp=sp: e.activation(
                    abf(A_PT + sp, 1024), sb[:, 0:1024], AF.Exp,
                    scale=ISCALE, bias=stc(SC_NEGC)[:, 0:1]), "SC")
                if kc >= 1:
                    emit_av(kc - 1)
            emit_av(31)
            s_exp_prev = s_exp[31]
            W("vector", "TE", te_av[31])
            W("vector", "VE", vhd.get((pr, qb - 1), vhd.get((pr - 1, 3))))
            O("vector", lambda e: e.tensor_copy(CTX[0:65, :], P[0][0:65, :]))
            v_cx = O("vector", lambda e: e.tensor_copy(MNA[0:65, :], P[1][0:65, :]), "VE")
            vev[(pr, qb)] = v_cx
            W("scalar", "VE", v_cx)
            g_rs = dma("scalar", TGR[0:1, 1024:1536], CTX[64:65, 0:512], "GD")
            g_rs2 = dma("scalar", TGR[0:1, 1536:2048], MNA[64:65, 0:512], "GD")
            W("vector", "GD", g_rs2)
            O("vector", lambda e: e.reciprocal(TGR[0:1, 1024:1536], TGR[0:1, 1024:1536]))
            O("vector", lambda e: e.reciprocal(TGR[0:1, 1536:2048], TGR[0:1, 1536:2048]))
            v_rc = O("vector", lambda e: e.drain(), "VE")
            v_h = 0
            for hh in range(2):
                if hh == 0:
                    W("tensor", "VE", v_rc)
                else:
                    W("tensor", "VE", v_h)
                te_rp = O("tensor", lambda e, hh=hh: e.matmul(
                    P6[0:64, 0:512], ONES[0:1, 0:64],
                    TGR[0:1, 1024 + hh * 512:1536 + hh * 512],
                    start=True, stop=True), "TE")
                W("vector", "TE", te_rp)
                W("vector", f"SH{hh}", st_h[hh])
                src = CTX if hh == 0 else MNA
                v_h = O("vector", lambda e, hh=hh, src=src: e.tensor_tensor(
                    abf(A_HST + hh * 512, 512, 0, 64), src[0:64, :], P6[0:64, 0:512],
                    ALU.mult), "VE")
                W("scalar", "VE", v_h)
                st_h[hh] = dma("scalar",
                    htd[64 * hh:64 * (hh + 1),
                        pr * 2048 + qb * 512:pr * 2048 + qb * 512 + 512],
                    abf(A_HST + hh * 512, 512, 0, 64), f"SH{hh}")
            vhd[(pr, qb)] = v_h
        te_pr[pr] = C["TE"]

    W("sync", "TE", C["TE"])
    W("sync", "GD", C["GD"])
    v_bo = bias_bcast(bord[:], C["VE"], te_guard=C["SC"])
    ld_wo = 0
    for ic in range(8):
        ld_wo = dma("sync", abf(A_GUQ + ic * 1024, 1024), wobd[:, ic * 1024:(ic + 1) * 1024])
    te_op = {}
    s_op = {}
    v_xo = {}
    for t in range(NT):
        p = t % 2
        W("sync", "TE", te_op.get(t - 2))
        if t == 0:
            W("sync", "SH0", C["SH0"])
            W("sync", "SH1", C["SH1"])
        ld_h = 0
        for fc in range(8):
            ld_h = dma("sync", abf(A_QT + p * 1024 + fc * 128, 128),
                       htd[:, fc * 2048 + t * 128:fc * 2048 + t * 128 + 128], f"LK{p}")
        W("tensor", f"LK{p}", ld_h)
        if t == 0:
            W("tensor", "LD", ld_wo)
        if t == 0:
            W("tensor", "SC", C["SC"])
            W("tensor", "VE", C["VE"])
        W("tensor", "VE", v_xo.get(t - 2))
        pbank = PA if p == 0 else PB
        ob0 = pbank[:, 0:512]
        ob1 = pbank[:, 512:1024]
        for fc in range(8):
            O("tensor", lambda e, fc=fc, p=p, ob0=ob0: e.matmul(
                ob0, abf(A_QT + p * 1024 + fc * 128, 128),
                abf(A_GUQ + fc * 1024, 512), start=(fc == 0), stop=(fc == 7)))
            te_ot = O("tensor", lambda e, fc=fc, p=p, ob1=ob1: e.matmul(
                ob1, abf(A_QT + p * 1024 + fc * 128, 128),
                abf(A_GUQ + fc * 1024 + 512, 512), start=(fc == 0), stop=(fc == 7)),
                "TE" if fc == 7 else None)
        te_op[t] = te_ot
        W("scalar", "TE", te_op[t])
        W("scalar", "SC", s_op.get(t - 1, 0))
        W("scalar", "VE", v_xo.get(t - 1))
        s_a = O("scalar", lambda e, t=t: e.activation(
            XLN[:, 0:1024], X[:, t * 1024:(t + 1) * 1024], AF.Identity,
            bias=stc(SC_NMR2 + t)[:, 0:1], scale=stc(SC_RSTD2 + t)[:, 0:1]), "SC")
        s_op[t] = s_a
        W("vector", "SC", s_a)
        if t == 0:
            W("vector", "VE", v_bo)
        for oh in range(2):
            ob = (PA if p == 0 else PB)[:, oh * 512:(oh + 1) * 512]
            O("vector", lambda e, oh=oh, ob=ob: e.tensor_tensor(
                TGR[:, oh * 512:(oh + 1) * 512], ob,
                BB[:, oh * 512:(oh + 1) * 512], ALU.add))
        vdrain()
        O("vector", lambda e: e.tensor_tensor(
            TGR[:, 1024:2048], TGR[:, 0:1024], XLN[:, 0:1024], ALU.add))
        vdrain()
        v_xo[t] = O("vector", lambda e, t=t: e.tensor_tensor(
            X[:, t * 1024:(t + 1) * 1024], X[:, t * 1024:(t + 1) * 1024],
            TGR[:, 1024:2048], ALU.add), "VE")
        vdrain()
    fst["te_blk"] = te_op[NT - 1]
    fst["s_down"] = s_op[NT - 1]

    if stop_after != "attn":
        ln_prepass(SC_RSTD1, SC_NMR1)
        ffn(SC_RSTD1, SC_NMR1)

    W("scalar", "VE", C["VE"])
    for t in range(NT):
        dma("scalar", outd[128 * t:128 * t + 128, :], X[:, t * 1024:(t + 1) * 1024], "S20")
    for sn in ("S20", "S21", "SV0", "SV1", "SH0", "SH1", "GS0", "GS1"):
        W("sync", sn, C[sn])
        W("gpsimd", sn, C[sn])
    prog.emit_block()
    stack.close()
    return nc


def _pm(w, nj, nic):
    return np.ascontiguousarray(
        w.reshape(nj, 128, nic, 128).transpose(3, 0, 2, 1).reshape(128, nj * nic * 128))


def make_inputs(L0_attn_x, L0_attn_w_qkv, L0_attn_b_qkv, L0_attn_w_o, L0_attn_b_o,
                L0_ffn_w_gate, L0_ffn_w_up, L0_ffn_w_down, n_cores=8):
    x = np.ascontiguousarray(np.asarray(L0_attn_x, dtype=np.float32))
    w_qkv = np.asarray(L0_attn_w_qkv, dtype=np.float32)
    b_qkv = np.asarray(L0_attn_b_qkv, dtype=np.float32)
    w_o = np.asarray(L0_attn_w_o, dtype=np.float32)
    b_o = np.asarray(L0_attn_b_o, dtype=np.float32)
    w_gate = np.asarray(L0_ffn_w_gate, dtype=np.float32)
    w_up = np.asarray(L0_ffn_w_up, dtype=np.float32)
    w_down = np.asarray(L0_ffn_w_down, dtype=np.float32)

    shared = {
        "wgpm": _pm(w_gate, 32, 8),
        "wupm": _pm(w_up, 32, 8),
        "wdpm": np.ascontiguousarray(
            w_down.T.reshape(32, 128, 1024).transpose(1, 0, 2).reshape(128, 32 * 1024)),
        "wqkpm": _pm(w_qkv[0:2048], 16, 8),
        "wvpm": np.ascontiguousarray(
            w_qkv[2048:3072].T.reshape(8, 128, 1024).transpose(1, 0, 2).reshape(128, 8 * 1024)),
        "wopm": np.ascontiguousarray(
            w_o.T.reshape(8, 128, 1024).transpose(1, 0, 2).reshape(128, 8 * 1024)),
        "bqkc": np.ascontiguousarray(b_qkv[0:2048].reshape(16, 128).T),
        "bvr": np.ascontiguousarray(b_qkv[2048:3072].reshape(1, 1024)),
        "bor": np.ascontiguousarray(b_o.reshape(1, 1024)),
        "ident": np.eye(128, dtype=ml_dtypes.bfloat16),
        "identf": np.eye(128, dtype=np.float32),
    }
    in_maps = []
    for c in range(n_cores):
        b, h = c // 2, c % 2
        m = dict(shared)
        m["x"] = np.ascontiguousarray(x[b, 2048 * h:2048 * h + 2048])
        in_maps.append(m)
    return in_maps


def kernel(L0_attn_x, L0_attn_w_qkv, L0_attn_b_qkv, L0_attn_w_o, L0_attn_b_o,
           L0_ffn_w_gate, L0_ffn_w_up, L0_ffn_w_down):
    in_maps = make_inputs(L0_attn_x, L0_attn_w_qkv, L0_attn_b_qkv, L0_attn_w_o,
                          L0_attn_b_o, L0_ffn_w_gate, L0_ffn_w_up, L0_ffn_w_down)
    nc = bass.Bass("TRN2", target_bir_lowering=False)
    build(nc)
    res = run_bass_kernel_spmd(nc, in_maps, core_ids=list(range(8)))
    out = np.empty((4, 4096, 1024), dtype=np.float32)
    for c in range(8):
        b, h = c // 2, c % 2
        out[b, 2048 * h:2048 * h + 2048] = res.results[c]["out"]
    return out
